# version 01
# baseline (speedup 1.0000x reference)
import math
import numpy as np
import ml_dtypes

import concourse.bass as bass
import concourse.mybir as mybir
from concourse.bass_core_rust import IndirectOffsetOnAxis
from concourse.bass_utils import run_bass_kernel_spmd

F32 = mybir.dt.float32
BF16 = mybir.dt.bfloat16
U32 = mybir.dt.uint32
AF = mybir.ActivationFunctionType
AL = mybir.AluOpType
AX = mybir.AxisListType

bfloat16 = ml_dtypes.bfloat16

S = 2048
D = 1024
H = 16
DH = 64
NT = S // 128
NC = D // 128
E = 8
CAP = 320
NSLOT = E * CAP
NB = NSLOT // 128
HID = 4096
SM_SCALE = float(1.0 / math.sqrt(2 * DH))

PI = float(np.pi)
TWO_PI = float(2 * np.pi)
INV_2PI = float(1.0 / (2 * np.pi))
MAGIC = float(1.5 * 2.0 ** 23)
CW1 = float(np.float32(6.28125))
CW2 = float(np.float32(2.0 * np.pi - 6.28125))
NEG_BIG = -1.0e30

ENGINES = ("sync", "gpsimd", "vector", "scalar", "tensor")

BF_TOTAL = 76800
oc_ident = 0
oc_lt128 = 128
oc_onescol = 256
oc_rowsel = 512
oc_lt16 = 2560
oc_ones1 = 2576
oc_apbf = 2704
oc_ohs = 2720
oc_oh2 = 2728
oc_wr = 2736
oc_aplo = 2800
oR1 = 2816
o_ln1tm = oR1
o_attn = oR1
o_hfm = oR1
o_yst = oR1 + 10240
o_g0 = oR1 + 10752
o_g1 = oR1 + 11776
o_b2slot = oR1 + 12800
oR2 = 19200
o_ln1fm = oR2
o_ln2fm = oR2
o_wA = oR2
oR3 = 35584
o_vtm = oR3
o_ln2tm = oR3
o_buffm = oR3
oR4 = 52224
o_qp = oR4
o_kp = oR4 + 4096
o_sC = oR4 + 8192
o_sS = oR4 + 10240
o_psb = oR4 + 12288
o_wB = 56064

FP_TOTAL = 13312
of_identf = 0
of_trimask = 128
of_trow = 256
of_erange = 2304
of_im9 = 2312
of_tokid = 2320
of_iotap = 2336
of_eps = 2337
of_hpi = 2338
of_one = 2339
of_pp = 2340
of_stats = 2388
of_routT = 2396
of_b1l = 2492
of_br = 2748
of_rW = 2756
of_rB = 3780
of_expall = of_rW + 0
of_recall = of_rW + 128
of_sc2 = of_rW + 144
of_e1m9 = of_rW + 146
of_e2m9 = of_rW + 147
of_Earr = of_rW + 160
of_Parr = of_rW + 224
of_Garr = of_rW + 288
of_posall = of_rW + 352
of_prod = of_rW + 416
of_nk = of_rW + 432
of_kp = of_rW + 496
of_slc = of_rW + 560
of_sca = of_rW + 624
of_scb = of_rW + 688
of_tr64 = of_rW + 752
of_nine = of_rW + 816
of_r8a = of_rW + 824
of_r8b = of_rW + 832
oT = 4808
of_thq, of_k2, of_u2, of_w2 = oT, oT + 2048, oT + 4096, oT + 6144
of_th, of_k1, of_u1, of_w1 = oT, oT + 1024, oT + 2048, oT + 3072
of_sv, of_cv = oT + 4096, oT + 5120
of_xtile, of_sq = oT + 6144, oT + 7168
of_scsrc = oT
of_tokf = 13000
of_prob = 13020
of_srt = 13028
of_oh1 = 13036
of_oh2 = 13044
of_rt1 = 13052
of_rt2 = 13060
assert 13068 <= FP_TOTAL


class Prog:
    def __init__(self):
        self.steps = {e: [] for e in ENGINES}
        self.sem_val = {}

    def emit(self, eng, fn):
        self.steps[eng].append(fn)

    def inc(self, sem, n):
        k = sem.num
        self.sem_val[k] = self.sem_val.get(k, 0) + n
        return self.sem_val[k]

    def val(self, sem):
        return self.sem_val.get(sem.num, 0)

    def dma(self, q, out, in_, sem, **kw):
        v = self.inc(sem, 16)
        self.emit(q, lambda e, out=out, in_=in_, sem=sem, kw=kw:
                  e.dma_start(out=out, in_=in_, **kw).then_inc(sem, 16))
        return v

    def igather(self, out, in_, idx_col, sem):
        v = self.inc(sem, 16)
        self.emit("gpsimd", lambda e, out=out, in_=in_, idx_col=idx_col, sem=sem:
                  e.indirect_dma_start(out=out, out_offset=None, in_=in_,
                                       in_offset=IndirectOffsetOnAxis(idx_col, 0))
                  .then_inc(sem, 16))
        return v

    def iscatter(self, out, idx_col, in_, sem):
        v = self.inc(sem, 16)
        self.emit("gpsimd", lambda e, out=out, in_=in_, idx_col=idx_col, sem=sem:
                  e.indirect_dma_start(out=out,
                                       out_offset=IndirectOffsetOnAxis(idx_col, 0),
                                       in_=in_, in_offset=None)
                  .then_inc(sem, 16))
        return v

    def wait(self, eng, sem, v):
        if v > 0:
            self.emit(eng, lambda e, sem=sem, v=v: e.wait_ge(sem, v))

    def op(self, eng, fn, sem=None, n=1):
        if sem is None:
            self.emit(eng, fn)
            return None
        v = self.inc(sem, n)
        self.emit(eng, lambda e, fn=fn, sem=sem, n=n: fn(e).then_inc(sem, n))
        return v


def _consts_host():
    c = {}
    c["c_ident_bf"] = np.eye(128, dtype=bfloat16)
    c["c_ident_f32"] = np.eye(128, dtype=np.float32)
    c["c_lt128"] = np.triu(np.ones((128, 128), np.float32), k=1).astype(bfloat16)
    c["c_lt16"] = np.triu(np.ones((16, 16), np.float32), k=1).astype(bfloat16)
    m = np.zeros((128, 128), np.float32)
    m[np.arange(128)[:, None] > np.arange(128)[None, :]] = NEG_BIG
    c["c_trimask"] = m
    c["c_trow"] = np.broadcast_to(np.arange(S, dtype=np.float32), (128, S)).copy()
    oc = np.zeros((128, 16 * 16), np.float32)
    for t in range(16):
        oc[:, 16 * t + t] = 1.0
    c["c_onescol"] = oc.astype(bfloat16)
    rs = np.zeros((16, 16 * 128), np.float32)
    for t in range(16):
        rs[t, 128 * t:128 * (t + 1)] = 1.0
    c["c_rowsel"] = rs.astype(bfloat16)
    c["c_erange"] = np.broadcast_to(np.arange(E, dtype=np.float32), (128, E)).copy()
    c["c_im9"] = np.broadcast_to(np.arange(E, dtype=np.float32) - 9.0, (128, E)).copy()
    c["c_tokid"] = (np.arange(NT)[None, :] * 128
                    + np.arange(128)[:, None]).astype(np.float32).copy()
    c["c_iotap"] = np.arange(128, dtype=np.float32).reshape(128, 1).copy()
    c["c_tr64"] = np.broadcast_to(
        (np.arange(128, dtype=np.float32) + NSLOT).reshape(128, 1), (128, 64)).copy()
    c["c_ones1"] = np.ones((1, 128), dtype=bfloat16)
    return c


def _prep_inputs(ins):
    f32 = np.float32
    d = {}

    def pair_layout(w):
        out = np.zeros((128, 8), f32)
        for hp in range(8):
            out[0:64, hp] = w[2 * hp]
            out[64:128, hp] = w[2 * hp + 1]
        return out

    for nm in ("w_q", "b_q", "phi_q", "w_k", "b_k", "phi_k"):
        d["p_" + nm] = pair_layout(np.asarray(ins["L0_attn_" + nm], f32))
    for nm in ("w_v", "b_v"):
        d["r_" + nm] = np.broadcast_to(
            np.asarray(ins["L0_attn_" + nm], f32).reshape(-1), (128, D)).copy()
    for nm in ("w_out", "b_out"):
        d["r_" + nm] = np.broadcast_to(
            np.asarray(ins["L0_attn_" + nm], f32), (128, D)).copy()
    d["r_br"] = np.broadcast_to(np.asarray(ins["L0_ffn_br"], f32), (128, E)).copy()
    d["w_r"] = np.ascontiguousarray(np.asarray(ins["L0_ffn_Wr"], f32))
    d["w1"] = np.ascontiguousarray(np.asarray(ins["L0_ffn_W1"], f32))
    d["w2"] = np.ascontiguousarray(np.asarray(ins["L0_ffn_W2"], f32))
    b1 = np.asarray(ins["L0_ffn_b1"], f32)
    b1l = np.zeros((128, E * 32), f32)
    for e in range(E):
        for ht in range(32):
            b1l[:, e * 32 + ht] = b1[e, 128 * ht:128 * (ht + 1)]
    d["b1l"] = b1l
    d["b2"] = np.ascontiguousarray(np.asarray(ins["L0_ffn_b2"], f32))
    return d


def build_program(dbg=(), phases=11):
    nc = bass.Bass("TRN2", target_bir_lowering=False)
    ap = {}

    def IN(name, shape, dt=F32):
        ap[name] = nc.dram_tensor(name, list(shape), dt, kind="ExternalInput").ap()

    def OUT(name, shape, dt=F32):
        ap[name] = nc.dram_tensor(name, list(shape), dt, kind="ExternalOutput").ap()

    def INT(name, shape, dt=F32):
        ap[name] = nc.dram_tensor(name, list(shape), dt, kind="Internal").ap()

    IN("x", (S, D))
    for nm in ("p_w_q", "p_b_q", "p_phi_q", "p_w_k", "p_b_k", "p_phi_k"):
        IN(nm, (128, 8))
    for nm in ("r_w_v", "r_b_v", "r_w_out", "r_b_out"):
        IN(nm, (128, D))
    IN("r_br", (128, E))
    IN("w_r", (D, E))
    IN("w1", (E, D, HID))
    IN("w2", (E, HID, D))
    IN("b1l", (128, E * 32))
    IN("b2", (E, D))
    IN("c_ident_bf", (128, 128), BF16)
    IN("c_ident_f32", (128, 128))
    IN("c_lt128", (128, 128), BF16)
    IN("c_lt16", (16, 16), BF16)
    IN("c_trimask", (128, 128))
    IN("c_trow", (128, S))
    IN("c_onescol", (128, 256), BF16)
    IN("c_rowsel", (16, 2048), BF16)
    IN("c_erange", (128, E))
    IN("c_im9", (128, E))
    IN("c_tokid", (128, NT))
    IN("c_iotap", (128, 1))
    IN("c_tr64", (128, 64))
    IN("c_ones1", (1, 128), BF16)

    INT("ln2_hbm", (S, D), BF16)
    INT("x1_hbm", (S, D))
    INT("slotbuf", (NSLOT + 128, 64))
    INT("y_hbm", (NSLOT, D), BF16)
    OUT("out", (S, D))

    dbg_out = {}
    dbg_shapes = {
        "ln1": ((128, NT * D), BF16), "v": ((128, NT * H * 65), BF16),
        "q0": ((128, 2 * S), BF16), "k0": ((128, 2 * S), BF16),
        "attn": ((128, NT * D), BF16), "ln2": ((128, NT * D), BF16),
        "route": ((128, 256), F32), "u3": ((128, 64), U32), "scs": ((128, NT * 64), F32), "fpd": ((128, 128), F32), "x1": ((128, NT * D), F32), "thout": ((128, NT * D), F32),
        "tok128": ((128, NB), U32), "buf": ((128, 8 * NSLOT), BF16),
        "h0": ((128, 32 * CAP), BF16),
    }
    for nm in dbg:
        sh, dt = dbg_shapes[nm]
        OUT("dbg_" + nm, sh, dt)
        dbg_out[nm] = ap["dbg_" + nm]

    P = Prog()

    from contextlib import ExitStack
    with ExitStack() as _stack:
        BFh = _stack.enter_context(nc.sbuf_tensor("BF", [128, BF_TOTAL], BF16))
        FPh = _stack.enter_context(nc.sbuf_tensor("FP", [128, FP_TOTAL], F32))
        SMh = _stack.enter_context(nc.sbuf_tensor("SM", [128, 1024], F32))
        U3h = _stack.enter_context(nc.sbuf_tensor("U3", [128, 64], U32))
        PS0 = _stack.enter_context(nc.psum_tensor("PS0", [128, 512], F32))
        PS1 = _stack.enter_context(nc.psum_tensor("PS1", [128, 512], F32))
        PS2 = _stack.enter_context(nc.psum_tensor("PS2", [128, 512], F32))
        PS3 = _stack.enter_context(nc.psum_tensor("PS3", [128, 512], F32))
        PT_B0 = _stack.enter_context(nc.psum_tensor("PT_B0", [128, 128], BF16))
        PT_B1 = _stack.enter_context(nc.psum_tensor("PT_B1", [128, 128], BF16))
        PT_F = _stack.enter_context(nc.psum_tensor("PT_F", [128, 128], F32))
        s_ld = _stack.enter_context(nc.semaphore("s_ld"))
        s_w = _stack.enter_context(nc.semaphore("s_w"))
        s_dv = _stack.enter_context(nc.semaphore("s_dv"))
        s_ac = _stack.enter_context(nc.semaphore("s_ac"))
        s_mm = _stack.enter_context(nc.semaphore("s_mm"))
        s_ex = _stack.enter_context(nc.semaphore("s_ex"))
        s_sh = _stack.enter_context(nc.semaphore("s_sh"))
        s_io = _stack.enter_context(nc.semaphore("s_io"))
        s_g = _stack.enter_context(nc.semaphore("s_g"))
        s_st = _stack.enter_context(nc.semaphore("s_st"))
        block = _stack.enter_context(nc.Block())
        def B2(off, n, parts=128, p0=0):
            return BFh[p0:p0 + parts, off:off + n]

        def B3(off, a, b):
            return BFh[0:128, off:off + a * b].rearrange("p (a b) -> p a b", a=a)

        def F2(off, n, parts=128, p0=0):
            return FPh[p0:p0 + parts, off:off + n]

        def spacer():
            P.op("vector", lambda e: e.tensor_copy(out=SMh[:, 768:1024],
                                                   in_=SMh[:, 512:768]))

        ident_bf = B2(oc_ident, 128)
        ident_f = F2(of_identf, 128)
        trimask = F2(of_trimask, 128)
        eps_ap = F2(of_eps, 1)
        halfpi_ap = F2(of_hpi, 1)
        erange = F2(of_erange, E)
        im9 = F2(of_im9, E)

        for nm, (off, n, parts, isbf) in {
            "c_ident_bf": (oc_ident, 128, 128, True),
            "c_lt128": (oc_lt128, 128, 128, True),
            "c_onescol": (oc_onescol, 256, 128, True),
            "c_rowsel": (oc_rowsel, 2048, 16, True),
            "c_lt16": (oc_lt16, 16, 16, True),
            "c_ones1": (oc_ones1, 128, 1, True),
            "c_ident_f32": (of_identf, 128, 128, False),
            "c_trimask": (of_trimask, 128, 128, False),
            "c_trow": (of_trow, S, 128, False),
            "c_erange": (of_erange, E, 128, False),
            "c_im9": (of_im9, E, 128, False),
            "c_tokid": (of_tokid, NT, 128, False),
            "c_iotap": (of_iotap, 1, 128, False),
            "b1l": (of_b1l, E * 32, 128, False),
            "r_br": (of_br, E, 128, False),
        }.items():
            dst = B2(off, n, parts=parts) if isbf else F2(off, n, parts=parts)
            P.dma("sync", dst, ap[nm], s_ld)
        P.dma("gpsimd", B3(oc_wr, NC, E),
              ap["w_r"].rearrange("(c p) e -> p c e", p=128), s_ld)
        P.op("vector", lambda e: e.memset(eps_ap, 1e-5))
        P.op("vector", lambda e: e.memset(F2(of_one, 1), 1.0))
        P.op("vector", lambda e: e.memset(halfpi_ap, PI / 2), s_dv)

        names = ("p_w_q", "p_b_q", "p_phi_q", "p_w_k", "p_b_k", "p_phi_k")
        for i, nm in enumerate(names):
            dst = F2(of_pp + 8 * i, 8)
            lv = P.dma("sync", dst, ap[nm], s_ld)
            if i in (0, 3):
                P.wait("scalar", s_ld, lv)
                P.wait("scalar", s_dv, P.val(s_dv))
                P.op("scalar", lambda e, dst=dst: e.activation(
                    out=dst, in_=dst, func=AF.Abs))
                P.op("scalar", lambda e, dst=dst: e.activation(
                    out=dst, in_=dst, func=AF.Ln, bias=F2(of_one, 1)))
                P.op("scalar", lambda e, dst=dst: e.activation(
                    out=dst, in_=dst, func=AF.Exp, scale=-1.0), s_ac)
        ld_consts = P.val(s_ld)
        ac_pp = P.val(s_ac)

        def emit_ln(src_ap, dst_bf_ap):
            s_sum = F2(of_stats, 1); s_ss = F2(of_stats + 1, 1)
            s_msq = F2(of_stats + 2, 1); s_ssD = F2(of_stats + 3, 1)
            s_var = F2(of_stats + 4, 1); s_rstd = F2(of_stats + 5, 1)
            s_negm = F2(of_stats + 6, 1); s_mb = F2(of_stats + 7, 1)
            sq = F2(of_sq, D)
            P.wait("scalar", s_dv, P.val(s_dv))
            P.op("vector", lambda e: e.reduce_sum(out=s_sum, in_=src_ap, axis=AX.X),
                 s_dv)
            P.op("scalar", lambda e: e.activation(out=sq, in_=src_ap, func=AF.Square,
                                                  accum_out=s_ss), s_ac)
            P.wait("scalar", s_dv, P.val(s_dv))
            P.op("scalar", lambda e: e.activation(out=s_msq, in_=s_sum,
                                                  func=AF.Square, scale=1.0 / D))
            P.op("scalar", lambda e: e.activation(out=s_ssD, in_=s_ss, func=AF.Copy,
                                                  scale=1.0 / D))
            P.op("scalar", lambda e: e.activation(out=s_negm, in_=s_sum, func=AF.Copy,
                                                  scale=-1.0 / D), s_ac)
            P.wait("vector", s_ac, P.val(s_ac))
            P.op("vector", lambda e: e.tensor_sub(out=s_var, in0=s_ssD, in1=s_msq),
                 s_dv)
            P.wait("scalar", s_dv, P.val(s_dv))
            P.op("scalar", lambda e: e.activation(out=s_rstd, in_=s_var, func=AF.Ln,
                                                  bias=eps_ap))
            P.op("scalar", lambda e: e.activation(out=s_rstd, in_=s_rstd, func=AF.Exp,
                                                  scale=-0.5), s_ac)
            P.wait("vector", s_ac, P.val(s_ac))
            P.op("vector", lambda e: e.tensor_scalar(out=s_mb, in0=s_negm,
                                                     scalar1=s_rstd, scalar2=None,
                                                     op0=AL.mult), s_dv)
            spacer()
            P.op("vector", lambda e: e.tensor_scalar(out=dst_bf_ap, in0=src_ap,
                                                     scalar1=s_rstd, scalar2=s_mb,
                                                     op0=AL.mult, op1=AL.add), s_dv)

        def emit_trig(tho, n, ko, uo, wo, sin_out, cos_out, add_out=None,
                      add_srcs=None):
            th = F2(tho, n)
            k = F2(ko, n); u = F2(uo, n); w = F2(wo, n)
            P.op("vector", lambda e: e.tensor_scalar(out=k, in0=th, scalar1=INV_2PI,
                                                     scalar2=MAGIC, op0=AL.mult,
                                                     op1=AL.add))
            P.op("vector", lambda e: e.tensor_scalar(out=k, in0=k, scalar1=MAGIC,
                                                     scalar2=None, op0=AL.subtract))
            P.op("vector", lambda e: e.scalar_tensor_tensor(
                out=u, in0=k, scalar=-CW1, in1=th, op0=AL.mult, op1=AL.add))
            P.op("vector", lambda e: e.scalar_tensor_tensor(
                out=u, in0=k, scalar=-CW2, in1=u, op0=AL.mult, op1=AL.add))
            P.op("vector", lambda e: e.tensor_scalar(out=w, in0=u, scalar1=PI / 2,
                                                     scalar2=None, op0=AL.is_gt))
            P.op("vector", lambda e: e.scalar_tensor_tensor(
                out=w, in0=w, scalar=-TWO_PI, in1=u, op0=AL.mult, op1=AL.add))
            P.op("vector", lambda e: e.tensor_scalar(out=w, in0=w, scalar1=PI / 2,
                                                     scalar2=-1.5 * PI, op0=AL.min,
                                                     op1=AL.max))
            P.op("vector", lambda e: e.tensor_scalar(out=u, in0=u, scalar1=PI,
                                                     scalar2=-PI, op0=AL.min,
                                                     op1=AL.max), s_dv)
            P.wait("scalar", s_dv, P.val(s_dv))
            P.op("scalar", lambda e: e.activation(out=sin_out, in_=u, func=AF.Sin))
            P.op("scalar", lambda e: e.activation(out=cos_out, in_=w, func=AF.Sin,
                                                  bias=halfpi_ap), s_ac)
            if add_out is not None:
                a0, a1 = add_srcs if add_srcs is not None else (sin_out, cos_out)
                P.wait("vector", s_ac, P.val(s_ac))
                P.op("vector", lambda e: e.tensor_add(out=add_out, in0=a0,
                                                      in1=a1), s_dv)

        x_tiled = ap["x"].rearrange("(t p) d -> t p d", p=128)
        P.wait("vector", s_ld, ld_consts)
        P.wait("scalar", s_ld, ld_consts)
        for t in range(NT):
            lv = P.dma("sync", F2(of_xtile, D), x_tiled[t], s_ld)
            P.wait("vector", s_ld, lv)
            P.wait("scalar", s_ld, lv)
            emit_ln(F2(of_xtile, D), B2(o_ln1tm + t * D, D))
            P.wait("sync", s_dv, P.val(s_dv))
            P.wait("sync", s_ac, P.val(s_ac))
        ln1_dv = P.val(s_dv)
        if "ln1" in dbg:
            P.wait("gpsimd", s_dv, ln1_dv)
            P.dma("gpsimd", dbg_out["ln1"], B2(o_ln1tm, NT * D), s_io)

        if phases >= 2:
            P.wait("tensor", s_ld, ld_consts)
            P.wait("tensor", s_dv, ln1_dv)
            tcnt = 0
            for t in range(NT):
                for c in range(NC):
                    src = B2(o_ln1tm + t * D + 128 * c, 128)
                    pt = PT_B0[:, 0:128] if (tcnt % 2 == 0) else PT_B1[:, 0:128]
                    P.op("tensor", lambda e, pt=pt, src=src:
                         e.transpose(pt, src, ident_bf), s_mm)
                    P.wait("vector", s_mm, P.val(s_mm))
                    dst = B2(o_ln1fm + c * S + 128 * t, 128)
                    P.op("vector", lambda e, dst=dst, pt=pt:
                         e.tensor_copy(out=dst, in_=pt), s_dv)
                    if tcnt >= 1:
                        P.wait("tensor", s_dv, P.val(s_dv) - 1)
                    tcnt += 1
            ln1fm_dv = P.val(s_dv)

        if phases >= 3:
            lv = P.dma("sync", F2(of_rW, D), ap["r_w_v"], s_ld)
            lv = P.dma("sync", F2(of_rB, D), ap["r_b_v"], s_ld)
            P.wait("vector", s_ld, lv)
            P.wait("scalar", s_ld, P.val(s_ld))
            P.op("scalar", lambda e: e.activation(out=F2(of_rW, D), in_=F2(of_rW, D),
                                                  func=AF.Abs))
            P.op("scalar", lambda e: e.activation(out=F2(of_rW, D), in_=F2(of_rW, D),
                                                  func=AF.Ln, bias=F2(of_one, 1)))
            P.op("scalar", lambda e: e.activation(out=F2(of_rW, D), in_=F2(of_rW, D),
                                                  func=AF.Exp, scale=-1.0), s_ac)
            vtm4 = BFh[0:128, o_vtm:o_vtm + NT * H * 65].rearrange(
                "p (t h c) -> p t h c", t=NT, h=H)
            P.op("vector", lambda e: e.memset(vtm4[:, :, :, 64:65], 1.0), s_dv)
            P.wait("vector", s_ac, P.val(s_ac))
            for t in range(NT):
                th = F2(of_th, D)
                src = B2(o_ln1tm + t * D, D)
                P.op("vector", lambda e, th=th, src=src: e.tensor_tensor(
                    out=th, in0=src, in1=F2(of_rW, D), op=AL.mult))
                P.op("vector", lambda e, th=th: e.tensor_tensor(
                    out=th, in0=th, in1=F2(of_rB, D), op=AL.add))
                vts = vtm4[:, t, :, 0:64]
                sv3 = F2(of_sv, D).rearrange("p (h c) -> p h c", h=H)
                cv3 = F2(of_cv, D).rearrange("p (h c) -> p h c", h=H)
                emit_trig(of_th, D, of_k1, of_u1, of_w1,
                          F2(of_sv, D), F2(of_cv, D), add_out=vts,
                          add_srcs=(sv3, cv3))
            v_dv = P.val(s_dv)
            if "v" in dbg:
                P.wait("gpsimd", s_dv, v_dv)
                P.dma("gpsimd", dbg_out["v"], B2(o_vtm, NT * H * 65), s_io)

        if phases >= 4:
            for hp in range(8):
                mm_prev = P.val(s_mm)
                for (dst, base) in ((o_qp, 0), (o_kp, 24)):
                    inv_ap = F2(of_pp + base + 0 + hp, 1)
                    b_ap = F2(of_pp + base + 8 + hp, 1)
                    phi_ap = F2(of_pp + base + 16 + hp, 1)
                    th = F2(of_thq, S)
                    src = B2(o_ln1fm + hp * S, S)
                    P.wait("vector", s_ac, ac_pp)
                    P.wait("vector", s_dv, max(ln1fm_dv, v_dv))
                    P.wait("vector", s_mm, mm_prev)
                    P.op("vector", lambda e, th=th, src=src, inv_ap=inv_ap, b_ap=b_ap:
                         e.tensor_scalar(out=th, in0=src, scalar1=inv_ap, scalar2=b_ap,
                                         op0=AL.mult, op1=AL.add))
                    P.op("vector", lambda e, th=th, phi_ap=phi_ap:
                         e.scalar_tensor_tensor(out=th, in0=F2(of_trow, S),
                                                scalar=phi_ap, in1=th,
                                                op0=AL.mult, op1=AL.add))
                    emit_trig(of_thq, S, of_k2, of_u2, of_w2,
                              B2(o_sS, S), B2(o_sC, S))
                    P.wait("vector", s_ac, P.val(s_ac))
                    P.op("vector", lambda e, dst=dst: e.tensor_copy(
                        out=BFh[0:64, dst:dst + S], in_=BFh[0:64, o_sC:o_sC + S]))
                    P.op("vector", lambda e, dst=dst: e.tensor_copy(
                        out=BFh[64:128, dst + S:dst + 2 * S],
                        in_=BFh[64:128, o_sS:o_sS + S]), s_dv)
                    P.wait("gpsimd", s_ac, P.val(s_ac))
                    P.wait("gpsimd", s_mm, mm_prev)
                    P.dma("gpsimd", BFh[64:128, dst:dst + S],
                          BFh[0:64, o_sS:o_sS + S], s_sh)
                    P.dma("gpsimd", BFh[0:64, dst + S:dst + 2 * S],
                          BFh[64:128, o_sC:o_sC + S], s_sh)
                    P.wait("vector", s_sh, P.val(s_sh))
                    P.wait("scalar", s_sh, P.val(s_sh))
                qk_dv = P.val(s_dv)
                qk_sh = P.val(s_sh)
                P.wait("tensor", s_dv, qk_dv)
                P.wait("tensor", s_sh, qk_sh)
                for hh in range(2):
                    h = 2 * hp + hh
                    qoff = o_qp + hh * S
                    koff = o_kp + hh * S
                    for qt in range(4):
                        av_ps = PS2 if (qt % 2 == 0) else PS3
                        nkt = 4 * qt + 4
                        for kt in range(nkt):
                            dd = kt - 4 * qt
                            qlo_rel = 128 * dd if dd >= 0 else 0
                            qlo = qt * 512 + qlo_rel
                            qn = 512 - qlo_rel
                            sc_ps = PS0 if (kt % 2 == 0) else PS1
                            P.wait("tensor", s_ex, max(0, P.val(s_ex) - 1))
                            P.op("tensor", lambda e, sc_ps=sc_ps, koff=koff, kt=kt,
                                 qoff=qoff, qlo=qlo, qn=qn:
                                 e.matmul(sc_ps[:, 0:qn], B2(koff + kt * 128, 128),
                                          B2(qoff + qlo, qn), start=True, stop=True),
                                 s_mm)
                            if dd >= 0:
                                P.wait("vector", s_mm, P.val(s_mm))
                                P.op("vector", lambda e, sc_ps=sc_ps:
                                     e.tensor_tensor(out=sc_ps[:, 0:128],
                                                     in0=sc_ps[:, 0:128],
                                                     in1=trimask, op=AL.add), s_dv)
                                P.wait("scalar", s_dv, P.val(s_dv))
                            else:
                                P.wait("scalar", s_mm, P.val(s_mm))
                            psb = o_psb + (kt % 2) * 512
                            P.op("scalar", lambda e, psb=psb, sc_ps=sc_ps, qn=qn:
                                 e.activation(out=B2(psb, qn), in_=sc_ps[:, 0:qn],
                                              func=AF.Exp, scale=SM_SCALE), s_ex)
                            P.wait("tensor", s_ex, P.val(s_ex))
                            vsl = vtm4[:, kt, h, :]
                            P.op("tensor", lambda e, av_ps=av_ps, vsl=vsl, psb=psb,
                                 qlo_rel=qlo_rel, qn=qn, first=(kt == 0),
                                 last=(kt == nkt - 1):
                                 e.matmul(av_ps[0:65, qlo_rel:qlo_rel + qn], vsl,
                                          B2(psb, qn), start=first, stop=last), s_mm)
                        P.wait("vector", s_mm, P.val(s_mm))
                        stg = SMh[0:65, 0:512]
                        P.op("vector", lambda e, av_ps=av_ps, stg=stg:
                             e.tensor_copy(out=stg, in_=av_ps[0:65, 0:512]), s_dv)
                        P.wait("tensor", s_dv, P.val(s_dv))
                        for sub in range(4):
                            P.op("tensor", lambda e, stg=stg, sub=sub:
                                 e.transpose(PT_F[:, 0:65],
                                             stg[0:65, 128 * sub:128 * (sub + 1)],
                                             ident_f[0:65, 0:65]), s_mm)
                            P.wait("vector", s_mm, P.val(s_mm))
                            t_idx = qt * 4 + sub
                            rec = F2(of_stats + 7, 1)
                            P.op("vector", lambda e, rec=rec:
                                 e.tensor_copy(out=rec, in_=PT_F[:, 64:65]), s_dv)
                            P.wait("scalar", s_dv, P.val(s_dv))
                            P.op("scalar", lambda e, rec=rec:
                                 e.activation(out=rec, in_=rec, func=AF.Ln))
                            P.op("scalar", lambda e, rec=rec:
                                 e.activation(out=rec, in_=rec, func=AF.Exp,
                                              scale=-1.0), s_ac)
                            P.wait("vector", s_ac, P.val(s_ac))
                            dst = B2(o_attn + t_idx * D + h * 64, 64)
                            P.op("vector", lambda e, dst=dst, rec=rec:
                                 e.tensor_scalar(out=dst, in0=PT_F[:, 0:64],
                                                 scalar1=rec, scalar2=None,
                                                 op0=AL.mult), s_dv)
                            P.wait("tensor", s_dv, P.val(s_dv))
            attn_dv = P.val(s_dv)
            if "attn" in dbg:
                P.wait("gpsimd", s_dv, attn_dv)
                P.dma("gpsimd", dbg_out["attn"], B2(o_attn, NT * D), s_io)
            if "q0" in dbg:
                P.wait("gpsimd", s_dv, attn_dv)
                P.dma("gpsimd", dbg_out["q0"], B2(o_qp, 2 * S), s_io)
            if "k0" in dbg:
                P.wait("gpsimd", s_dv, attn_dv)
                P.dma("gpsimd", dbg_out["k0"], B2(o_kp, 2 * S), s_io)

        if phases >= 5:
            P.wait("sync", s_dv, attn_dv)
            lv = P.dma("sync", F2(of_rW, D), ap["r_w_out"], s_ld)
            lv = P.dma("sync", F2(of_rB, D), ap["r_b_out"], s_ld)
            P.wait("vector", s_ld, lv)
            P.wait("vector", s_dv, attn_dv)
            P.wait("scalar", s_ld, P.val(s_ld))
            P.op("scalar", lambda e: e.activation(out=F2(of_rW, D), in_=F2(of_rW, D),
                                                  func=AF.Abs))
            P.op("scalar", lambda e: e.activation(out=F2(of_rW, D), in_=F2(of_rW, D),
                                                  func=AF.Ln, bias=F2(of_one, 1)))
            P.op("scalar", lambda e: e.activation(out=F2(of_rW, D), in_=F2(of_rW, D),
                                                  func=AF.Exp, scale=-1.0), s_ac)
            P.wait("vector", s_ac, P.val(s_ac))
            x1_tiled = ap["x1_hbm"].rearrange("(t p) d -> t p d", p=128)
            ln2_hbm_t = ap["ln2_hbm"].rearrange("(t p) d -> t p d", p=128)
            for t in range(NT):
                th = F2(of_th, D)
                att_bf = B2(o_attn + t * D, D)
                P.op("vector", lambda e, th=th, att_bf=att_bf: e.tensor_tensor(
                    out=th, in0=att_bf, in1=F2(of_rW, D), op=AL.mult))
                P.op("vector", lambda e, th=th: e.tensor_tensor(
                    out=th, in0=th, in1=F2(of_rB, D), op=AL.add))
                emit_trig(of_th, D, of_k1, of_u1, of_w1, F2(of_sv, D), F2(of_cv, D),
                          add_out=F2(of_sq, D))
                lvx = P.dma("sync", F2(of_xtile, D), x_tiled[t], s_ld)
                P.wait("vector", s_ld, lvx)
                P.op("vector", lambda e: e.tensor_tensor(
                    out=F2(of_xtile, D), in0=F2(of_xtile, D),
                    in1=F2(of_sq, D), op=AL.add), s_dv)
                P.wait("gpsimd", s_dv, P.val(s_dv))
                P.dma("gpsimd", x1_tiled[t], F2(of_xtile, D), s_io)
                if "x1" in dbg:
                    P.dma("gpsimd", dbg_out["x1"][:, t * D:(t + 1) * D],
                          F2(of_xtile, D), s_io)
                if "thout" in dbg:
                    P.dma("gpsimd", dbg_out["thout"][:, t * D:(t + 1) * D],
                          F2(of_sq, D), s_io)
                emit_ln(F2(of_xtile, D), B2(o_ln2tm + t * D, D))
                P.wait("gpsimd", s_dv, P.val(s_dv))
                P.dma("gpsimd", ln2_hbm_t[t], B2(o_ln2tm + t * D, D), s_io)
                P.wait("sync", s_dv, P.val(s_dv))
                P.wait("sync", s_ac, P.val(s_ac))
                P.wait("sync", s_io, P.val(s_io))
            ln2_dv = P.val(s_dv)
            ln2_io = P.val(s_io)
            if "ln2" in dbg:
                P.wait("gpsimd", s_dv, ln2_dv)
                P.dma("gpsimd", dbg_out["ln2"], B2(o_ln2tm, NT * D), s_io)

        if phases >= 6:
            P.wait("tensor", s_dv, ln2_dv)
            tcnt = 0
            for t in range(NT):
                for c in range(NC):
                    src = B2(o_ln2tm + t * D + 128 * c, 128)
                    pt = PT_B0[:, 0:128] if (tcnt % 2 == 0) else PT_B1[:, 0:128]
                    P.op("tensor", lambda e, pt=pt, src=src:
                         e.transpose(pt, src, ident_bf), s_mm)
                    P.wait("vector", s_mm, P.val(s_mm))
                    dst = B2(o_ln2fm + c * S + 128 * t, 128)
                    P.op("vector", lambda e, dst=dst, pt=pt:
                         e.tensor_copy(out=dst, in_=pt), s_dv)
                    if tcnt >= 1:
                        P.wait("tensor", s_dv, P.val(s_dv) - 1)
                    tcnt += 1
            ln2fm_dv = P.val(s_dv)
            P.wait("tensor", s_dv, ln2fm_dv)
            P.wait("tensor", s_ld, P.val(s_ld))
            P.wait("sync", s_dv, ln2_dv)
            lv64 = P.dma("sync", F2(of_tr64, 64), ap["c_tr64"], s_ld)
            P.op("vector", lambda e: e.memset(F2(of_nine, 1), 9.0), s_dv)
            nine_ap = F2(of_nine, 1)
            for t in range(NT):
                for c in range(NC):
                    P.op("tensor", lambda e, t=t, c=c:
                         e.matmul(PS0[:, 0:E], B2(o_ln2fm + c * S + 128 * t, 128),
                                  B3(oc_wr, NC, E)[:, c, :],
                                  start=(c == 0), stop=(c == NC - 1)),
                         s_mm if c == NC - 1 else None)
                P.wait("vector", s_mm, P.val(s_mm))
                lg = F2(of_prod, E)
                expP = F2(of_expall + E * t, E)
                ssum = F2(of_recall + t, 1)
                P.op("vector", lambda e, lg=lg: e.tensor_add(
                    out=lg, in0=PS0[:, 0:E], in1=F2(of_br, E)), s_dv)
                P.wait("tensor", s_dv, P.val(s_dv))
                P.wait("scalar", s_dv, P.val(s_dv))
                P.op("scalar", lambda e, lg=lg, expP=expP, ssum=ssum:
                     e.activation(out=expP, in_=lg, func=AF.Exp, accum_out=ssum))
                P.op("scalar", lambda e, ssum=ssum:
                     e.activation(out=ssum, in_=ssum, func=AF.Ln))
                P.op("scalar", lambda e, ssum=ssum:
                     e.activation(out=ssum, in_=ssum, func=AF.Exp, scale=-1.0), s_ac)
                P.wait("vector", s_ac, P.val(s_ac))
                srt = F2(of_srt, E)
                P.op("vector", lambda e, srt=srt, expP=expP:
                     e.max(out=srt, in_=expP), s_dv)
                P.wait("scalar", s_dv, P.val(s_dv))
                sc0 = F2(of_sc2, 1); sc1 = F2(of_sc2 + 1, 1)
                P.op("scalar", lambda e, srt=srt, sc0=sc0:
                     e.activation(out=sc0, in_=srt[:, 0:1], func=AF.Copy))
                P.op("scalar", lambda e, srt=srt, sc1=sc1:
                     e.activation(out=sc1, in_=srt[:, 1:2], func=AF.Copy), s_ac)
                P.wait("vector", s_ac, P.val(s_ac))
                spacer()
                P.op("vector", lambda e, srt=srt, ssum=ssum, t=t: e.tensor_scalar(
                    out=F2(of_Parr + 2 * t, 1), in0=srt[:, 0:1], scalar1=ssum,
                    scalar2=None, op0=AL.mult))
                P.op("vector", lambda e, srt=srt, ssum=ssum, t=t: e.tensor_scalar(
                    out=F2(of_Parr + 2 * t + 1, 1), in0=srt[:, 1:2], scalar1=ssum,
                    scalar2=None, op0=AL.mult), s_dv)
                r8a = F2(of_r8a, E)
                P.op("vector", lambda e, r8a=r8a, expP=expP, sc0=sc0:
                     e.scalar_tensor_tensor(out=r8a, in0=expP, scalar=sc0, in1=im9,
                                            op0=AL.is_equal, op1=AL.mult), s_dv)
                spacer()
                P.op("vector", lambda e, r8a=r8a: e.tensor_reduce(
                    out=F2(of_e1m9, 1), in_=r8a, axis=AX.X, op=AL.min), s_dv)
                P.wait("scalar", s_dv, P.val(s_dv))
                P.op("scalar", lambda e, t=t: e.activation(
                    out=F2(of_Earr + 2 * t, 1), in_=F2(of_e1m9, 1), func=AF.Relu,
                    bias=nine_ap), s_ac)
                P.wait("vector", s_ac, P.val(s_ac))
                e1ap = F2(of_Earr + 2 * t, 1)
                P.op("vector", lambda e, e1ap=e1ap: e.tensor_scalar(
                    out=B2(oc_ohs, E), in0=erange, scalar1=e1ap, scalar2=None,
                    op0=AL.is_equal), s_dv)
                r8b = F2(of_r8b, E)
                P.op("vector", lambda e, r8b=r8b, expP=expP, sc1=sc1:
                     e.scalar_tensor_tensor(out=r8b, in0=expP, scalar=sc1, in1=im9,
                                            op0=AL.is_equal, op1=AL.mult))
                P.op("vector", lambda e, e1ap=e1ap: e.tensor_scalar(
                    out=F2(of_prod + 8, E), in0=erange, scalar1=e1ap, scalar2=20.0,
                    op0=AL.is_equal, op1=AL.mult), s_dv)
                spacer()
                P.op("vector", lambda e, r8b=r8b: e.tensor_add(
                    out=r8b, in0=r8b, in1=F2(of_prod + 8, E)), s_dv)
                spacer()
                P.op("vector", lambda e, r8b=r8b: e.tensor_reduce(
                    out=F2(of_e2m9, 1), in_=r8b, axis=AX.X, op=AL.min), s_dv)
                P.wait("scalar", s_dv, P.val(s_dv))
                P.op("scalar", lambda e, t=t: e.activation(
                    out=F2(of_Earr + 2 * t + 1, 1), in_=F2(of_e2m9, 1), func=AF.Relu,
                    bias=nine_ap), s_ac)
                P.wait("vector", s_ac, P.val(s_ac))
                e2ap = F2(of_Earr + 2 * t + 1, 1)
                P.op("vector", lambda e, e2ap=e2ap: e.tensor_scalar(
                    out=B2(oc_oh2, E), in0=erange, scalar1=e2ap, scalar2=None,
                    op0=AL.is_equal), s_dv)
                P.wait("tensor", s_dv, P.val(s_dv))
                P.op("tensor", lambda e, t=t:
                     e.matmul(PS1[0:16, 0:E], B2(oc_onescol + 16 * t, 16),
                              B2(oc_ohs, E), start=(t == 0), stop=False))
                P.op("tensor", lambda e, t=t:
                     e.matmul(PS1[0:16, 0:E], B2(oc_onescol + 16 * t, 16),
                              B2(oc_oh2, E), start=False, stop=(t == NT - 1)),
                     s_mm if t == NT - 1 else None)
            P.wait("vector", s_mm, P.val(s_mm))
            P.op("vector", lambda e: e.tensor_copy(
                out=BFh[0:16, oc_apbf + 8:oc_apbf + 16], in_=PS1[0:16, 0:E]), s_dv)
            P.wait("tensor", s_dv, P.val(s_dv))
            P.op("tensor", lambda e: e.matmul(
                PS2[0:16, 0:E], BFh[0:16, oc_lt16:oc_lt16 + 16],
                BFh[0:16, oc_apbf + 8:oc_apbf + 16], start=True, stop=True), s_mm)
            P.wait("vector", s_mm, P.val(s_mm))
            a_f = F2(of_prod, E, parts=16)
            kq = F2(of_prod + 8, E, parts=16)
            P.op("vector", lambda e, a_f=a_f: e.tensor_copy(
                out=a_f, in_=PS2[0:16, 0:E]), s_dv)
            spacer()
            P.op("vector", lambda e, a_f=a_f, kq=kq: e.tensor_scalar(
                out=kq, in0=a_f, scalar1=1.0 / 256.0, scalar2=MAGIC - 0.5,
                op0=AL.mult, op1=AL.add), s_dv)
            spacer()
            P.op("vector", lambda e, kq=kq: e.tensor_scalar(
                out=kq, in0=kq, scalar1=MAGIC, scalar2=None, op0=AL.subtract), s_dv)
            spacer()
            P.op("vector", lambda e, kq=kq: e.tensor_scalar(
                out=BFh[0:16, oc_apbf:oc_apbf + 8], in0=kq, scalar1=256.0,
                scalar2=None, op0=AL.mult), s_dv)
            P.op("vector", lambda e, kq=kq, a_f=a_f: e.scalar_tensor_tensor(
                out=BFh[0:16, oc_aplo:oc_aplo + 8], in0=kq, scalar=-256.0,
                in1=a_f, op0=AL.mult, op1=AL.add), s_dv)
            apre_dv = P.val(s_dv)

        if phases >= 7:
            P.wait("tensor", s_dv, apre_dv)
            for t in range(NT):
                e1ap = F2(of_Earr + 2 * t, 1)
                e2ap = F2(of_Earr + 2 * t + 1, 1)
                P.wait("vector", s_mm, P.val(s_mm))
                P.op("vector", lambda e, e1ap=e1ap: e.tensor_scalar(
                    out=B2(oc_ohs, E), in0=erange, scalar1=e1ap, scalar2=None,
                    op0=AL.is_equal))
                P.op("vector", lambda e, e2ap=e2ap: e.tensor_scalar(
                    out=B2(oc_oh2, E), in0=erange, scalar1=e2ap, scalar2=None,
                    op0=AL.is_equal), s_dv)
                P.wait("tensor", s_dv, P.val(s_dv))
                P.op("tensor", lambda e, t=t: e.matmul(
                    PS0[:, 0:E],
                    BFh[0:16, oc_rowsel + 128 * t:oc_rowsel + 128 * (t + 1)],
                    BFh[0:16, oc_apbf:oc_apbf + 8], start=True, stop=False))
                P.op("tensor", lambda e, t=t: e.matmul(
                    PS0[:, 0:E],
                    BFh[0:16, oc_rowsel + 128 * t:oc_rowsel + 128 * (t + 1)],
                    BFh[0:16, oc_aplo:oc_aplo + 8], start=False, stop=False))
                P.op("tensor", lambda e: e.matmul(
                    PS0[:, 0:E], BFh[0:128, oc_lt128:oc_lt128 + 128],
                    B2(oc_ohs, E), start=False, stop=False))
                P.op("tensor", lambda e: e.matmul(
                    PS0[:, 0:E], BFh[0:128, oc_lt128:oc_lt128 + 128],
                    B2(oc_oh2, E), start=False, stop=True), s_mm)
                P.wait("vector", s_mm, P.val(s_mm))
                P.op("vector", lambda e, e1ap=e1ap: e.scalar_tensor_tensor(
                    out=F2(of_prod, E), in0=erange, scalar=e1ap, in1=PS0[:, 0:E],
                    op0=AL.is_equal, op1=AL.mult))
                P.op("vector", lambda e, e2ap=e2ap: e.scalar_tensor_tensor(
                    out=F2(of_prod + 8, E), in0=erange, scalar=e2ap, in1=PS0[:, 0:E],
                    op0=AL.is_equal, op1=AL.mult), s_dv)
                spacer()
                P.op("vector", lambda e, t=t: e.reduce_sum(
                    out=F2(of_posall + 2 * t, 1), in_=F2(of_prod, E), axis=AX.X))
                P.op("vector", lambda e, t=t: e.reduce_sum(
                    out=F2(of_posall + 2 * t + 1, 1), in_=F2(of_prod + 8, E),
                    axis=AX.X), s_dv)
                P.wait("tensor", s_dv, P.val(s_dv))
            P.wait("vector", s_ld, lv64)
            spacer()
            posall = F2(of_posall, 64)
            nk = F2(of_nk, 64); kp = F2(of_kp, 64)
            slc = F2(of_slc, 64); sca = F2(of_sca, 64); scb = F2(of_scb, 64)
            P.op("vector", lambda e: e.tensor_scalar(
                out=nk, in0=posall, scalar1=float(CAP) - 0.5, scalar2=None,
                op0=AL.is_gt), s_dv)
            P.wait("scalar", s_dv, P.val(s_dv))
            P.op("scalar", lambda e: e.activation(
                out=kp, in_=nk, func=AF.Relu, scale=-1.0, bias=F2(of_one, 1)), s_ac)
            P.wait("vector", s_ac, P.val(s_ac))
            P.op("vector", lambda e: e.tensor_mul(
                out=F2(of_Garr, 64), in0=F2(of_Parr, 64), in1=kp), s_dv)
            P.op("vector", lambda e: e.tensor_scalar(
                out=slc, in0=posall, scalar1=float(CAP - 1), scalar2=None,
                op0=AL.min), s_dv)
            spacer()
            P.op("vector", lambda e: e.scalar_tensor_tensor(
                out=slc, in0=F2(of_Earr, 64), scalar=float(CAP), in1=slc,
                op0=AL.mult, op1=AL.add), s_dv)
            spacer()
            P.op("vector", lambda e: e.tensor_copy(
                out=U3h[0:128, 0:32], in_=F2(of_slc, 32)), s_dv)
            P.op("vector", lambda e: e.tensor_mul(out=sca, in0=slc, in1=kp))
            P.op("vector", lambda e: e.tensor_mul(out=scb, in0=F2(of_tr64, 64),
                                                  in1=nk), s_dv)
            spacer()
            P.op("vector", lambda e: e.tensor_add(out=sca, in0=sca, in1=scb), s_dv)
            spacer()
            P.op("vector", lambda e: e.tensor_copy(
                out=U3h[0:128, 32:64], in_=F2(of_sca, 32)), s_dv)
            route_dv = P.val(s_dv)
            if "route" in dbg:
                P.wait("gpsimd", s_dv, route_dv)
                P.dma("gpsimd", dbg_out["route"],
                      F2(of_Earr, 256), s_io)

        if phases >= 8:
            P.op("vector", lambda e: e.memset(SMh[:, 0:512], 0.0), s_dv)
            zdv = P.val(s_dv)
            P.wait("gpsimd", s_dv, zdv)
            sb_flat = ap["slotbuf"].rearrange("(a p) c -> p a c", p=128)
            for (lo, hi) in ((0, 8), (8, 16), (16, 21)):
                P.dma("gpsimd", sb_flat[:, lo:hi, :],
                      SMh[:, 0:(hi - lo) * 64].rearrange("p (a c) -> p a c", c=64), s_io)
            zio = P.val(s_io)
            P.wait("vector", s_dv, route_dv)
            P.op("vector", lambda e: e.memset(F2(of_scsrc, NT * 64), 0.0), s_dv)
            scs3 = FPh[0:128, of_scsrc:of_scsrc + NT * 64].rearrange(
                "p (t c) -> p t c", t=NT)
            P.wait("vector", s_dv, P.val(s_dv))
            P.op("vector", lambda e: e.tensor_scalar(
                out=scs3[:, :, 0:1].rearrange("p t c -> p (t c)"),
                in0=F2(of_tokid, NT),
                scalar1=1.0, scalar2=None, op0=AL.add), s_dv)
            scdv = P.val(s_dv)
            P.wait("gpsimd", s_dv, scdv)
            P.wait("gpsimd", s_io, zio)
            for t in range(NT):
                for kk in range(2):
                    idx_col = U3h[0:128, 32 + 2 * t + kk:33 + 2 * t + kk]
                    P.iscatter(ap["slotbuf"], idx_col, scs3[:, t, :], s_g)
            P.wait("gpsimd", s_g, P.val(s_g))
            if "u3" in dbg:
                P.dma("gpsimd", dbg_out["u3"], U3h[0:128, 0:64], s_io)
            if "scs" in dbg:
                P.dma("gpsimd", dbg_out["scs"], F2(of_scsrc, NT * 64), s_io)
            if "fpd" in dbg:
                P.dma("gpsimd", dbg_out["fpd"], F2(2300, 128), s_io)
            tok_src = ap["slotbuf"].rearrange("(b p) c -> p b c", p=128)
            P.dma("gpsimd",
                  FPh[0:128, of_tokf:of_tokf + NB].rearrange("p (a b) -> p a b", b=1),
                  tok_src[:, 0:NB, 0:1], s_io, allow_slow_non_contiguous=True)
            rv = P.val(s_io)
            P.wait("vector", s_io, rv)
            P.op("vector", lambda e: e.tensor_scalar(
                out=U3h[0:128, 2 * NT:2 * NT + NB], in0=F2(of_tokf, NB),
                scalar1=-1.0, scalar2=0.0, op0=AL.add, op1=AL.max), s_dv)
            tok_dv = P.val(s_dv)
            if "tok128" in dbg:
                P.wait("gpsimd", s_dv, tok_dv)
                P.dma("gpsimd", dbg_out["tok128"], U3h[0:128, 2 * NT:2 * NT + NB], s_io)

        if phases >= 9:
            buf3 = B3(o_buffm, 8, NSLOT)
            P.wait("gpsimd", s_dv, tok_dv)
            P.wait("gpsimd", s_io, ln2_io)
            mm_after_tile = []
            for b in range(NB):
                idx_col = U3h[0:128, 2 * NT + b:2 * NT + b + 1]
                if b >= 1:
                    P.wait("gpsimd", s_mm, mm_after_tile[b - 1])
                P.igather(B2(o_g0, D), ap["ln2_hbm"], idx_col, s_g)
                gv = P.val(s_g)
                P.wait("tensor", s_g, gv)
                for c in range(NC):
                    pt = PT_B0[:, 0:128] if (c % 2 == 0) else PT_B1[:, 0:128]
                    P.op("tensor", lambda e, pt=pt, c=c:
                         e.transpose(pt, B2(o_g0 + 128 * c, 128), ident_bf), s_mm)
                    P.wait("vector", s_mm, P.val(s_mm))
                    dst = buf3[:, c, 128 * b:128 * (b + 1)]
                    P.op("vector", lambda e, dst=dst, pt=pt:
                         e.tensor_copy(out=dst, in_=pt), s_dv)
                    P.wait("tensor", s_dv, P.val(s_dv) - 1)
                mm_after_tile.append(P.val(s_mm))
            buf_dv = P.val(s_dv)
            if "buf" in dbg:
                P.wait("gpsimd", s_dv, buf_dv)
                P.dma("gpsimd", dbg_out["buf"], B2(o_buffm, 8 * NSLOT), s_io)

        if phases >= 10:
            h3 = B3(o_hfm, 32, CAP)
            w1_h = ap["w1"].rearrange("e (c p) h -> e p c h", p=128)
            w2_h = ap["w2"].rearrange("e (hc p) d -> e p hc d", p=128)
            slot_off = [o_wA, o_wB]
            slot_free_mm = [P.val(s_mm), P.val(s_mm)]
            loads = []
            for e_i in range(E):
                loads += [(e_i, "w1", 0), (e_i, "w1", 1), (e_i, "w2", 0), (e_i, "w2", 1)]
            lv_load = [None] * len(loads)

            def ensure_load(i):
                if i >= len(loads) or lv_load[i] is not None:
                    return
                e_i, kind, half = loads[i]
                sl = i % 2
                P.wait("gpsimd", s_mm, slot_free_mm[sl])
                if kind == "w1":
                    src = w1_h[e_i][:, :, 2048 * half:2048 * (half + 1)]
                    dst = B3(slot_off[sl], 8, 2048)
                else:
                    src = w2_h[e_i][:, :, 512 * half:512 * (half + 1)]
                    dst = B3(slot_off[sl], 32, 512)
                lv_load[i] = P.dma("gpsimd", dst, src, s_w)

            P.wait("tensor", s_dv, buf_dv)
            ensure_load(0)
            ensure_load(1)
            for e_i in range(E):
                for half in range(2):
                    i = 4 * e_i + half
                    ensure_load(i)
                    ensure_load(i + 1)
                    P.wait("tensor", s_w, lv_load[i])
                    woff = slot_off[i % 2]
                    for ht in range(16):
                        ps = PS0 if (ht % 2 == 0) else PS1
                        P.wait("tensor", s_ac, max(0, P.val(s_ac) - 1))
                        for c in range(NC):
                            P.op("tensor",
                                 lambda e, ps=ps, woff=woff, c=c, ht=ht, e_i=e_i:
                                 e.matmul(ps[:, 0:CAP],
                                          B3(woff, 8, 2048)[:, c,
                                                            128 * ht:128 * (ht + 1)],
                                          buf3[:, c, e_i * CAP:(e_i + 1) * CAP],
                                          start=(c == 0), stop=(c == NC - 1)),
                                 s_mm if c == NC - 1 else None)
                        P.wait("scalar", s_mm, P.val(s_mm))
                        hti = 16 * half + ht
                        b1_ap = F2(of_b1l + e_i * 32 + hti, 1)
                        P.op("scalar", lambda e, ps=ps, hti=hti, b1_ap=b1_ap:
                             e.activation(out=h3[:, hti, :], in_=ps[:, 0:CAP],
                                          func=AF.Relu, bias=b1_ap), s_ac)
                    slot_free_mm[i % 2] = P.val(s_mm)
                P.wait("tensor", s_ac, P.val(s_ac))
                for chalf in range(2):
                    i = 4 * e_i + 2 + chalf
                    ensure_load(i)
                    ensure_load(i + 1)
                    P.wait("gpsimd", s_mm, P.val(s_mm))
                    b2lv = P.dma("gpsimd", B2(o_b2slot, 512, parts=1),
                                 ap["b2"][loads[i][0]:loads[i][0] + 1,
                                          512 * chalf:512 * (chalf + 1)], s_w)
                    P.wait("tensor", s_w, max(lv_load[i], b2lv))
                    woff = slot_off[i % 2]
                    for ss in range(3):
                        sn = 128 if ss < 2 else 64
                        ps = PS2 if (ss % 2 == 0) else PS3
                        P.wait("tensor", s_dv, max(0, P.val(s_dv) - 1))
                        for hh2 in range(32):
                            P.op("tensor", lambda e, ps=ps, woff=woff, hh2=hh2, ss=ss,
                                 sn=sn:
                                 e.matmul(ps[0:sn, 0:512],
                                          h3[:, hh2, ss * 128:ss * 128 + sn],
                                          B3(woff, 32, 512)[:, hh2, :],
                                          start=(hh2 == 0), stop=False))
                        P.op("tensor", lambda e, ps=ps, sn=sn:
                             e.matmul(ps[0:sn, 0:512],
                                      BFh[0:1, oc_ones1:oc_ones1 + sn],
                                      BFh[0:1, o_b2slot:o_b2slot + 512],
                                      start=False, stop=True), s_mm)
                        P.wait("vector", s_mm, P.val(s_mm))
                        P.op("vector", lambda e, ps=ps, sn=sn:
                             e.tensor_copy(out=B2(o_yst, 512, parts=sn),
                                           in_=ps[0:sn, 0:512]), s_dv)
                        ydv = P.val(s_dv)
                        P.wait("gpsimd", s_dv, ydv)
                        P.dma("gpsimd",
                              ap["y_hbm"][e_i * CAP + ss * 128:
                                          e_i * CAP + ss * 128 + sn,
                                          512 * chalf:512 * (chalf + 1)],
                              B2(o_yst, 512, parts=sn), s_io)
                        P.wait("vector", s_io, P.val(s_io))
                    slot_free_mm[i % 2] = P.val(s_mm)
            y_io = P.val(s_io)
            if "h0" in dbg:
                hv = P.val(s_ac)
                P.wait("gpsimd", s_ac, hv)
                P.dma("gpsimd", dbg_out["h0"], B2(o_hfm, 32 * CAP), s_io)

        if phases >= 11:
            out_t = ap["out"].rearrange("(t p) d -> t p d", p=128)
            P.wait("gpsimd", s_io, y_io)
            for t in range(NT):
                lvx = P.dma("sync", F2(of_xtile, D), x1_tiled[t], s_ld)
                P.igather(B2(o_g0, D), ap["y_hbm"],
                          U3h[0:128, 2 * t:2 * t + 1], s_g)
                g1v = P.igather(B2(o_g1, D), ap["y_hbm"],
                                U3h[0:128, 2 * t + 1:2 * t + 2], s_g)
                P.wait("vector", s_ld, lvx)
                P.wait("vector", s_g, g1v)
                g0_ap = F2(of_Garr + 2 * t, 1)
                g1_ap = F2(of_Garr + 2 * t + 1, 1)
                P.op("vector", lambda e, g0_ap=g0_ap: e.scalar_tensor_tensor(
                    out=F2(of_xtile, D), in0=B2(o_g0, D), scalar=g0_ap,
                    in1=F2(of_xtile, D), op0=AL.mult, op1=AL.add))
                P.op("vector", lambda e, g1_ap=g1_ap: e.scalar_tensor_tensor(
                    out=F2(of_xtile, D), in0=B2(o_g1, D), scalar=g1_ap,
                    in1=F2(of_xtile, D), op0=AL.mult, op1=AL.add), s_dv)
                odv = P.val(s_dv)
                P.wait("gpsimd", s_dv, odv)
                P.dma("gpsimd", out_t[t], F2(of_xtile, D), s_st)
                P.wait("sync", s_dv, odv)
                P.wait("sync", s_st, P.val(s_st))
                P.wait("gpsimd", s_g, P.val(s_g))
            P.wait("gpsimd", s_st, P.val(s_st))
            P.wait("gpsimd", s_io, P.val(s_io))
            P.wait("sync", s_st, P.val(s_st))

        for sem in (s_st, s_io, s_g, s_sh, s_w, s_ld):
            P.wait("gpsimd", sem, P.val(sem))
        P.wait("gpsimd", s_dv, P.val(s_dv))
        P.wait("gpsimd", s_ac, P.val(s_ac))
        P.wait("gpsimd", s_mm, P.val(s_mm))
        P.wait("gpsimd", s_ex, P.val(s_ex))

        @block.sync
        def _(e):
            for fn in P.steps["sync"]:
                fn(e)

        @block.gpsimd
        def _(e):
            for fn in P.steps["gpsimd"]:
                fn(e)

        @block.vector
        def _(e):
            for fn in P.steps["vector"]:
                fn(e)

        @block.scalar
        def _(e):
            for fn in P.steps["scalar"]:
                fn(e)

        @block.tensor
        def _(e):
            for fn in P.steps["tensor"]:
                fn(e)

    return nc


_consts_cache = None


def kernel(**inputs):
    global _consts_cache
    if _consts_cache is None:
        _consts_cache = _consts_host()
    consts = _consts_cache
    shared = _prep_inputs(inputs)
    x = np.asarray(inputs["L0_attn_x"], np.float32)
    B = x.shape[0]
    nc = build_program()
    in_maps = []
    for b in range(B):
        m = {"x": np.ascontiguousarray(x[b])}
        m.update(shared)
        m.update(consts)
        in_maps.append(m)
    res = run_bass_kernel_spmd(nc, in_maps, core_ids=list(range(B)))
    out = np.stack([res.results[b]["out"] for b in range(B)], axis=0)
    return out


# version 04
# speedup vs baseline: 1.3095x; 1.3095x over previous
import math
import numpy as np
import ml_dtypes

import concourse.bass as bass
import concourse.mybir as mybir
from concourse.bass_core_rust import IndirectOffsetOnAxis
from concourse.bass_utils import run_bass_kernel_spmd

F32 = mybir.dt.float32
BF16 = mybir.dt.bfloat16
U32 = mybir.dt.uint32
AF = mybir.ActivationFunctionType
AL = mybir.AluOpType
AX = mybir.AxisListType

bfloat16 = ml_dtypes.bfloat16

S = 2048
D = 1024
H = 16
DH = 64
NT = S // 128
NC = D // 128
E = 8
CAP = 320
NSLOT = E * CAP
NB = NSLOT // 128
HID = 4096
SM_SCALE = float(1.0 / math.sqrt(2 * DH))

PI = float(np.pi)
TWO_PI = float(2 * np.pi)
INV_2PI = float(1.0 / (2 * np.pi))
MAGIC = float(1.5 * 2.0 ** 23)
CW1 = float(np.float32(6.28125))
CW2 = float(np.float32(2.0 * np.pi - 6.28125))
NEG_BIG = -1.0e30

ENGINES = ("sync", "gpsimd", "vector", "scalar", "tensor")

BF_TOTAL = 76800
oc_ident = 0
oc_lt128 = 128
oc_onescol = 256
oc_rowsel = 512
oc_lt16 = 2560
oc_ones1 = 2576
oc_apbf = 2704
oc_ohs = 2720
oc_oh2 = 2728
oc_wr = 2736
oc_aplo = 2800
oR1 = 2816
o_ln1tm = oR1
o_attn = oR1
o_hfm = oR1
o_yst = oR1 + 10240
o_g0 = oR1 + 10752
o_g1 = oR1 + 11776
o_b2slot = oR1 + 12800
oR2 = 19200
o_ln1fm = oR2
o_ln2fm = oR2
o_wA = oR2
oR3 = 35584
o_vtm = oR3
o_ln2tm = oR3
o_buffm = oR3
oR4 = 52224
o_qp = oR4
o_kp = oR4 + 4096
o_sC = oR4 + 8192
o_sS = oR4 + 10240
o_psb = oR4 + 12288
o_wB = 56064

FP_TOTAL = 13312
of_identf = 0
of_trimask = 128
of_trow = 256
of_erange = 2304
of_im9 = 2312
of_tokid = 2320
of_iotap = 2336
of_eps = 2337
of_hpi = 2338
of_one = 2339
of_pp = 2340
of_stats = 2388
of_routT = 2396
of_b1l = 2492
of_br = 2748
of_rW = 2756
of_rB = 3780
of_expall = of_rW + 0
of_recall = of_rW + 128
of_sc2 = of_rW + 144
of_e1m9 = of_rW + 146
of_e2m9 = of_rW + 147
of_Earr = of_rW + 160
of_Parr = of_rW + 224
of_Garr = of_rW + 288
of_posall = of_rW + 352
of_prod = of_rW + 416
of_nk = of_rW + 432
of_kp = of_rW + 496
of_slc = of_rW + 560
of_sca = of_rW + 624
of_scb = of_rW + 688
of_tr64 = of_rW + 752
of_nine = of_rW + 816
of_r8a = of_rW + 824
of_r8b = of_rW + 832
oT = 4808
of_thq, of_k2, of_u2, of_w2 = oT, oT + 2048, oT + 4096, oT + 6144
of_th, of_k1, of_u1, of_w1 = oT, oT + 1024, oT + 2048, oT + 3072
of_sv, of_cv = oT + 4096, oT + 5120
of_xtile, of_sq = oT + 6144, oT + 7168
of_scsrc = oT
of_tokf = 13000
of_prob = 13020
of_srt = 13028
of_oh1 = 13036
of_oh2 = 13044
of_rt1 = 13052
of_rt2 = 13060
assert 13068 <= FP_TOTAL


class Prog:
    def __init__(self):
        self.steps = {e: [] for e in ENGINES}
        self.sem_val = {}

    def emit(self, eng, fn):
        self.steps[eng].append(fn)

    def inc(self, sem, n):
        k = sem.num
        self.sem_val[k] = self.sem_val.get(k, 0) + n
        return self.sem_val[k]

    def val(self, sem):
        return self.sem_val.get(sem.num, 0)

    def dma(self, q, out, in_, sem, **kw):
        v = self.inc(sem, 16)
        self.emit(q, lambda e, out=out, in_=in_, sem=sem, kw=kw:
                  e.dma_start(out=out, in_=in_, **kw).then_inc(sem, 16))
        return v

    def igather(self, out, in_, idx_col, sem):
        v = self.inc(sem, 16)
        self.emit("gpsimd", lambda e, out=out, in_=in_, idx_col=idx_col, sem=sem:
                  e.indirect_dma_start(out=out, out_offset=None, in_=in_,
                                       in_offset=IndirectOffsetOnAxis(idx_col, 0))
                  .then_inc(sem, 16))
        return v

    def iscatter(self, out, idx_col, in_, sem):
        v = self.inc(sem, 16)
        self.emit("gpsimd", lambda e, out=out, in_=in_, idx_col=idx_col, sem=sem:
                  e.indirect_dma_start(out=out,
                                       out_offset=IndirectOffsetOnAxis(idx_col, 0),
                                       in_=in_, in_offset=None)
                  .then_inc(sem, 16))
        return v

    def wait(self, eng, sem, v):
        if v > 0:
            self.emit(eng, lambda e, sem=sem, v=v: e.wait_ge(sem, v))

    def op(self, eng, fn, sem=None, n=1):
        if sem is None:
            self.emit(eng, fn)
            return None
        v = self.inc(sem, n)
        self.emit(eng, lambda e, fn=fn, sem=sem, n=n: fn(e).then_inc(sem, n))
        return v


def _consts_host():
    c = {}
    c["c_ident_bf"] = np.eye(128, dtype=bfloat16)
    c["c_ident_f32"] = np.eye(128, dtype=np.float32)
    c["c_lt128"] = np.triu(np.ones((128, 128), np.float32), k=1).astype(bfloat16)
    c["c_lt16"] = np.triu(np.ones((16, 16), np.float32), k=1).astype(bfloat16)
    m = np.zeros((128, 128), np.float32)
    m[np.arange(128)[:, None] > np.arange(128)[None, :]] = NEG_BIG
    c["c_trimask"] = m
    c["c_trow"] = np.broadcast_to(np.arange(S, dtype=np.float32), (128, S)).copy()
    oc = np.zeros((128, 16 * 16), np.float32)
    for t in range(16):
        oc[:, 16 * t + t] = 1.0
    c["c_onescol"] = oc.astype(bfloat16)
    rs = np.zeros((16, 16 * 128), np.float32)
    for t in range(16):
        rs[t, 128 * t:128 * (t + 1)] = 1.0
    c["c_rowsel"] = rs.astype(bfloat16)
    c["c_erange"] = np.broadcast_to(np.arange(E, dtype=np.float32), (128, E)).copy()
    c["c_im9"] = np.broadcast_to(np.arange(E, dtype=np.float32) - 9.0, (128, E)).copy()
    c["c_tokid"] = (np.arange(NT)[None, :] * 128
                    + np.arange(128)[:, None]).astype(np.float32).copy()
    c["c_iotap"] = np.arange(128, dtype=np.float32).reshape(128, 1).copy()
    c["c_tr64"] = np.broadcast_to(
        (np.arange(128, dtype=np.float32) + NSLOT).reshape(128, 1), (128, 64)).copy()
    c["c_ones1"] = np.ones((1, 128), dtype=bfloat16)
    return c


def _prep_inputs(ins):
    f32 = np.float32
    d = {}

    def pair_layout(w):
        out = np.zeros((128, 8), f32)
        for hp in range(8):
            out[0:64, hp] = w[2 * hp]
            out[64:128, hp] = w[2 * hp + 1]
        return out

    for nm in ("w_q", "b_q", "phi_q", "w_k", "b_k", "phi_k"):
        d["p_" + nm] = pair_layout(np.asarray(ins["L0_attn_" + nm], f32))
    for nm in ("w_v", "b_v"):
        d["r_" + nm] = np.broadcast_to(
            np.asarray(ins["L0_attn_" + nm], f32).reshape(-1), (128, D)).copy()
    for nm in ("w_out", "b_out"):
        d["r_" + nm] = np.broadcast_to(
            np.asarray(ins["L0_attn_" + nm], f32), (128, D)).copy()
    d["r_br"] = np.broadcast_to(np.asarray(ins["L0_ffn_br"], f32), (128, E)).copy()
    d["w_r"] = np.ascontiguousarray(np.asarray(ins["L0_ffn_Wr"], f32))
    d["w1"] = np.ascontiguousarray(np.asarray(ins["L0_ffn_W1"], f32))
    d["w2"] = np.ascontiguousarray(np.asarray(ins["L0_ffn_W2"], f32))
    b1 = np.asarray(ins["L0_ffn_b1"], f32)
    b1l = np.zeros((128, E * 32), f32)
    for e in range(E):
        for ht in range(32):
            b1l[:, e * 32 + ht] = b1[e, 128 * ht:128 * (ht + 1)]
    d["b1l"] = b1l
    d["b2"] = np.ascontiguousarray(np.asarray(ins["L0_ffn_b2"], f32))
    return d


def build_program(dbg=(), phases=11):
    nc = bass.Bass("TRN2", target_bir_lowering=False)
    ap = {}

    def IN(name, shape, dt=F32):
        ap[name] = nc.dram_tensor(name, list(shape), dt, kind="ExternalInput").ap()

    def OUT(name, shape, dt=F32):
        ap[name] = nc.dram_tensor(name, list(shape), dt, kind="ExternalOutput").ap()

    def INT(name, shape, dt=F32):
        ap[name] = nc.dram_tensor(name, list(shape), dt, kind="Internal").ap()

    IN("x", (S, D))
    for nm in ("p_w_q", "p_b_q", "p_phi_q", "p_w_k", "p_b_k", "p_phi_k"):
        IN(nm, (128, 8))
    for nm in ("r_w_v", "r_b_v", "r_w_out", "r_b_out"):
        IN(nm, (128, D))
    IN("r_br", (128, E))
    IN("w_r", (D, E))
    IN("w1", (E, D, HID))
    IN("w2", (E, HID, D))
    IN("b1l", (128, E * 32))
    IN("b2", (E, D))
    IN("c_ident_bf", (128, 128), BF16)
    IN("c_ident_f32", (128, 128))
    IN("c_lt128", (128, 128), BF16)
    IN("c_lt16", (16, 16), BF16)
    IN("c_trimask", (128, 128))
    IN("c_trow", (128, S))
    IN("c_onescol", (128, 256), BF16)
    IN("c_rowsel", (16, 2048), BF16)
    IN("c_erange", (128, E))
    IN("c_im9", (128, E))
    IN("c_tokid", (128, NT))
    IN("c_iotap", (128, 1))
    IN("c_tr64", (128, 64))
    IN("c_ones1", (1, 128), BF16)

    INT("ln2_hbm", (S, D), BF16)
    INT("x1_hbm", (S, D))
    INT("slotbuf", (NSLOT + 128, 64))
    INT("y_hbm", (NSLOT, D), BF16)
    OUT("out", (S, D))

    dbg_out = {}
    dbg_shapes = {
        "ln1": ((128, NT * D), BF16), "v": ((128, NT * H * 65), BF16),
        "q0": ((128, 2 * S), BF16), "k0": ((128, 2 * S), BF16),
        "attn": ((128, NT * D), BF16), "ln2": ((128, NT * D), BF16),
        "route": ((128, 256), F32), "u3": ((128, 64), U32), "scs": ((128, NT * 64), F32), "fpd": ((128, 128), F32), "x1": ((128, NT * D), F32), "thout": ((128, NT * D), F32),
        "tok128": ((128, NB), U32), "buf": ((128, 8 * NSLOT), BF16),
        "h0": ((128, 32 * CAP), BF16),
    }
    for nm in dbg:
        sh, dt = dbg_shapes[nm]
        OUT("dbg_" + nm, sh, dt)
        dbg_out[nm] = ap["dbg_" + nm]

    P = Prog()

    from contextlib import ExitStack
    with ExitStack() as _stack:
        BFh = _stack.enter_context(nc.sbuf_tensor("BF", [128, BF_TOTAL], BF16))
        FPh = _stack.enter_context(nc.sbuf_tensor("FP", [128, FP_TOTAL], F32))
        SMh = _stack.enter_context(nc.sbuf_tensor("SM", [128, 1024], F32))
        U3h = _stack.enter_context(nc.sbuf_tensor("U3", [128, 64], U32))
        PS0 = _stack.enter_context(nc.psum_tensor("PS0", [128, 512], F32))
        PS1 = _stack.enter_context(nc.psum_tensor("PS1", [128, 512], F32))
        PS2 = _stack.enter_context(nc.psum_tensor("PS2", [128, 512], F32))
        PS3 = _stack.enter_context(nc.psum_tensor("PS3", [128, 512], F32))
        PT_B0 = _stack.enter_context(nc.psum_tensor("PT_B0", [128, 128], BF16))
        PT_B1 = _stack.enter_context(nc.psum_tensor("PT_B1", [128, 128], BF16))
        PT_F = _stack.enter_context(nc.psum_tensor("PT_F", [128, 128], F32))
        PS4 = _stack.enter_context(nc.psum_tensor("PS4", [128, 512], F32))
        s_ld = _stack.enter_context(nc.semaphore("s_ld"))
        s_w = _stack.enter_context(nc.semaphore("s_w"))
        s_w2 = _stack.enter_context(nc.semaphore("s_w2"))
        s_dv = _stack.enter_context(nc.semaphore("s_dv"))
        s_ac = _stack.enter_context(nc.semaphore("s_ac"))
        s_mm = _stack.enter_context(nc.semaphore("s_mm"))
        s_ex = _stack.enter_context(nc.semaphore("s_ex"))
        s_sh = _stack.enter_context(nc.semaphore("s_sh"))
        s_io = _stack.enter_context(nc.semaphore("s_io"))
        s_g = _stack.enter_context(nc.semaphore("s_g"))
        s_st = _stack.enter_context(nc.semaphore("s_st"))
        block = _stack.enter_context(nc.Block())
        def B2(off, n, parts=128, p0=0):
            return BFh[p0:p0 + parts, off:off + n]

        def B3(off, a, b):
            return BFh[0:128, off:off + a * b].rearrange("p (a b) -> p a b", a=a)

        def F2(off, n, parts=128, p0=0):
            return FPh[p0:p0 + parts, off:off + n]

        def spacer():
            P.op("vector", lambda e: e.tensor_copy(out=SMh[:, 768:1024],
                                                   in_=SMh[:, 512:768]))

        ident_bf = B2(oc_ident, 128)
        ident_f = F2(of_identf, 128)
        trimask = F2(of_trimask, 128)
        eps_ap = F2(of_eps, 1)
        halfpi_ap = F2(of_hpi, 1)
        erange = F2(of_erange, E)
        im9 = F2(of_im9, E)

        for nm, (off, n, parts, isbf) in {
            "c_ident_bf": (oc_ident, 128, 128, True),
            "c_lt128": (oc_lt128, 128, 128, True),
            "c_onescol": (oc_onescol, 256, 128, True),
            "c_rowsel": (oc_rowsel, 2048, 16, True),
            "c_lt16": (oc_lt16, 16, 16, True),
            "c_ones1": (oc_ones1, 128, 1, True),
            "c_ident_f32": (of_identf, 128, 128, False),
            "c_trimask": (of_trimask, 128, 128, False),
            "c_trow": (of_trow, S, 128, False),
            "c_erange": (of_erange, E, 128, False),
            "c_im9": (of_im9, E, 128, False),
            "c_tokid": (of_tokid, NT, 128, False),
            "c_iotap": (of_iotap, 1, 128, False),
            "b1l": (of_b1l, E * 32, 128, False),
            "r_br": (of_br, E, 128, False),
        }.items():
            dst = B2(off, n, parts=parts) if isbf else F2(off, n, parts=parts)
            P.dma("sync", dst, ap[nm], s_ld)
        P.dma("gpsimd", B3(oc_wr, NC, E),
              ap["w_r"].rearrange("(c p) e -> p c e", p=128), s_ld)
        P.op("vector", lambda e: e.memset(eps_ap, 1e-5))
        P.op("vector", lambda e: e.memset(F2(of_one, 1), 1.0))
        P.op("vector", lambda e: e.memset(halfpi_ap, PI / 2), s_dv)

        names = ("p_w_q", "p_b_q", "p_phi_q", "p_w_k", "p_b_k", "p_phi_k")
        for i, nm in enumerate(names):
            dst = F2(of_pp + 8 * i, 8)
            lv = P.dma("sync", dst, ap[nm], s_ld)
            if i in (0, 3):
                P.wait("scalar", s_ld, lv)
                P.wait("scalar", s_dv, P.val(s_dv))
                P.op("scalar", lambda e, dst=dst: e.activation(
                    out=dst, in_=dst, func=AF.Abs))
                P.op("scalar", lambda e, dst=dst: e.activation(
                    out=dst, in_=dst, func=AF.Ln, bias=F2(of_one, 1)))
                P.op("scalar", lambda e, dst=dst: e.activation(
                    out=dst, in_=dst, func=AF.Exp, scale=-1.0), s_ac)
        ld_consts = P.val(s_ld)
        ac_pp = P.val(s_ac)

        def emit_ln(src_ap, dst_bf_ap):
            s_sum = F2(of_stats, 1); s_ss = F2(of_stats + 1, 1)
            s_msq = F2(of_stats + 2, 1); s_ssD = F2(of_stats + 3, 1)
            s_var = F2(of_stats + 4, 1); s_rstd = F2(of_stats + 5, 1)
            s_negm = F2(of_stats + 6, 1); s_mb = F2(of_stats + 7, 1)
            sq = F2(of_sq, D)
            P.wait("scalar", s_dv, P.val(s_dv))
            P.op("vector", lambda e: e.reduce_sum(out=s_sum, in_=src_ap, axis=AX.X),
                 s_dv)
            P.op("scalar", lambda e: e.activation(out=sq, in_=src_ap, func=AF.Square,
                                                  accum_out=s_ss), s_ac)
            P.wait("scalar", s_dv, P.val(s_dv))
            P.op("scalar", lambda e: e.activation(out=s_msq, in_=s_sum,
                                                  func=AF.Square, scale=1.0 / D))
            P.op("scalar", lambda e: e.activation(out=s_ssD, in_=s_ss, func=AF.Copy,
                                                  scale=1.0 / D))
            P.op("scalar", lambda e: e.activation(out=s_negm, in_=s_sum, func=AF.Copy,
                                                  scale=-1.0 / D), s_ac)
            P.wait("vector", s_ac, P.val(s_ac))
            P.op("vector", lambda e: e.tensor_sub(out=s_var, in0=s_ssD, in1=s_msq),
                 s_dv)
            P.wait("scalar", s_dv, P.val(s_dv))
            P.op("scalar", lambda e: e.activation(out=s_rstd, in_=s_var, func=AF.Ln,
                                                  bias=eps_ap))
            P.op("scalar", lambda e: e.activation(out=s_rstd, in_=s_rstd, func=AF.Exp,
                                                  scale=-0.5), s_ac)
            P.wait("vector", s_ac, P.val(s_ac))
            P.op("vector", lambda e: e.tensor_scalar(out=s_mb, in0=s_negm,
                                                     scalar1=s_rstd, scalar2=None,
                                                     op0=AL.mult), s_dv)
            spacer()
            P.op("vector", lambda e: e.tensor_scalar(out=dst_bf_ap, in0=src_ap,
                                                     scalar1=s_rstd, scalar2=s_mb,
                                                     op0=AL.mult, op1=AL.add), s_dv)

        def emit_trig(tho, n, ko, uo, wo, sin_out, cos_out, add_out=None,
                      add_srcs=None):
            th = F2(tho, n)
            k = F2(ko, n); u = F2(uo, n); w = F2(wo, n)
            P.op("vector", lambda e: e.tensor_scalar(out=k, in0=th, scalar1=INV_2PI,
                                                     scalar2=MAGIC, op0=AL.mult,
                                                     op1=AL.add))
            P.op("vector", lambda e: e.tensor_scalar(out=k, in0=k, scalar1=MAGIC,
                                                     scalar2=None, op0=AL.subtract))
            P.op("vector", lambda e: e.scalar_tensor_tensor(
                out=u, in0=k, scalar=-CW1, in1=th, op0=AL.mult, op1=AL.add))
            P.op("vector", lambda e: e.scalar_tensor_tensor(
                out=u, in0=k, scalar=-CW2, in1=u, op0=AL.mult, op1=AL.add))
            P.op("vector", lambda e: e.tensor_scalar(out=w, in0=u, scalar1=PI / 2,
                                                     scalar2=None, op0=AL.is_gt))
            P.op("vector", lambda e: e.scalar_tensor_tensor(
                out=w, in0=w, scalar=-TWO_PI, in1=u, op0=AL.mult, op1=AL.add))
            P.op("vector", lambda e: e.tensor_scalar(out=w, in0=w, scalar1=PI / 2,
                                                     scalar2=-1.5 * PI, op0=AL.min,
                                                     op1=AL.max))
            P.op("vector", lambda e: e.tensor_scalar(out=u, in0=u, scalar1=PI,
                                                     scalar2=-PI, op0=AL.min,
                                                     op1=AL.max), s_dv)
            P.wait("scalar", s_dv, P.val(s_dv))
            P.op("scalar", lambda e: e.activation(out=sin_out, in_=u, func=AF.Sin))
            P.op("scalar", lambda e: e.activation(out=cos_out, in_=w, func=AF.Sin,
                                                  bias=halfpi_ap), s_ac)
            if add_out is not None:
                a0, a1 = add_srcs if add_srcs is not None else (sin_out, cos_out)
                P.wait("vector", s_ac, P.val(s_ac))
                P.op("vector", lambda e: e.tensor_add(out=add_out, in0=a0,
                                                      in1=a1), s_dv)

        x_tiled = ap["x"].rearrange("(t p) d -> t p d", p=128)
        P.wait("vector", s_ld, ld_consts)
        P.wait("scalar", s_ld, ld_consts)
        for t in range(NT):
            lv = P.dma("sync", F2(of_xtile, D), x_tiled[t], s_ld)
            P.wait("vector", s_ld, lv)
            P.wait("scalar", s_ld, lv)
            emit_ln(F2(of_xtile, D), B2(o_ln1tm + t * D, D))
            P.wait("sync", s_dv, P.val(s_dv))
            P.wait("sync", s_ac, P.val(s_ac))
        ln1_dv = P.val(s_dv)
        if "ln1" in dbg:
            P.wait("gpsimd", s_dv, ln1_dv)
            P.dma("gpsimd", dbg_out["ln1"], B2(o_ln1tm, NT * D), s_io)

        if phases >= 2:
            P.wait("tensor", s_ld, ld_consts)
            P.wait("tensor", s_dv, ln1_dv)
            tcnt = 0
            for t in range(NT):
                for c in range(NC):
                    src = B2(o_ln1tm + t * D + 128 * c, 128)
                    pt = PT_B0[:, 0:128] if (tcnt % 2 == 0) else PT_B1[:, 0:128]
                    P.op("tensor", lambda e, pt=pt, src=src:
                         e.transpose(pt, src, ident_bf), s_mm)
                    P.wait("vector", s_mm, P.val(s_mm))
                    dst = B2(o_ln1fm + c * S + 128 * t, 128)
                    P.op("vector", lambda e, dst=dst, pt=pt:
                         e.tensor_copy(out=dst, in_=pt), s_dv)
                    if tcnt >= 1:
                        P.wait("tensor", s_dv, P.val(s_dv) - 1)
                    tcnt += 1
            ln1fm_dv = P.val(s_dv)

        if phases >= 3:
            lv = P.dma("sync", F2(of_rW, D), ap["r_w_v"], s_ld)
            lv = P.dma("sync", F2(of_rB, D), ap["r_b_v"], s_ld)
            P.wait("vector", s_ld, lv)
            P.wait("scalar", s_ld, P.val(s_ld))
            P.op("scalar", lambda e: e.activation(out=F2(of_rW, D), in_=F2(of_rW, D),
                                                  func=AF.Abs))
            P.op("scalar", lambda e: e.activation(out=F2(of_rW, D), in_=F2(of_rW, D),
                                                  func=AF.Ln, bias=F2(of_one, 1)))
            P.op("scalar", lambda e: e.activation(out=F2(of_rW, D), in_=F2(of_rW, D),
                                                  func=AF.Exp, scale=-1.0), s_ac)
            vtm4 = BFh[0:128, o_vtm:o_vtm + NT * H * 65].rearrange(
                "p (t h c) -> p t h c", t=NT, h=H)
            P.op("vector", lambda e: e.memset(vtm4[:, :, :, 64:65], 1.0), s_dv)
            P.wait("vector", s_ac, P.val(s_ac))
            for t in range(NT):
                th = F2(of_th, D)
                src = B2(o_ln1tm + t * D, D)
                P.op("vector", lambda e, th=th, src=src: e.tensor_tensor(
                    out=th, in0=src, in1=F2(of_rW, D), op=AL.mult))
                P.op("vector", lambda e, th=th: e.tensor_tensor(
                    out=th, in0=th, in1=F2(of_rB, D), op=AL.add))
                vts = vtm4[:, t, :, 0:64]
                sv3 = F2(of_sv, D).rearrange("p (h c) -> p h c", h=H)
                cv3 = F2(of_cv, D).rearrange("p (h c) -> p h c", h=H)
                emit_trig(of_th, D, of_k1, of_u1, of_w1,
                          F2(of_sv, D), F2(of_cv, D), add_out=vts,
                          add_srcs=(sv3, cv3))
            v_dv = P.val(s_dv)
            if "v" in dbg:
                P.wait("gpsimd", s_dv, v_dv)
                P.dma("gpsimd", dbg_out["v"], B2(o_vtm, NT * H * 65), s_io)

        if phases >= 4:
            av_free_dv = [0, 0]
            for hp in range(8):
                mm_prev = P.val(s_mm)
                for (dst, base) in ((o_qp, 0), (o_kp, 24)):
                    inv_ap = F2(of_pp + base + 0 + hp, 1)
                    b_ap = F2(of_pp + base + 8 + hp, 1)
                    phi_ap = F2(of_pp + base + 16 + hp, 1)
                    th = F2(of_thq, S)
                    src = B2(o_ln1fm + hp * S, S)
                    P.wait("vector", s_ac, ac_pp)
                    P.wait("vector", s_dv, max(ln1fm_dv, v_dv))
                    P.wait("vector", s_mm, mm_prev)
                    P.op("vector", lambda e, th=th, src=src, inv_ap=inv_ap, b_ap=b_ap:
                         e.tensor_scalar(out=th, in0=src, scalar1=inv_ap, scalar2=b_ap,
                                         op0=AL.mult, op1=AL.add))
                    P.op("vector", lambda e, th=th, phi_ap=phi_ap:
                         e.scalar_tensor_tensor(out=th, in0=F2(of_trow, S),
                                                scalar=phi_ap, in1=th,
                                                op0=AL.mult, op1=AL.add))
                    emit_trig(of_thq, S, of_k2, of_u2, of_w2,
                              B2(o_sS, S), B2(o_sC, S))
                    P.wait("vector", s_ac, P.val(s_ac))
                    P.op("vector", lambda e, dst=dst: e.tensor_copy(
                        out=BFh[0:64, dst:dst + S], in_=BFh[0:64, o_sC:o_sC + S]))
                    P.op("vector", lambda e, dst=dst: e.tensor_copy(
                        out=BFh[64:128, dst + S:dst + 2 * S],
                        in_=BFh[64:128, o_sS:o_sS + S]), s_dv)
                    P.wait("gpsimd", s_ac, P.val(s_ac))
                    P.wait("gpsimd", s_mm, mm_prev)
                    P.dma("gpsimd", BFh[64:128, dst:dst + S],
                          BFh[0:64, o_sS:o_sS + S], s_sh)
                    P.dma("gpsimd", BFh[0:64, dst + S:dst + 2 * S],
                          BFh[64:128, o_sC:o_sC + S], s_sh)
                    P.wait("vector", s_sh, P.val(s_sh))
                    P.wait("scalar", s_sh, P.val(s_sh))
                qk_dv = P.val(s_dv)
                qk_sh = P.val(s_sh)
                P.wait("tensor", s_dv, qk_dv)
                P.wait("tensor", s_sh, qk_sh)
                for hh in range(2):
                    h = 2 * hp + hh
                    qoff = o_qp + hh * S
                    koff = o_kp + hh * S
                    SCB = (PS0, PS1, PS4)
                    for qt in range(4):
                        av_ps = PS2 if (qt % 2 == 0) else PS3
                        nkt = 4 * qt + 4
                        scv = [0] * nkt
                        exv = [0] * nkt
                        avv = [0] * nkt

                        def emit_sc(kt):
                            dd = kt - 4 * qt
                            qlo_rel = 128 * dd if dd >= 0 else 0
                            qlo = qt * 512 + qlo_rel
                            qn = 512 - qlo_rel
                            sc_ps = SCB[kt % 3]
                            if kt >= 3:
                                P.wait("tensor", s_ex, exv[kt - 3])
                            P.op("tensor", lambda e, sc_ps=sc_ps, koff=koff, kt=kt,
                                 qoff=qoff, qlo=qlo, qn=qn:
                                 e.matmul(sc_ps[:, 0:qn], B2(koff + kt * 128, 128),
                                          B2(qoff + qlo, qn), start=True, stop=True),
                                 s_mm)
                            scv[kt] = P.val(s_mm)

                        def emit_exp(kt):
                            dd = kt - 4 * qt
                            qlo_rel = 128 * dd if dd >= 0 else 0
                            qn = 512 - qlo_rel
                            sc_ps = SCB[kt % 3]
                            if dd >= 0:
                                P.wait("vector", s_mm, scv[kt])
                                P.op("vector", lambda e, sc_ps=sc_ps:
                                     e.tensor_tensor(out=sc_ps[:, 0:128],
                                                     in0=sc_ps[:, 0:128],
                                                     in1=trimask, op=AL.add), s_dv)
                                P.wait("scalar", s_dv, P.val(s_dv))
                            else:
                                P.wait("scalar", s_mm, scv[kt])
                            if kt >= 4:
                                P.wait("scalar", s_mm, avv[kt - 4])
                            psb = o_psb + (kt % 4) * 512
                            P.op("scalar", lambda e, psb=psb, sc_ps=sc_ps, qn=qn:
                                 e.activation(out=B2(psb, qn), in_=sc_ps[:, 0:qn],
                                              func=AF.Exp, scale=SM_SCALE), s_ex)
                            exv[kt] = P.val(s_ex)

                        def emit_av(kt):
                            dd = kt - 4 * qt
                            qlo_rel = 128 * dd if dd >= 0 else 0
                            qn = 512 - qlo_rel
                            psb = o_psb + (kt % 4) * 512
                            P.wait("tensor", s_ex, exv[kt])
                            if kt == 0 and av_free_dv[qt % 2]:
                                P.wait("tensor", s_dv, av_free_dv[qt % 2])
                            vsl = vtm4[:, kt, h, :]
                            P.op("tensor", lambda e, av_ps=av_ps, vsl=vsl, psb=psb,
                                 qlo_rel=qlo_rel, qn=qn, first=(kt == 0),
                                 last=(kt == nkt - 1):
                                 e.matmul(av_ps[0:65, qlo_rel:qlo_rel + qn], vsl,
                                          B2(psb, qn), start=first, stop=last), s_mm)
                            avv[kt] = P.val(s_mm)

                        for kt in range(nkt):
                            emit_sc(kt)
                            emit_exp(kt)
                            if kt >= 2:
                                emit_av(kt - 2)
                        emit_av(nkt - 2)
                        emit_av(nkt - 1)

                        stg = SMh[0:65, 0:512]
                        P.wait("vector", s_mm, avv[nkt - 1])
                        P.op("vector", lambda e, av_ps=av_ps, stg=stg:
                             e.tensor_copy(out=stg, in_=av_ps[0:65, 0:512]), s_dv)
                        av_free_dv[qt % 2] = P.val(s_dv)
                        oS = of_sq
                        cp_dv = []
                        for sub in range(4):
                            P.wait("tensor", s_dv,
                                   cp_dv[sub - 1] if sub >= 1
                                   else av_free_dv[qt % 2])
                            P.op("tensor", lambda e, stg=stg, sub=sub:
                                 e.transpose(PT_F[:, 0:65],
                                             stg[0:65, 128 * sub:128 * (sub + 1)],
                                             ident_f[0:65, 0:65]), s_mm)
                            P.wait("vector", s_mm, P.val(s_mm))
                            P.op("vector", lambda e, sub=sub, oS=oS: e.tensor_copy(
                                 out=F2(oS + 65 * sub, 65), in_=PT_F[:, 0:65]),
                                 s_dv)
                            cp_dv.append(P.val(s_dv))
                        sums3 = FPh[0:128, oS:oS + 260].rearrange(
                            "p (s c) -> p s c", s=4)[:, :, 64:65]
                        P.wait("scalar", s_dv, cp_dv[-1])
                        P.op("scalar", lambda e, sums3=sums3: e.activation(
                             out=sums3, in_=sums3, func=AF.Ln))
                        P.op("scalar", lambda e, sums3=sums3: e.activation(
                             out=sums3, in_=sums3, func=AF.Exp, scale=-1.0), s_ac)
                        P.wait("vector", s_ac, P.val(s_ac))
                        for sub in range(4):
                            t_idx = qt * 4 + sub
                            dst = B2(o_attn + t_idx * D + h * 64, 64)
                            rec = F2(oS + 65 * sub + 64, 1)
                            P.op("vector", lambda e, dst=dst, rec=rec, oS=oS,
                                 sub=sub:
                                 e.tensor_scalar(out=dst, in0=F2(oS + 65 * sub, 64),
                                                 scalar1=rec, scalar2=None,
                                                 op0=AL.mult), s_dv)
            attn_dv = P.val(s_dv)
            if "attn" in dbg:
                P.wait("gpsimd", s_dv, attn_dv)
                P.dma("gpsimd", dbg_out["attn"], B2(o_attn, NT * D), s_io)
            if "q0" in dbg:
                P.wait("gpsimd", s_dv, attn_dv)
                P.dma("gpsimd", dbg_out["q0"], B2(o_qp, 2 * S), s_io)
            if "k0" in dbg:
                P.wait("gpsimd", s_dv, attn_dv)
                P.dma("gpsimd", dbg_out["k0"], B2(o_kp, 2 * S), s_io)

        if phases >= 5:
            P.wait("sync", s_dv, attn_dv)
            lv = P.dma("sync", F2(of_rW, D), ap["r_w_out"], s_ld)
            lv = P.dma("sync", F2(of_rB, D), ap["r_b_out"], s_ld)
            P.wait("vector", s_ld, lv)
            P.wait("vector", s_dv, attn_dv)
            P.wait("scalar", s_ld, P.val(s_ld))
            P.op("scalar", lambda e: e.activation(out=F2(of_rW, D), in_=F2(of_rW, D),
                                                  func=AF.Abs))
            P.op("scalar", lambda e: e.activation(out=F2(of_rW, D), in_=F2(of_rW, D),
                                                  func=AF.Ln, bias=F2(of_one, 1)))
            P.op("scalar", lambda e: e.activation(out=F2(of_rW, D), in_=F2(of_rW, D),
                                                  func=AF.Exp, scale=-1.0), s_ac)
            P.wait("vector", s_ac, P.val(s_ac))
            x1_tiled = ap["x1_hbm"].rearrange("(t p) d -> t p d", p=128)
            ln2_hbm_t = ap["ln2_hbm"].rearrange("(t p) d -> t p d", p=128)
            for t in range(NT):
                th = F2(of_th, D)
                att_bf = B2(o_attn + t * D, D)
                P.op("vector", lambda e, th=th, att_bf=att_bf: e.tensor_tensor(
                    out=th, in0=att_bf, in1=F2(of_rW, D), op=AL.mult))
                P.op("vector", lambda e, th=th: e.tensor_tensor(
                    out=th, in0=th, in1=F2(of_rB, D), op=AL.add))
                emit_trig(of_th, D, of_k1, of_u1, of_w1, F2(of_sv, D), F2(of_cv, D),
                          add_out=F2(of_sq, D))
                lvx = P.dma("sync", F2(of_xtile, D), x_tiled[t], s_ld)
                P.wait("vector", s_ld, lvx)
                P.op("vector", lambda e: e.tensor_tensor(
                    out=F2(of_xtile, D), in0=F2(of_xtile, D),
                    in1=F2(of_sq, D), op=AL.add), s_dv)
                P.wait("gpsimd", s_dv, P.val(s_dv))
                P.dma("gpsimd", x1_tiled[t], F2(of_xtile, D), s_io)
                if "x1" in dbg:
                    P.dma("gpsimd", dbg_out["x1"][:, t * D:(t + 1) * D],
                          F2(of_xtile, D), s_io)
                if "thout" in dbg:
                    P.dma("gpsimd", dbg_out["thout"][:, t * D:(t + 1) * D],
                          F2(of_sq, D), s_io)
                emit_ln(F2(of_xtile, D), B2(o_ln2tm + t * D, D))
                P.wait("gpsimd", s_dv, P.val(s_dv))
                P.dma("gpsimd", ln2_hbm_t[t], B2(o_ln2tm + t * D, D), s_io)
                P.wait("sync", s_dv, P.val(s_dv))
                P.wait("sync", s_ac, P.val(s_ac))
                P.wait("sync", s_io, P.val(s_io))
            ln2_dv = P.val(s_dv)
            ln2_io = P.val(s_io)
            if "ln2" in dbg:
                P.wait("gpsimd", s_dv, ln2_dv)
                P.dma("gpsimd", dbg_out["ln2"], B2(o_ln2tm, NT * D), s_io)

        if phases >= 6:
            P.wait("tensor", s_dv, ln2_dv)
            tcnt = 0
            for t in range(NT):
                for c in range(NC):
                    src = B2(o_ln2tm + t * D + 128 * c, 128)
                    pt = PT_B0[:, 0:128] if (tcnt % 2 == 0) else PT_B1[:, 0:128]
                    P.op("tensor", lambda e, pt=pt, src=src:
                         e.transpose(pt, src, ident_bf), s_mm)
                    P.wait("vector", s_mm, P.val(s_mm))
                    dst = B2(o_ln2fm + c * S + 128 * t, 128)
                    P.op("vector", lambda e, dst=dst, pt=pt:
                         e.tensor_copy(out=dst, in_=pt), s_dv)
                    if tcnt >= 1:
                        P.wait("tensor", s_dv, P.val(s_dv) - 1)
                    tcnt += 1
            ln2fm_dv = P.val(s_dv)
            P.wait("tensor", s_dv, ln2fm_dv)
            P.wait("tensor", s_ld, P.val(s_ld))
            P.wait("sync", s_dv, ln2_dv)
            lv64 = P.dma("sync", F2(of_tr64, 64), ap["c_tr64"], s_ld)
            P.op("vector", lambda e: e.memset(F2(of_nine, 1), 9.0), s_dv)
            nine_ap = F2(of_nine, 1)
            for t in range(NT):
                for c in range(NC):
                    P.op("tensor", lambda e, t=t, c=c:
                         e.matmul(PS0[:, 0:E], B2(o_ln2fm + c * S + 128 * t, 128),
                                  B3(oc_wr, NC, E)[:, c, :],
                                  start=(c == 0), stop=(c == NC - 1)),
                         s_mm if c == NC - 1 else None)
                P.wait("vector", s_mm, P.val(s_mm))
                lg = F2(of_prod, E)
                expP = F2(of_expall + E * t, E)
                ssum = F2(of_recall + t, 1)
                P.op("vector", lambda e, lg=lg: e.tensor_add(
                    out=lg, in0=PS0[:, 0:E], in1=F2(of_br, E)), s_dv)
                P.wait("tensor", s_dv, P.val(s_dv))
                P.wait("scalar", s_dv, P.val(s_dv))
                P.op("scalar", lambda e, lg=lg, expP=expP, ssum=ssum:
                     e.activation(out=expP, in_=lg, func=AF.Exp, accum_out=ssum))
                P.op("scalar", lambda e, ssum=ssum:
                     e.activation(out=ssum, in_=ssum, func=AF.Ln))
                P.op("scalar", lambda e, ssum=ssum:
                     e.activation(out=ssum, in_=ssum, func=AF.Exp, scale=-1.0), s_ac)
                P.wait("vector", s_ac, P.val(s_ac))
                srt = F2(of_srt, E)
                P.op("vector", lambda e, srt=srt, expP=expP:
                     e.max(out=srt, in_=expP), s_dv)
                P.wait("scalar", s_dv, P.val(s_dv))
                sc0 = F2(of_sc2, 1); sc1 = F2(of_sc2 + 1, 1)
                P.op("scalar", lambda e, srt=srt, sc0=sc0:
                     e.activation(out=sc0, in_=srt[:, 0:1], func=AF.Copy))
                P.op("scalar", lambda e, srt=srt, sc1=sc1:
                     e.activation(out=sc1, in_=srt[:, 1:2], func=AF.Copy), s_ac)
                P.wait("vector", s_ac, P.val(s_ac))
                spacer()
                P.op("vector", lambda e, srt=srt, ssum=ssum, t=t: e.tensor_scalar(
                    out=F2(of_Parr + 2 * t, 1), in0=srt[:, 0:1], scalar1=ssum,
                    scalar2=None, op0=AL.mult))
                P.op("vector", lambda e, srt=srt, ssum=ssum, t=t: e.tensor_scalar(
                    out=F2(of_Parr + 2 * t + 1, 1), in0=srt[:, 1:2], scalar1=ssum,
                    scalar2=None, op0=AL.mult), s_dv)
                r8a = F2(of_r8a, E)
                P.op("vector", lambda e, r8a=r8a, expP=expP, sc0=sc0:
                     e.scalar_tensor_tensor(out=r8a, in0=expP, scalar=sc0, in1=im9,
                                            op0=AL.is_equal, op1=AL.mult), s_dv)
                spacer()
                P.op("vector", lambda e, r8a=r8a: e.tensor_reduce(
                    out=F2(of_e1m9, 1), in_=r8a, axis=AX.X, op=AL.min), s_dv)
                P.wait("scalar", s_dv, P.val(s_dv))
                P.op("scalar", lambda e, t=t: e.activation(
                    out=F2(of_Earr + 2 * t, 1), in_=F2(of_e1m9, 1), func=AF.Relu,
                    bias=nine_ap), s_ac)
                P.wait("vector", s_ac, P.val(s_ac))
                e1ap = F2(of_Earr + 2 * t, 1)
                P.op("vector", lambda e, e1ap=e1ap: e.tensor_scalar(
                    out=B2(oc_ohs, E), in0=erange, scalar1=e1ap, scalar2=None,
                    op0=AL.is_equal), s_dv)
                r8b = F2(of_r8b, E)
                P.op("vector", lambda e, r8b=r8b, expP=expP, sc1=sc1:
                     e.scalar_tensor_tensor(out=r8b, in0=expP, scalar=sc1, in1=im9,
                                            op0=AL.is_equal, op1=AL.mult))
                P.op("vector", lambda e, e1ap=e1ap: e.tensor_scalar(
                    out=F2(of_prod + 8, E), in0=erange, scalar1=e1ap, scalar2=20.0,
                    op0=AL.is_equal, op1=AL.mult), s_dv)
                spacer()
                P.op("vector", lambda e, r8b=r8b: e.tensor_add(
                    out=r8b, in0=r8b, in1=F2(of_prod + 8, E)), s_dv)
                spacer()
                P.op("vector", lambda e, r8b=r8b: e.tensor_reduce(
                    out=F2(of_e2m9, 1), in_=r8b, axis=AX.X, op=AL.min), s_dv)
                P.wait("scalar", s_dv, P.val(s_dv))
                P.op("scalar", lambda e, t=t: e.activation(
                    out=F2(of_Earr + 2 * t + 1, 1), in_=F2(of_e2m9, 1), func=AF.Relu,
                    bias=nine_ap), s_ac)
                P.wait("vector", s_ac, P.val(s_ac))
                e2ap = F2(of_Earr + 2 * t + 1, 1)
                P.op("vector", lambda e, e2ap=e2ap: e.tensor_scalar(
                    out=B2(oc_oh2, E), in0=erange, scalar1=e2ap, scalar2=None,
                    op0=AL.is_equal), s_dv)
                P.wait("tensor", s_dv, P.val(s_dv))
                P.op("tensor", lambda e, t=t:
                     e.matmul(PS1[0:16, 0:E], B2(oc_onescol + 16 * t, 16),
                              B2(oc_ohs, E), start=(t == 0), stop=False))
                P.op("tensor", lambda e, t=t:
                     e.matmul(PS1[0:16, 0:E], B2(oc_onescol + 16 * t, 16),
                              B2(oc_oh2, E), start=False, stop=(t == NT - 1)),
                     s_mm if t == NT - 1 else None)
            P.wait("vector", s_mm, P.val(s_mm))
            P.op("vector", lambda e: e.tensor_copy(
                out=BFh[0:16, oc_apbf + 8:oc_apbf + 16], in_=PS1[0:16, 0:E]), s_dv)
            P.wait("tensor", s_dv, P.val(s_dv))
            P.op("tensor", lambda e: e.matmul(
                PS2[0:16, 0:E], BFh[0:16, oc_lt16:oc_lt16 + 16],
                BFh[0:16, oc_apbf + 8:oc_apbf + 16], start=True, stop=True), s_mm)
            P.wait("vector", s_mm, P.val(s_mm))
            a_f = F2(of_prod, E, parts=16)
            kq = F2(of_prod + 8, E, parts=16)
            P.op("vector", lambda e, a_f=a_f: e.tensor_copy(
                out=a_f, in_=PS2[0:16, 0:E]), s_dv)
            spacer()
            P.op("vector", lambda e, a_f=a_f, kq=kq: e.tensor_scalar(
                out=kq, in0=a_f, scalar1=1.0 / 256.0, scalar2=MAGIC - 0.5,
                op0=AL.mult, op1=AL.add), s_dv)
            spacer()
            P.op("vector", lambda e, kq=kq: e.tensor_scalar(
                out=kq, in0=kq, scalar1=MAGIC, scalar2=None, op0=AL.subtract), s_dv)
            spacer()
            P.op("vector", lambda e, kq=kq: e.tensor_scalar(
                out=BFh[0:16, oc_apbf:oc_apbf + 8], in0=kq, scalar1=256.0,
                scalar2=None, op0=AL.mult), s_dv)
            P.op("vector", lambda e, kq=kq, a_f=a_f: e.scalar_tensor_tensor(
                out=BFh[0:16, oc_aplo:oc_aplo + 8], in0=kq, scalar=-256.0,
                in1=a_f, op0=AL.mult, op1=AL.add), s_dv)
            apre_dv = P.val(s_dv)

        if phases >= 10:
            w1_h = ap["w1"].rearrange("e (c p) h -> e p c h", p=128)
            w2_h = ap["w2"].rearrange("e (hc p) d -> e p hc d", p=128)
            slot_off = [o_wA, o_wB]
            pre_free = P.val(s_mm)
            slot_free_mm = [pre_free, pre_free]
            loads = []
            for e_i in range(E):
                loads += [(e_i, "w1", 0), (e_i, "w1", 1),
                          (e_i, "w2", 0), (e_i, "w2", 1)]
            lv_load = [None] * len(loads)
            cv_load = [0] * len(loads)
            conv_scr = [oT + 4096, oT + 6144]
            conv_free = [P.val(s_dv), P.val(s_dv)]

            def ensure_load(i):
                if i >= len(loads) or lv_load[i] is not None:
                    return
                e_i, kind, half = loads[i]
                sl = i % 2
                P.wait("gpsimd", s_mm, slot_free_mm[sl])
                if kind == "w1":
                    src = w1_h[e_i][:, :, 2048 * half:2048 * (half + 1)]
                    dst = B3(slot_off[sl], 8, 2048)
                    lv_load[i] = P.dma("gpsimd", dst[:, 0:4, :], src[:, 0:4, :],
                                       s_w)
                    csl = [src[:, 4 + j, :] for j in range(4)]
                    cdl = [dst[:, 4 + j, :] for j in range(4)]
                else:
                    src = w2_h[e_i][:, :, 512 * half:512 * (half + 1)]
                    dst = B3(slot_off[sl], 32, 512)
                    lv_load[i] = P.dma("gpsimd", dst[:, 0:16, :], src[:, 0:16, :],
                                       s_w)
                    csl = [src[:, 16 + 4 * j:20 + 4 * j, :] for j in range(4)]
                    cdl = [dst[:, 16 + 4 * j:20 + 4 * j, :] for j in range(4)]
                for j in range(4):
                    scr = conv_scr[j % 2]
                    sv3 = FPh[0:128, scr:scr + 2048]
                    if kind == "w2":
                        sv3 = sv3.rearrange("p (a c) -> p a c", c=512)
                    P.wait("sync", s_dv, conv_free[j % 2])
                    cw = P.dma("sync", sv3, csl[j], s_w2)
                    P.wait("vector", s_w2, cw)
                    if j == 0:
                        P.wait("vector", s_mm, slot_free_mm[sl])
                    P.op("vector", lambda e, cd=cdl[j], sv3=sv3: e.tensor_copy(
                         out=cd, in_=sv3), s_dv)
                    conv_free[j % 2] = P.val(s_dv)
                cv_load[i] = P.val(s_dv)

            ensure_load(0)
            ensure_load(1)

        if phases >= 7:
            P.wait("tensor", s_dv, apre_dv)
            for t in range(NT):
                e1ap = F2(of_Earr + 2 * t, 1)
                e2ap = F2(of_Earr + 2 * t + 1, 1)
                P.wait("vector", s_mm, P.val(s_mm))
                P.op("vector", lambda e, e1ap=e1ap: e.tensor_scalar(
                    out=B2(oc_ohs, E), in0=erange, scalar1=e1ap, scalar2=None,
                    op0=AL.is_equal))
                P.op("vector", lambda e, e2ap=e2ap: e.tensor_scalar(
                    out=B2(oc_oh2, E), in0=erange, scalar1=e2ap, scalar2=None,
                    op0=AL.is_equal), s_dv)
                P.wait("tensor", s_dv, P.val(s_dv))
                P.op("tensor", lambda e, t=t: e.matmul(
                    PS0[:, 0:E],
                    BFh[0:16, oc_rowsel + 128 * t:oc_rowsel + 128 * (t + 1)],
                    BFh[0:16, oc_apbf:oc_apbf + 8], start=True, stop=False))
                P.op("tensor", lambda e, t=t: e.matmul(
                    PS0[:, 0:E],
                    BFh[0:16, oc_rowsel + 128 * t:oc_rowsel + 128 * (t + 1)],
                    BFh[0:16, oc_aplo:oc_aplo + 8], start=False, stop=False))
                P.op("tensor", lambda e: e.matmul(
                    PS0[:, 0:E], BFh[0:128, oc_lt128:oc_lt128 + 128],
                    B2(oc_ohs, E), start=False, stop=False))
                P.op("tensor", lambda e: e.matmul(
                    PS0[:, 0:E], BFh[0:128, oc_lt128:oc_lt128 + 128],
                    B2(oc_oh2, E), start=False, stop=True), s_mm)
                P.wait("vector", s_mm, P.val(s_mm))
                P.op("vector", lambda e, e1ap=e1ap: e.scalar_tensor_tensor(
                    out=F2(of_prod, E), in0=erange, scalar=e1ap, in1=PS0[:, 0:E],
                    op0=AL.is_equal, op1=AL.mult))
                P.op("vector", lambda e, e2ap=e2ap: e.scalar_tensor_tensor(
                    out=F2(of_prod + 8, E), in0=erange, scalar=e2ap, in1=PS0[:, 0:E],
                    op0=AL.is_equal, op1=AL.mult), s_dv)
                spacer()
                P.op("vector", lambda e, t=t: e.reduce_sum(
                    out=F2(of_posall + 2 * t, 1), in_=F2(of_prod, E), axis=AX.X))
                P.op("vector", lambda e, t=t: e.reduce_sum(
                    out=F2(of_posall + 2 * t + 1, 1), in_=F2(of_prod + 8, E),
                    axis=AX.X), s_dv)
                P.wait("tensor", s_dv, P.val(s_dv))
            P.wait("vector", s_ld, lv64)
            spacer()
            posall = F2(of_posall, 64)
            nk = F2(of_nk, 64); kp = F2(of_kp, 64)
            slc = F2(of_slc, 64); sca = F2(of_sca, 64); scb = F2(of_scb, 64)
            P.op("vector", lambda e: e.tensor_scalar(
                out=nk, in0=posall, scalar1=float(CAP) - 0.5, scalar2=None,
                op0=AL.is_gt), s_dv)
            P.wait("scalar", s_dv, P.val(s_dv))
            P.op("scalar", lambda e: e.activation(
                out=kp, in_=nk, func=AF.Relu, scale=-1.0, bias=F2(of_one, 1)), s_ac)
            P.wait("vector", s_ac, P.val(s_ac))
            P.op("vector", lambda e: e.tensor_mul(
                out=F2(of_Garr, 64), in0=F2(of_Parr, 64), in1=kp), s_dv)
            P.op("vector", lambda e: e.tensor_scalar(
                out=slc, in0=posall, scalar1=float(CAP - 1), scalar2=None,
                op0=AL.min), s_dv)
            spacer()
            P.op("vector", lambda e: e.scalar_tensor_tensor(
                out=slc, in0=F2(of_Earr, 64), scalar=float(CAP), in1=slc,
                op0=AL.mult, op1=AL.add), s_dv)
            spacer()
            P.op("vector", lambda e: e.tensor_copy(
                out=U3h[0:128, 0:32], in_=F2(of_slc, 32)), s_dv)
            P.op("vector", lambda e: e.tensor_mul(out=sca, in0=slc, in1=kp))
            P.op("vector", lambda e: e.tensor_mul(out=scb, in0=F2(of_tr64, 64),
                                                  in1=nk), s_dv)
            spacer()
            P.op("vector", lambda e: e.tensor_add(out=sca, in0=sca, in1=scb), s_dv)
            spacer()
            P.op("vector", lambda e: e.tensor_copy(
                out=U3h[0:128, 32:64], in_=F2(of_sca, 32)), s_dv)
            route_dv = P.val(s_dv)
            if "route" in dbg:
                P.wait("gpsimd", s_dv, route_dv)
                P.dma("gpsimd", dbg_out["route"],
                      F2(of_Earr, 256), s_io)

        if phases >= 8:
            P.op("vector", lambda e: e.memset(SMh[:, 0:512], 0.0), s_dv)
            zdv = P.val(s_dv)
            P.wait("gpsimd", s_dv, zdv)
            sb_flat = ap["slotbuf"].rearrange("(a p) c -> p a c", p=128)
            for (lo, hi) in ((0, 8), (8, 16), (16, 21)):
                P.dma("gpsimd", sb_flat[:, lo:hi, :],
                      SMh[:, 0:(hi - lo) * 64].rearrange("p (a c) -> p a c", c=64), s_io)
            zio = P.val(s_io)
            P.wait("vector", s_dv, route_dv)
            P.op("vector", lambda e: e.memset(F2(of_scsrc, NT * 64), 0.0), s_dv)
            scs3 = FPh[0:128, of_scsrc:of_scsrc + NT * 64].rearrange(
                "p (t c) -> p t c", t=NT)
            P.wait("vector", s_dv, P.val(s_dv))
            P.op("vector", lambda e: e.tensor_scalar(
                out=scs3[:, :, 0:1].rearrange("p t c -> p (t c)"),
                in0=F2(of_tokid, NT),
                scalar1=1.0, scalar2=None, op0=AL.add), s_dv)
            scdv = P.val(s_dv)
            P.wait("gpsimd", s_dv, scdv)
            P.wait("gpsimd", s_io, zio)
            for t in range(NT):
                for kk in range(2):
                    idx_col = U3h[0:128, 32 + 2 * t + kk:33 + 2 * t + kk]
                    P.iscatter(ap["slotbuf"], idx_col, scs3[:, t, :], s_g)
            P.wait("gpsimd", s_g, P.val(s_g))
            if "u3" in dbg:
                P.dma("gpsimd", dbg_out["u3"], U3h[0:128, 0:64], s_io)
            if "scs" in dbg:
                P.dma("gpsimd", dbg_out["scs"], F2(of_scsrc, NT * 64), s_io)
            if "fpd" in dbg:
                P.dma("gpsimd", dbg_out["fpd"], F2(2300, 128), s_io)
            tok_src = ap["slotbuf"].rearrange("(b p) c -> p b c", p=128)
            P.dma("gpsimd",
                  FPh[0:128, of_tokf:of_tokf + NB].rearrange("p (a b) -> p a b", b=1),
                  tok_src[:, 0:NB, 0:1], s_io, allow_slow_non_contiguous=True)
            rv = P.val(s_io)
            P.wait("vector", s_io, rv)
            P.op("vector", lambda e: e.tensor_scalar(
                out=U3h[0:128, 2 * NT:2 * NT + NB], in0=F2(of_tokf, NB),
                scalar1=-1.0, scalar2=0.0, op0=AL.add, op1=AL.max), s_dv)
            tok_dv = P.val(s_dv)
            if "tok128" in dbg:
                P.wait("gpsimd", s_dv, tok_dv)
                P.dma("gpsimd", dbg_out["tok128"], U3h[0:128, 2 * NT:2 * NT + NB], s_io)

        if phases >= 9:
            buf3 = B3(o_buffm, 8, NSLOT)
            P.wait("gpsimd", s_dv, tok_dv)
            P.wait("gpsimd", s_io, ln2_io)
            mm_after_tile = []
            for b in range(NB):
                idx_col = U3h[0:128, 2 * NT + b:2 * NT + b + 1]
                if b >= 1:
                    P.wait("gpsimd", s_mm, mm_after_tile[b - 1])
                P.igather(B2(o_g0, D), ap["ln2_hbm"], idx_col, s_g)
                gv = P.val(s_g)
                P.wait("tensor", s_g, gv)
                for c in range(NC):
                    pt = PT_B0[:, 0:128] if (c % 2 == 0) else PT_B1[:, 0:128]
                    P.op("tensor", lambda e, pt=pt, c=c:
                         e.transpose(pt, B2(o_g0 + 128 * c, 128), ident_bf), s_mm)
                    P.wait("vector", s_mm, P.val(s_mm))
                    dst = buf3[:, c, 128 * b:128 * (b + 1)]
                    P.op("vector", lambda e, dst=dst, pt=pt:
                         e.tensor_copy(out=dst, in_=pt), s_dv)
                    P.wait("tensor", s_dv, P.val(s_dv) - 1)
                mm_after_tile.append(P.val(s_mm))
            buf_dv = P.val(s_dv)
            if "buf" in dbg:
                P.wait("gpsimd", s_dv, buf_dv)
                P.dma("gpsimd", dbg_out["buf"], B2(o_buffm, 8 * NSLOT), s_io)

        if phases >= 10:
            h3 = B3(o_hfm, 32, CAP)
            P.wait("tensor", s_dv, buf_dv)
            for e_i in range(E):
                for half in range(2):
                    i = 4 * e_i + half
                    ensure_load(i)
                    ensure_load(i + 1)
                    P.wait("tensor", s_w, lv_load[i])
                    P.wait("tensor", s_dv, cv_load[i])
                    woff = slot_off[i % 2]
                    for ht in range(16):
                        ps = PS0 if (ht % 2 == 0) else PS1
                        P.wait("tensor", s_ac, max(0, P.val(s_ac) - 1))
                        for c in range(NC):
                            P.op("tensor",
                                 lambda e, ps=ps, woff=woff, c=c, ht=ht, e_i=e_i:
                                 e.matmul(ps[:, 0:CAP],
                                          B3(woff, 8, 2048)[:, c,
                                                            128 * ht:128 * (ht + 1)],
                                          buf3[:, c, e_i * CAP:(e_i + 1) * CAP],
                                          start=(c == 0), stop=(c == NC - 1)),
                                 s_mm if c == NC - 1 else None)
                        P.wait("scalar", s_mm, P.val(s_mm))
                        hti = 16 * half + ht
                        b1_ap = F2(of_b1l + e_i * 32 + hti, 1)
                        P.op("scalar", lambda e, ps=ps, hti=hti, b1_ap=b1_ap:
                             e.activation(out=h3[:, hti, :], in_=ps[:, 0:CAP],
                                          func=AF.Relu, bias=b1_ap), s_ac)
                    slot_free_mm[i % 2] = P.val(s_mm)
                P.wait("tensor", s_ac, P.val(s_ac))
                for chalf in range(2):
                    i = 4 * e_i + 2 + chalf
                    ensure_load(i)
                    ensure_load(i + 1)
                    P.wait("gpsimd", s_mm, P.val(s_mm))
                    b2lv = P.dma("gpsimd", B2(o_b2slot, 512, parts=1),
                                 ap["b2"][loads[i][0]:loads[i][0] + 1,
                                          512 * chalf:512 * (chalf + 1)], s_w)
                    P.wait("tensor", s_w, max(lv_load[i], b2lv))
                    P.wait("tensor", s_dv, cv_load[i])
                    woff = slot_off[i % 2]
                    for ss in range(3):
                        sn = 128 if ss < 2 else 64
                        ps = PS2 if (ss % 2 == 0) else PS3
                        P.wait("tensor", s_dv, max(0, P.val(s_dv) - 1))
                        for hh2 in range(32):
                            P.op("tensor", lambda e, ps=ps, woff=woff, hh2=hh2, ss=ss,
                                 sn=sn:
                                 e.matmul(ps[0:sn, 0:512],
                                          h3[:, hh2, ss * 128:ss * 128 + sn],
                                          B3(woff, 32, 512)[:, hh2, :],
                                          start=(hh2 == 0), stop=False))
                        P.op("tensor", lambda e, ps=ps, sn=sn:
                             e.matmul(ps[0:sn, 0:512],
                                      BFh[0:1, oc_ones1:oc_ones1 + sn],
                                      BFh[0:1, o_b2slot:o_b2slot + 512],
                                      start=False, stop=True), s_mm)
                        P.wait("vector", s_mm, P.val(s_mm))
                        P.op("vector", lambda e, ps=ps, sn=sn:
                             e.tensor_copy(out=B2(o_yst, 512, parts=sn),
                                           in_=ps[0:sn, 0:512]), s_dv)
                        ydv = P.val(s_dv)
                        P.wait("gpsimd", s_dv, ydv)
                        P.dma("gpsimd",
                              ap["y_hbm"][e_i * CAP + ss * 128:
                                          e_i * CAP + ss * 128 + sn,
                                          512 * chalf:512 * (chalf + 1)],
                              B2(o_yst, 512, parts=sn), s_io)
                        P.wait("vector", s_io, P.val(s_io))
                    slot_free_mm[i % 2] = P.val(s_mm)
            y_io = P.val(s_io)
            if "h0" in dbg:
                hv = P.val(s_ac)
                P.wait("gpsimd", s_ac, hv)
                P.dma("gpsimd", dbg_out["h0"], B2(o_hfm, 32 * CAP), s_io)

        if phases >= 11:
            out_t = ap["out"].rearrange("(t p) d -> t p d", p=128)
            P.wait("gpsimd", s_io, y_io)
            if phases >= 10:
                P.wait("sync", s_dv, max(conv_free))
            for t in range(NT):
                lvx = P.dma("sync", F2(of_xtile, D), x1_tiled[t], s_ld)
                P.igather(B2(o_g0, D), ap["y_hbm"],
                          U3h[0:128, 2 * t:2 * t + 1], s_g)
                g1v = P.igather(B2(o_g1, D), ap["y_hbm"],
                                U3h[0:128, 2 * t + 1:2 * t + 2], s_g)
                P.wait("vector", s_ld, lvx)
                P.wait("vector", s_g, g1v)
                g0_ap = F2(of_Garr + 2 * t, 1)
                g1_ap = F2(of_Garr + 2 * t + 1, 1)
                P.op("vector", lambda e, g0_ap=g0_ap: e.scalar_tensor_tensor(
                    out=F2(of_xtile, D), in0=B2(o_g0, D), scalar=g0_ap,
                    in1=F2(of_xtile, D), op0=AL.mult, op1=AL.add))
                P.op("vector", lambda e, g1_ap=g1_ap: e.scalar_tensor_tensor(
                    out=F2(of_xtile, D), in0=B2(o_g1, D), scalar=g1_ap,
                    in1=F2(of_xtile, D), op0=AL.mult, op1=AL.add), s_dv)
                odv = P.val(s_dv)
                P.wait("gpsimd", s_dv, odv)
                P.dma("gpsimd", out_t[t], F2(of_xtile, D), s_st)
                P.wait("sync", s_dv, odv)
                P.wait("sync", s_st, P.val(s_st))
                P.wait("gpsimd", s_g, P.val(s_g))
            P.wait("gpsimd", s_st, P.val(s_st))
            P.wait("gpsimd", s_io, P.val(s_io))
            P.wait("sync", s_st, P.val(s_st))

        for sem in (s_st, s_io, s_g, s_sh, s_w, s_ld):
            P.wait("gpsimd", sem, P.val(sem))
        P.wait("gpsimd", s_dv, P.val(s_dv))
        P.wait("gpsimd", s_ac, P.val(s_ac))
        P.wait("gpsimd", s_mm, P.val(s_mm))
        P.wait("gpsimd", s_ex, P.val(s_ex))

        @block.sync
        def _(e):
            for fn in P.steps["sync"]:
                fn(e)

        @block.gpsimd
        def _(e):
            for fn in P.steps["gpsimd"]:
                fn(e)

        @block.vector
        def _(e):
            for fn in P.steps["vector"]:
                fn(e)

        @block.scalar
        def _(e):
            for fn in P.steps["scalar"]:
                fn(e)

        @block.tensor
        def _(e):
            for fn in P.steps["tensor"]:
                fn(e)

    return nc


_consts_cache = None


def kernel(**inputs):
    global _consts_cache
    if _consts_cache is None:
        _consts_cache = _consts_host()
    consts = _consts_cache
    shared = _prep_inputs(inputs)
    x = np.asarray(inputs["L0_attn_x"], np.float32)
    B = x.shape[0]
    nc = build_program()
    in_maps = []
    for b in range(B):
        m = {"x": np.ascontiguousarray(x[b])}
        m.update(shared)
        m.update(consts)
        in_maps.append(m)
    res = run_bass_kernel_spmd(nc, in_maps, core_ids=list(range(B)))
    out = np.stack([res.results[b]["out"] for b in range(B)], axis=0)
    return out


# version 05
# speedup vs baseline: 1.3865x; 1.0589x over previous
import math
import numpy as np
import ml_dtypes

import concourse.bass as bass
import concourse.mybir as mybir
from concourse.bass_core_rust import IndirectOffsetOnAxis
from concourse.bass_utils import run_bass_kernel_spmd

F32 = mybir.dt.float32
BF16 = mybir.dt.bfloat16
U32 = mybir.dt.uint32
AF = mybir.ActivationFunctionType
AL = mybir.AluOpType
AX = mybir.AxisListType

bfloat16 = ml_dtypes.bfloat16

S = 2048
D = 1024
H = 16
DH = 64
NT = S // 128
NC = D // 128
E = 8
CAP = 320
NSLOT = E * CAP
NB = NSLOT // 128
HID = 4096
SM_SCALE = float(1.0 / math.sqrt(2 * DH))

PI = float(np.pi)
TWO_PI = float(2 * np.pi)
INV_2PI = float(1.0 / (2 * np.pi))
MAGIC = float(1.5 * 2.0 ** 23)
CW1 = float(np.float32(6.28125))
CW2 = float(np.float32(2.0 * np.pi - 6.28125))
NEG_BIG = -1.0e30

ENGINES = ("sync", "gpsimd", "vector", "scalar", "tensor")

BF_TOTAL = 76800
oc_ident = 0
oc_lt128 = 128
oc_onescol = 256
oc_rowsel = 512
oc_lt16 = 2560
oc_ones1 = 2576
oc_apbf = 2704
oc_ohs = 2720
oc_oh2 = 2728
oc_wr = 2736
oc_aplo = 2800
oR1 = 2816
o_ln1tm = oR1
o_attn = oR1
o_hfm = oR1
o_yst = oR1 + 10240
o_g0 = oR1 + 10752
o_g1 = oR1 + 11776
o_b2slot = oR1 + 12800
oR2 = 19200
o_ln1fm = oR2
o_ln2fm = oR2
o_wA = oR2
oR3 = 35584
o_vtm = oR3
o_ln2tm = oR3
o_buffm = oR3
oR4 = 52224
o_qp = oR4
o_kp = oR4 + 4096
o_sC = oR4 + 8192
o_sS = oR4 + 10240
o_psb = oR4 + 12288
o_wB = 56064

FP_TOTAL = 13312
of_identf = 0
of_trimask = 128
of_trow = 256
of_erange = 2304
of_im9 = 2312
of_tokid = 2320
of_iotap = 2336
of_eps = 2337
of_hpi = 2338
of_one = 2339
of_pp = 2340
of_stats = 2388
of_routT = 2396
of_b1l = 2492
of_br = 2748
of_rW = 2756
of_rB = 3780
of_expall = of_rW + 0
of_recall = of_rW + 128
of_sc2 = of_rW + 144
of_e1m9 = of_rW + 146
of_e2m9 = of_rW + 147
of_Earr = of_rW + 160
of_Parr = of_rW + 224
of_Garr = of_rW + 288
of_posall = of_rW + 352
of_prod = of_rW + 416
of_nk = of_rW + 432
of_kp = of_rW + 496
of_slc = of_rW + 560
of_sca = of_rW + 624
of_scb = of_rW + 688
of_tr64 = of_rW + 752
of_nine = of_rW + 816
of_r8a = of_rW + 824
of_r8b = of_rW + 832
oT = 4808
of_thq, of_k2, of_u2, of_w2 = oT, oT + 2048, oT + 4096, oT + 6144
of_th, of_k1, of_u1, of_w1 = oT, oT + 1024, oT + 2048, oT + 3072
of_sv, of_cv = oT + 4096, oT + 5120
of_xtile, of_sq = oT + 6144, oT + 7168
of_scsrc = oT
of_tokf = 13000
of_prob = 13020
of_srt = 13028
of_oh1 = 13036
of_oh2 = 13044
of_rt1 = 13052
of_rt2 = 13060
assert 13068 <= FP_TOTAL


class Prog:
    def __init__(self):
        self.steps = {e: [] for e in ENGINES}
        self.sem_val = {}

    def emit(self, eng, fn):
        self.steps[eng].append(fn)

    def inc(self, sem, n):
        k = sem.num
        self.sem_val[k] = self.sem_val.get(k, 0) + n
        return self.sem_val[k]

    def val(self, sem):
        return self.sem_val.get(sem.num, 0)

    def dma(self, q, out, in_, sem, **kw):
        v = self.inc(sem, 16)
        self.emit(q, lambda e, out=out, in_=in_, sem=sem, kw=kw:
                  e.dma_start(out=out, in_=in_, **kw).then_inc(sem, 16))
        return v

    def igather(self, out, in_, idx_col, sem):
        v = self.inc(sem, 16)
        self.emit("gpsimd", lambda e, out=out, in_=in_, idx_col=idx_col, sem=sem:
                  e.indirect_dma_start(out=out, out_offset=None, in_=in_,
                                       in_offset=IndirectOffsetOnAxis(idx_col, 0))
                  .then_inc(sem, 16))
        return v

    def iscatter(self, out, idx_col, in_, sem):
        v = self.inc(sem, 16)
        self.emit("gpsimd", lambda e, out=out, in_=in_, idx_col=idx_col, sem=sem:
                  e.indirect_dma_start(out=out,
                                       out_offset=IndirectOffsetOnAxis(idx_col, 0),
                                       in_=in_, in_offset=None)
                  .then_inc(sem, 16))
        return v

    def wait(self, eng, sem, v):
        if v > 0:
            self.emit(eng, lambda e, sem=sem, v=v: e.wait_ge(sem, v))

    def op(self, eng, fn, sem=None, n=1):
        if sem is None:
            self.emit(eng, fn)
            return None
        v = self.inc(sem, n)
        self.emit(eng, lambda e, fn=fn, sem=sem, n=n: fn(e).then_inc(sem, n))
        return v


def _consts_host():
    c = {}
    c["c_ident_bf"] = np.eye(128, dtype=bfloat16)
    c["c_ident_f32"] = np.eye(128, dtype=np.float32)
    c["c_lt128"] = np.triu(np.ones((128, 128), np.float32), k=1).astype(bfloat16)
    c["c_lt16"] = np.triu(np.ones((16, 16), np.float32), k=1).astype(bfloat16)
    m = np.zeros((128, 128), np.float32)
    m[np.arange(128)[:, None] > np.arange(128)[None, :]] = NEG_BIG
    c["c_trimask"] = m
    c["c_trow"] = np.broadcast_to(np.arange(S, dtype=np.float32), (128, S)).copy()
    oc = np.zeros((128, 16 * 16), np.float32)
    for t in range(16):
        oc[:, 16 * t + t] = 1.0
    c["c_onescol"] = oc.astype(bfloat16)
    rs = np.zeros((16, 16 * 128), np.float32)
    for t in range(16):
        rs[t, 128 * t:128 * (t + 1)] = 1.0
    c["c_rowsel"] = rs.astype(bfloat16)
    c["c_erange"] = np.broadcast_to(np.arange(E, dtype=np.float32), (128, E)).copy()
    c["c_im9"] = np.broadcast_to(np.arange(E, dtype=np.float32) - 9.0, (128, E)).copy()
    c["c_tokid"] = (np.arange(NT)[None, :] * 128
                    + np.arange(128)[:, None]).astype(np.float32).copy()
    c["c_iotap"] = np.arange(128, dtype=np.float32).reshape(128, 1).copy()
    c["c_tr64"] = np.broadcast_to(
        (np.arange(128, dtype=np.float32) + NSLOT).reshape(128, 1), (128, 64)).copy()
    c["c_ones1"] = np.ones((1, 128), dtype=bfloat16)
    return c


def _prep_inputs(ins):
    f32 = np.float32
    d = {}

    def pair_layout(w):
        out = np.zeros((128, 8), f32)
        for hp in range(8):
            out[0:64, hp] = w[2 * hp]
            out[64:128, hp] = w[2 * hp + 1]
        return out

    for nm in ("w_q", "b_q", "phi_q", "w_k", "b_k", "phi_k"):
        d["p_" + nm] = pair_layout(np.asarray(ins["L0_attn_" + nm], f32))
    for nm in ("w_v", "b_v"):
        d["r_" + nm] = np.broadcast_to(
            np.asarray(ins["L0_attn_" + nm], f32).reshape(-1), (128, D)).copy()
    for nm in ("w_out", "b_out"):
        d["r_" + nm] = np.broadcast_to(
            np.asarray(ins["L0_attn_" + nm], f32), (128, D)).copy()
    d["r_br"] = np.broadcast_to(np.asarray(ins["L0_ffn_br"], f32), (128, E)).copy()
    d["w_r"] = np.ascontiguousarray(np.asarray(ins["L0_ffn_Wr"], f32))
    d["w1"] = np.ascontiguousarray(np.asarray(ins["L0_ffn_W1"], f32))
    d["w2"] = np.ascontiguousarray(np.asarray(ins["L0_ffn_W2"], f32))
    b1 = np.asarray(ins["L0_ffn_b1"], f32)
    b1l = np.zeros((128, E * 32), f32)
    for e in range(E):
        for ht in range(32):
            b1l[:, e * 32 + ht] = b1[e, 128 * ht:128 * (ht + 1)]
    d["b1l"] = b1l
    d["b2"] = np.ascontiguousarray(np.asarray(ins["L0_ffn_b2"], f32))
    return d


def build_program(dbg=(), phases=11):
    nc = bass.Bass("TRN2", target_bir_lowering=False)
    ap = {}

    def IN(name, shape, dt=F32):
        ap[name] = nc.dram_tensor(name, list(shape), dt, kind="ExternalInput").ap()

    def OUT(name, shape, dt=F32):
        ap[name] = nc.dram_tensor(name, list(shape), dt, kind="ExternalOutput").ap()

    def INT(name, shape, dt=F32):
        ap[name] = nc.dram_tensor(name, list(shape), dt, kind="Internal").ap()

    IN("x", (S, D))
    for nm in ("p_w_q", "p_b_q", "p_phi_q", "p_w_k", "p_b_k", "p_phi_k"):
        IN(nm, (128, 8))
    for nm in ("r_w_v", "r_b_v", "r_w_out", "r_b_out"):
        IN(nm, (128, D))
    IN("r_br", (128, E))
    IN("w_r", (D, E))
    IN("w1", (E, D, HID))
    IN("w2", (E, HID, D))
    IN("b1l", (128, E * 32))
    IN("b2", (E, D))
    IN("c_ident_bf", (128, 128), BF16)
    IN("c_ident_f32", (128, 128))
    IN("c_lt128", (128, 128), BF16)
    IN("c_lt16", (16, 16), BF16)
    IN("c_trimask", (128, 128))
    IN("c_trow", (128, S))
    IN("c_onescol", (128, 256), BF16)
    IN("c_rowsel", (16, 2048), BF16)
    IN("c_erange", (128, E))
    IN("c_im9", (128, E))
    IN("c_tokid", (128, NT))
    IN("c_iotap", (128, 1))
    IN("c_tr64", (128, 64))
    IN("c_ones1", (1, 128), BF16)

    INT("w1s", (E, D, HID), BF16)
    INT("w2s", (E, HID, D), BF16)
    INT("ln2_hbm", (S, D), BF16)
    INT("x1_hbm", (S, D))
    INT("slotbuf", (NSLOT + 128, 64))
    INT("y_hbm", (NSLOT, D), BF16)
    OUT("out", (S, D))

    dbg_out = {}
    dbg_shapes = {
        "ln1": ((128, NT * D), BF16), "v": ((128, NT * H * 65), BF16),
        "q0": ((128, 2 * S), BF16), "k0": ((128, 2 * S), BF16),
        "attn": ((128, NT * D), BF16), "ln2": ((128, NT * D), BF16),
        "route": ((128, 256), F32), "u3": ((128, 64), U32), "scs": ((128, NT * 64), F32), "fpd": ((128, 128), F32), "x1": ((128, NT * D), F32), "thout": ((128, NT * D), F32),
        "tok128": ((128, NB), U32), "buf": ((128, 8 * NSLOT), BF16),
        "h0": ((128, 32 * CAP), BF16),
    }
    for nm in dbg:
        sh, dt = dbg_shapes[nm]
        OUT("dbg_" + nm, sh, dt)
        dbg_out[nm] = ap["dbg_" + nm]

    P = Prog()

    from contextlib import ExitStack
    with ExitStack() as _stack:
        BFh = _stack.enter_context(nc.sbuf_tensor("BF", [128, BF_TOTAL], BF16))
        FPh = _stack.enter_context(nc.sbuf_tensor("FP", [128, FP_TOTAL], F32))
        SMh = _stack.enter_context(nc.sbuf_tensor("SM", [128, 1024], F32))
        U3h = _stack.enter_context(nc.sbuf_tensor("U3", [128, 64], U32))
        PS0 = _stack.enter_context(nc.psum_tensor("PS0", [128, 512], F32))
        PS1 = _stack.enter_context(nc.psum_tensor("PS1", [128, 512], F32))
        PS2 = _stack.enter_context(nc.psum_tensor("PS2", [128, 512], F32))
        PS3 = _stack.enter_context(nc.psum_tensor("PS3", [128, 512], F32))
        PT_B0 = _stack.enter_context(nc.psum_tensor("PT_B0", [128, 128], BF16))
        PT_B1 = _stack.enter_context(nc.psum_tensor("PT_B1", [128, 128], BF16))
        PT_F = _stack.enter_context(nc.psum_tensor("PT_F", [128, 128], F32))
        PS4 = _stack.enter_context(nc.psum_tensor("PS4", [128, 512], F32))
        s_ld = _stack.enter_context(nc.semaphore("s_ld"))
        s_w = _stack.enter_context(nc.semaphore("s_w"))
        s_w2 = _stack.enter_context(nc.semaphore("s_w2"))
        s_dv = _stack.enter_context(nc.semaphore("s_dv"))
        s_ac = _stack.enter_context(nc.semaphore("s_ac"))
        s_mm = _stack.enter_context(nc.semaphore("s_mm"))
        s_ex = _stack.enter_context(nc.semaphore("s_ex"))
        s_sh = _stack.enter_context(nc.semaphore("s_sh"))
        s_io = _stack.enter_context(nc.semaphore("s_io"))
        s_g = _stack.enter_context(nc.semaphore("s_g"))
        s_st = _stack.enter_context(nc.semaphore("s_st"))
        block = _stack.enter_context(nc.Block())
        def B2(off, n, parts=128, p0=0):
            return BFh[p0:p0 + parts, off:off + n]

        def B3(off, a, b):
            return BFh[0:128, off:off + a * b].rearrange("p (a b) -> p a b", a=a)

        def F2(off, n, parts=128, p0=0):
            return FPh[p0:p0 + parts, off:off + n]

        def spacer():
            P.op("vector", lambda e: e.tensor_copy(out=SMh[:, 768:1024],
                                                   in_=SMh[:, 512:768]))

        ident_bf = B2(oc_ident, 128)
        ident_f = F2(of_identf, 128)
        trimask = F2(of_trimask, 128)
        eps_ap = F2(of_eps, 1)
        halfpi_ap = F2(of_hpi, 1)
        erange = F2(of_erange, E)
        im9 = F2(of_im9, E)

        for nm, (off, n, parts, isbf) in {
            "c_ident_bf": (oc_ident, 128, 128, True),
            "c_lt128": (oc_lt128, 128, 128, True),
            "c_onescol": (oc_onescol, 256, 128, True),
            "c_rowsel": (oc_rowsel, 2048, 16, True),
            "c_lt16": (oc_lt16, 16, 16, True),
            "c_ones1": (oc_ones1, 128, 1, True),
            "c_ident_f32": (of_identf, 128, 128, False),
            "c_trimask": (of_trimask, 128, 128, False),
            "c_trow": (of_trow, S, 128, False),
            "c_erange": (of_erange, E, 128, False),
            "c_im9": (of_im9, E, 128, False),
            "c_tokid": (of_tokid, NT, 128, False),
            "c_iotap": (of_iotap, 1, 128, False),
            "b1l": (of_b1l, E * 32, 128, False),
            "r_br": (of_br, E, 128, False),
        }.items():
            dst = B2(off, n, parts=parts) if isbf else F2(off, n, parts=parts)
            P.dma("sync", dst, ap[nm], s_ld)
        P.dma("gpsimd", B3(oc_wr, NC, E),
              ap["w_r"].rearrange("(c p) e -> p c e", p=128), s_ld)
        P.op("vector", lambda e: e.memset(eps_ap, 1e-5))
        P.op("vector", lambda e: e.memset(F2(of_one, 1), 1.0))
        P.op("vector", lambda e: e.memset(halfpi_ap, PI / 2), s_dv)

        names = ("p_w_q", "p_b_q", "p_phi_q", "p_w_k", "p_b_k", "p_phi_k")
        for i, nm in enumerate(names):
            dst = F2(of_pp + 8 * i, 8)
            lv = P.dma("sync", dst, ap[nm], s_ld)
            if i in (0, 3):
                P.wait("scalar", s_ld, lv)
                P.wait("scalar", s_dv, P.val(s_dv))
                P.op("scalar", lambda e, dst=dst: e.activation(
                    out=dst, in_=dst, func=AF.Abs))
                P.op("scalar", lambda e, dst=dst: e.activation(
                    out=dst, in_=dst, func=AF.Ln, bias=F2(of_one, 1)))
                P.op("scalar", lambda e, dst=dst: e.activation(
                    out=dst, in_=dst, func=AF.Exp, scale=-1.0), s_ac)
        ld_consts = P.val(s_ld)
        ac_pp = P.val(s_ac)

        def emit_ln(src_ap, dst_bf_ap):
            s_sum = F2(of_stats, 1); s_ss = F2(of_stats + 1, 1)
            s_msq = F2(of_stats + 2, 1); s_ssD = F2(of_stats + 3, 1)
            s_var = F2(of_stats + 4, 1); s_rstd = F2(of_stats + 5, 1)
            s_negm = F2(of_stats + 6, 1); s_mb = F2(of_stats + 7, 1)
            sq = F2(of_sq, D)
            P.wait("scalar", s_dv, P.val(s_dv))
            P.op("vector", lambda e: e.reduce_sum(out=s_sum, in_=src_ap, axis=AX.X),
                 s_dv)
            P.op("scalar", lambda e: e.activation(out=sq, in_=src_ap, func=AF.Square,
                                                  accum_out=s_ss), s_ac)
            P.wait("scalar", s_dv, P.val(s_dv))
            P.op("scalar", lambda e: e.activation(out=s_msq, in_=s_sum,
                                                  func=AF.Square, scale=1.0 / D))
            P.op("scalar", lambda e: e.activation(out=s_ssD, in_=s_ss, func=AF.Copy,
                                                  scale=1.0 / D))
            P.op("scalar", lambda e: e.activation(out=s_negm, in_=s_sum, func=AF.Copy,
                                                  scale=-1.0 / D), s_ac)
            P.wait("vector", s_ac, P.val(s_ac))
            P.op("vector", lambda e: e.tensor_sub(out=s_var, in0=s_ssD, in1=s_msq),
                 s_dv)
            P.wait("scalar", s_dv, P.val(s_dv))
            P.op("scalar", lambda e: e.activation(out=s_rstd, in_=s_var, func=AF.Ln,
                                                  bias=eps_ap))
            P.op("scalar", lambda e: e.activation(out=s_rstd, in_=s_rstd, func=AF.Exp,
                                                  scale=-0.5), s_ac)
            P.wait("vector", s_ac, P.val(s_ac))
            P.op("vector", lambda e: e.tensor_scalar(out=s_mb, in0=s_negm,
                                                     scalar1=s_rstd, scalar2=None,
                                                     op0=AL.mult), s_dv)
            spacer()
            P.op("vector", lambda e: e.tensor_scalar(out=dst_bf_ap, in0=src_ap,
                                                     scalar1=s_rstd, scalar2=s_mb,
                                                     op0=AL.mult, op1=AL.add), s_dv)

        def emit_trig(tho, n, ko, uo, wo, sin_out, cos_out, add_out=None,
                      add_srcs=None):
            th = F2(tho, n)
            k = F2(ko, n); u = F2(uo, n); w = F2(wo, n)
            P.op("vector", lambda e: e.tensor_scalar(out=k, in0=th, scalar1=INV_2PI,
                                                     scalar2=MAGIC, op0=AL.mult,
                                                     op1=AL.add))
            P.op("vector", lambda e: e.tensor_scalar(out=k, in0=k, scalar1=MAGIC,
                                                     scalar2=None, op0=AL.subtract))
            P.op("vector", lambda e: e.scalar_tensor_tensor(
                out=u, in0=k, scalar=-CW1, in1=th, op0=AL.mult, op1=AL.add))
            P.op("vector", lambda e: e.scalar_tensor_tensor(
                out=u, in0=k, scalar=-CW2, in1=u, op0=AL.mult, op1=AL.add))
            P.op("vector", lambda e: e.tensor_scalar(out=w, in0=u, scalar1=PI / 2,
                                                     scalar2=None, op0=AL.is_gt))
            P.op("vector", lambda e: e.scalar_tensor_tensor(
                out=w, in0=w, scalar=-TWO_PI, in1=u, op0=AL.mult, op1=AL.add))
            P.op("vector", lambda e: e.tensor_scalar(out=w, in0=w, scalar1=PI / 2,
                                                     scalar2=-1.5 * PI, op0=AL.min,
                                                     op1=AL.max))
            P.op("vector", lambda e: e.tensor_scalar(out=u, in0=u, scalar1=PI,
                                                     scalar2=-PI, op0=AL.min,
                                                     op1=AL.max), s_dv)
            P.wait("scalar", s_dv, P.val(s_dv))
            P.op("scalar", lambda e: e.activation(out=sin_out, in_=u, func=AF.Sin))
            P.op("scalar", lambda e: e.activation(out=cos_out, in_=w, func=AF.Sin,
                                                  bias=halfpi_ap), s_ac)
            if add_out is not None:
                a0, a1 = add_srcs if add_srcs is not None else (sin_out, cos_out)
                P.wait("vector", s_ac, P.val(s_ac))
                P.op("vector", lambda e: e.tensor_add(out=add_out, in0=a0,
                                                      in1=a1), s_dv)

        x_tiled = ap["x"].rearrange("(t p) d -> t p d", p=128)
        P.wait("vector", s_ld, ld_consts)
        P.wait("scalar", s_ld, ld_consts)
        for t in range(NT):
            lv = P.dma("sync", F2(of_xtile, D), x_tiled[t], s_ld)
            P.wait("vector", s_ld, lv)
            P.wait("scalar", s_ld, lv)
            emit_ln(F2(of_xtile, D), B2(o_ln1tm + t * D, D))
            P.wait("sync", s_dv, P.val(s_dv))
            P.wait("sync", s_ac, P.val(s_ac))
        ln1_dv = P.val(s_dv)
        if "ln1" in dbg:
            P.wait("gpsimd", s_dv, ln1_dv)
            P.dma("gpsimd", dbg_out["ln1"], B2(o_ln1tm, NT * D), s_io)

        if phases >= 2:
            P.wait("tensor", s_ld, ld_consts)
            P.wait("tensor", s_dv, ln1_dv)
            tcnt = 0
            for t in range(NT):
                for c in range(NC):
                    src = B2(o_ln1tm + t * D + 128 * c, 128)
                    pt = PT_B0[:, 0:128] if (tcnt % 2 == 0) else PT_B1[:, 0:128]
                    P.op("tensor", lambda e, pt=pt, src=src:
                         e.transpose(pt, src, ident_bf), s_mm)
                    P.wait("vector", s_mm, P.val(s_mm))
                    dst = B2(o_ln1fm + c * S + 128 * t, 128)
                    P.op("vector", lambda e, dst=dst, pt=pt:
                         e.tensor_copy(out=dst, in_=pt), s_dv)
                    if tcnt >= 1:
                        P.wait("tensor", s_dv, P.val(s_dv) - 1)
                    tcnt += 1
            ln1fm_dv = P.val(s_dv)

        if phases >= 3:
            lv = P.dma("sync", F2(of_rW, D), ap["r_w_v"], s_ld)
            lv = P.dma("sync", F2(of_rB, D), ap["r_b_v"], s_ld)
            P.wait("vector", s_ld, lv)
            P.wait("scalar", s_ld, P.val(s_ld))
            P.op("scalar", lambda e: e.activation(out=F2(of_rW, D), in_=F2(of_rW, D),
                                                  func=AF.Abs))
            P.op("scalar", lambda e: e.activation(out=F2(of_rW, D), in_=F2(of_rW, D),
                                                  func=AF.Ln, bias=F2(of_one, 1)))
            P.op("scalar", lambda e: e.activation(out=F2(of_rW, D), in_=F2(of_rW, D),
                                                  func=AF.Exp, scale=-1.0), s_ac)
            vtm4 = BFh[0:128, o_vtm:o_vtm + NT * H * 65].rearrange(
                "p (t h c) -> p t h c", t=NT, h=H)
            P.op("vector", lambda e: e.memset(vtm4[:, :, :, 64:65], 1.0), s_dv)
            P.wait("vector", s_ac, P.val(s_ac))
            for t in range(NT):
                th = F2(of_th, D)
                src = B2(o_ln1tm + t * D, D)
                P.op("vector", lambda e, th=th, src=src: e.tensor_tensor(
                    out=th, in0=src, in1=F2(of_rW, D), op=AL.mult))
                P.op("vector", lambda e, th=th: e.tensor_tensor(
                    out=th, in0=th, in1=F2(of_rB, D), op=AL.add))
                vts = vtm4[:, t, :, 0:64]
                sv3 = F2(of_sv, D).rearrange("p (h c) -> p h c", h=H)
                cv3 = F2(of_cv, D).rearrange("p (h c) -> p h c", h=H)
                emit_trig(of_th, D, of_k1, of_u1, of_w1,
                          F2(of_sv, D), F2(of_cv, D), add_out=vts,
                          add_srcs=(sv3, cv3))
            v_dv = P.val(s_dv)
            if "v" in dbg:
                P.wait("gpsimd", s_dv, v_dv)
                P.dma("gpsimd", dbg_out["v"], B2(o_vtm, NT * H * 65), s_io)

        if phases >= 4:
            av_free_dv = [0, 0]
            w1c_s = ap["w1"].rearrange("e (c p) h -> e p c h", p=128)
            w1c_d = ap["w1s"].rearrange("e (c p) h -> e p c h", p=128)
            w2c_s = ap["w2"].rearrange("e (hc p) d -> e p hc d", p=128)
            w2c_d = ap["w2s"].rearrange("e (hc p) d -> e p hc d", p=128)
            cast_val = [0] * E
            for hp in range(8):
                mm_prev = P.val(s_mm)
                for (dst, base) in ((o_qp, 0), (o_kp, 24)):
                    inv_ap = F2(of_pp + base + 0 + hp, 1)
                    b_ap = F2(of_pp + base + 8 + hp, 1)
                    phi_ap = F2(of_pp + base + 16 + hp, 1)
                    th = F2(of_thq, S)
                    src = B2(o_ln1fm + hp * S, S)
                    P.wait("vector", s_ac, ac_pp)
                    P.wait("vector", s_dv, max(ln1fm_dv, v_dv))
                    P.wait("vector", s_mm, mm_prev)
                    P.op("vector", lambda e, th=th, src=src, inv_ap=inv_ap, b_ap=b_ap:
                         e.tensor_scalar(out=th, in0=src, scalar1=inv_ap, scalar2=b_ap,
                                         op0=AL.mult, op1=AL.add))
                    P.op("vector", lambda e, th=th, phi_ap=phi_ap:
                         e.scalar_tensor_tensor(out=th, in0=F2(of_trow, S),
                                                scalar=phi_ap, in1=th,
                                                op0=AL.mult, op1=AL.add))
                    emit_trig(of_thq, S, of_k2, of_u2, of_w2,
                              B2(o_sS, S), B2(o_sC, S))
                    P.wait("vector", s_ac, P.val(s_ac))
                    P.op("vector", lambda e, dst=dst: e.tensor_copy(
                        out=BFh[0:64, dst:dst + S], in_=BFh[0:64, o_sC:o_sC + S]))
                    P.op("vector", lambda e, dst=dst: e.tensor_copy(
                        out=BFh[64:128, dst + S:dst + 2 * S],
                        in_=BFh[64:128, o_sS:o_sS + S]), s_dv)
                    P.wait("gpsimd", s_ac, P.val(s_ac))
                    P.wait("gpsimd", s_mm, mm_prev)
                    P.dma("gpsimd", BFh[64:128, dst:dst + S],
                          BFh[0:64, o_sS:o_sS + S], s_sh)
                    P.dma("gpsimd", BFh[0:64, dst + S:dst + 2 * S],
                          BFh[64:128, o_sC:o_sC + S], s_sh)
                    P.wait("vector", s_sh, P.val(s_sh))
                    P.wait("scalar", s_sh, P.val(s_sh))
                if phases >= 10:
                    for c in range(8):
                        P.dma("gpsimd", w1c_d[hp][:, c, :], w1c_s[hp][:, c, :],
                              s_w)
                    for g4 in range(8):
                        cast_val[hp] = P.dma(
                            "gpsimd", w2c_d[hp][:, 4 * g4:4 * g4 + 4, :],
                            w2c_s[hp][:, 4 * g4:4 * g4 + 4, :], s_w)
                qk_dv = P.val(s_dv)
                qk_sh = P.val(s_sh)
                P.wait("tensor", s_dv, qk_dv)
                P.wait("tensor", s_sh, qk_sh)
                for hh in range(2):
                    h = 2 * hp + hh
                    qoff = o_qp + hh * S
                    koff = o_kp + hh * S
                    SCB = (PS0, PS1, PS4)
                    for qt in range(4):
                        av_ps = PS2 if (qt % 2 == 0) else PS3
                        nkt = 4 * qt + 4
                        scv = [0] * nkt
                        exv = [0] * nkt
                        avv = [0] * nkt

                        def emit_sc(kt):
                            dd = kt - 4 * qt
                            qlo_rel = 128 * dd if dd >= 0 else 0
                            qlo = qt * 512 + qlo_rel
                            qn = 512 - qlo_rel
                            sc_ps = SCB[kt % 3]
                            if kt >= 3:
                                P.wait("tensor", s_ex, exv[kt - 3])
                            P.op("tensor", lambda e, sc_ps=sc_ps, koff=koff, kt=kt,
                                 qoff=qoff, qlo=qlo, qn=qn:
                                 e.matmul(sc_ps[:, 0:qn], B2(koff + kt * 128, 128),
                                          B2(qoff + qlo, qn), start=True, stop=True),
                                 s_mm)
                            scv[kt] = P.val(s_mm)

                        def emit_exp(kt):
                            dd = kt - 4 * qt
                            qlo_rel = 128 * dd if dd >= 0 else 0
                            qn = 512 - qlo_rel
                            sc_ps = SCB[kt % 3]
                            if dd >= 0:
                                P.wait("vector", s_mm, scv[kt])
                                P.op("vector", lambda e, sc_ps=sc_ps:
                                     e.tensor_tensor(out=sc_ps[:, 0:128],
                                                     in0=sc_ps[:, 0:128],
                                                     in1=trimask, op=AL.add), s_dv)
                                P.wait("scalar", s_dv, P.val(s_dv))
                            else:
                                P.wait("scalar", s_mm, scv[kt])
                            if kt >= 4:
                                P.wait("scalar", s_mm, avv[kt - 4])
                            psb = o_psb + (kt % 4) * 512
                            P.op("scalar", lambda e, psb=psb, sc_ps=sc_ps, qn=qn:
                                 e.activation(out=B2(psb, qn), in_=sc_ps[:, 0:qn],
                                              func=AF.Exp, scale=SM_SCALE), s_ex)
                            exv[kt] = P.val(s_ex)

                        def emit_av(kt):
                            dd = kt - 4 * qt
                            qlo_rel = 128 * dd if dd >= 0 else 0
                            qn = 512 - qlo_rel
                            psb = o_psb + (kt % 4) * 512
                            P.wait("tensor", s_ex, exv[kt])
                            if kt == 0 and av_free_dv[qt % 2]:
                                P.wait("tensor", s_dv, av_free_dv[qt % 2])
                            vsl = vtm4[:, kt, h, :]
                            P.op("tensor", lambda e, av_ps=av_ps, vsl=vsl, psb=psb,
                                 qlo_rel=qlo_rel, qn=qn, first=(kt == 0),
                                 last=(kt == nkt - 1):
                                 e.matmul(av_ps[0:65, qlo_rel:qlo_rel + qn], vsl,
                                          B2(psb, qn), start=first, stop=last), s_mm)
                            avv[kt] = P.val(s_mm)

                        for kt in range(nkt):
                            emit_sc(kt)
                            emit_exp(kt)
                            if kt >= 2:
                                emit_av(kt - 2)
                        emit_av(nkt - 2)
                        emit_av(nkt - 1)

                        stg = SMh[0:65, 0:512]
                        P.wait("vector", s_mm, avv[nkt - 1])
                        P.op("vector", lambda e, av_ps=av_ps, stg=stg:
                             e.tensor_copy(out=stg, in_=av_ps[0:65, 0:512]), s_dv)
                        av_free_dv[qt % 2] = P.val(s_dv)
                        oS = of_sq
                        cp_dv = []
                        for sub in range(4):
                            P.wait("tensor", s_dv,
                                   cp_dv[sub - 1] if sub >= 1
                                   else av_free_dv[qt % 2])
                            P.op("tensor", lambda e, stg=stg, sub=sub:
                                 e.transpose(PT_F[:, 0:65],
                                             stg[0:65, 128 * sub:128 * (sub + 1)],
                                             ident_f[0:65, 0:65]), s_mm)
                            P.wait("vector", s_mm, P.val(s_mm))
                            P.op("vector", lambda e, sub=sub, oS=oS: e.tensor_copy(
                                 out=F2(oS + 65 * sub, 65), in_=PT_F[:, 0:65]),
                                 s_dv)
                            cp_dv.append(P.val(s_dv))
                        sums3 = FPh[0:128, oS:oS + 260].rearrange(
                            "p (s c) -> p s c", s=4)[:, :, 64:65]
                        P.wait("scalar", s_dv, cp_dv[-1])
                        P.op("scalar", lambda e, sums3=sums3: e.activation(
                             out=sums3, in_=sums3, func=AF.Ln))
                        P.op("scalar", lambda e, sums3=sums3: e.activation(
                             out=sums3, in_=sums3, func=AF.Exp, scale=-1.0), s_ac)
                        P.wait("vector", s_ac, P.val(s_ac))
                        for sub in range(4):
                            t_idx = qt * 4 + sub
                            dst = B2(o_attn + t_idx * D + h * 64, 64)
                            rec = F2(oS + 65 * sub + 64, 1)
                            P.op("vector", lambda e, dst=dst, rec=rec, oS=oS,
                                 sub=sub:
                                 e.tensor_scalar(out=dst, in0=F2(oS + 65 * sub, 64),
                                                 scalar1=rec, scalar2=None,
                                                 op0=AL.mult), s_dv)
            attn_dv = P.val(s_dv)
            if "attn" in dbg:
                P.wait("gpsimd", s_dv, attn_dv)
                P.dma("gpsimd", dbg_out["attn"], B2(o_attn, NT * D), s_io)
            if "q0" in dbg:
                P.wait("gpsimd", s_dv, attn_dv)
                P.dma("gpsimd", dbg_out["q0"], B2(o_qp, 2 * S), s_io)
            if "k0" in dbg:
                P.wait("gpsimd", s_dv, attn_dv)
                P.dma("gpsimd", dbg_out["k0"], B2(o_kp, 2 * S), s_io)

        if phases >= 5:
            P.wait("sync", s_dv, attn_dv)
            lv = P.dma("sync", F2(of_rW, D), ap["r_w_out"], s_ld)
            lv = P.dma("sync", F2(of_rB, D), ap["r_b_out"], s_ld)
            P.wait("vector", s_ld, lv)
            P.wait("vector", s_dv, attn_dv)
            P.wait("scalar", s_ld, P.val(s_ld))
            P.op("scalar", lambda e: e.activation(out=F2(of_rW, D), in_=F2(of_rW, D),
                                                  func=AF.Abs))
            P.op("scalar", lambda e: e.activation(out=F2(of_rW, D), in_=F2(of_rW, D),
                                                  func=AF.Ln, bias=F2(of_one, 1)))
            P.op("scalar", lambda e: e.activation(out=F2(of_rW, D), in_=F2(of_rW, D),
                                                  func=AF.Exp, scale=-1.0), s_ac)
            P.wait("vector", s_ac, P.val(s_ac))
            x1_tiled = ap["x1_hbm"].rearrange("(t p) d -> t p d", p=128)
            ln2_hbm_t = ap["ln2_hbm"].rearrange("(t p) d -> t p d", p=128)
            for t in range(NT):
                th = F2(of_th, D)
                att_bf = B2(o_attn + t * D, D)
                P.op("vector", lambda e, th=th, att_bf=att_bf: e.tensor_tensor(
                    out=th, in0=att_bf, in1=F2(of_rW, D), op=AL.mult))
                P.op("vector", lambda e, th=th: e.tensor_tensor(
                    out=th, in0=th, in1=F2(of_rB, D), op=AL.add))
                emit_trig(of_th, D, of_k1, of_u1, of_w1, F2(of_sv, D), F2(of_cv, D),
                          add_out=F2(of_sq, D))
                lvx = P.dma("sync", F2(of_xtile, D), x_tiled[t], s_ld)
                P.wait("vector", s_ld, lvx)
                P.op("vector", lambda e: e.tensor_tensor(
                    out=F2(of_xtile, D), in0=F2(of_xtile, D),
                    in1=F2(of_sq, D), op=AL.add), s_dv)
                P.wait("gpsimd", s_dv, P.val(s_dv))
                P.dma("gpsimd", x1_tiled[t], F2(of_xtile, D), s_io)
                if "x1" in dbg:
                    P.dma("gpsimd", dbg_out["x1"][:, t * D:(t + 1) * D],
                          F2(of_xtile, D), s_io)
                if "thout" in dbg:
                    P.dma("gpsimd", dbg_out["thout"][:, t * D:(t + 1) * D],
                          F2(of_sq, D), s_io)
                emit_ln(F2(of_xtile, D), B2(o_ln2tm + t * D, D))
                P.wait("gpsimd", s_dv, P.val(s_dv))
                P.dma("gpsimd", ln2_hbm_t[t], B2(o_ln2tm + t * D, D), s_io)
                P.wait("sync", s_dv, P.val(s_dv))
                P.wait("sync", s_ac, P.val(s_ac))
                P.wait("sync", s_io, P.val(s_io))
            ln2_dv = P.val(s_dv)
            ln2_io = P.val(s_io)
            if "ln2" in dbg:
                P.wait("gpsimd", s_dv, ln2_dv)
                P.dma("gpsimd", dbg_out["ln2"], B2(o_ln2tm, NT * D), s_io)

        if phases >= 6:
            P.wait("tensor", s_dv, ln2_dv)
            tcnt = 0
            for t in range(NT):
                for c in range(NC):
                    src = B2(o_ln2tm + t * D + 128 * c, 128)
                    pt = PT_B0[:, 0:128] if (tcnt % 2 == 0) else PT_B1[:, 0:128]
                    P.op("tensor", lambda e, pt=pt, src=src:
                         e.transpose(pt, src, ident_bf), s_mm)
                    P.wait("vector", s_mm, P.val(s_mm))
                    dst = B2(o_ln2fm + c * S + 128 * t, 128)
                    P.op("vector", lambda e, dst=dst, pt=pt:
                         e.tensor_copy(out=dst, in_=pt), s_dv)
                    if tcnt >= 1:
                        P.wait("tensor", s_dv, P.val(s_dv) - 1)
                    tcnt += 1
            ln2fm_dv = P.val(s_dv)
            P.wait("tensor", s_dv, ln2fm_dv)
            P.wait("tensor", s_ld, P.val(s_ld))
            P.wait("sync", s_dv, ln2_dv)
            lv64 = P.dma("sync", F2(of_tr64, 64), ap["c_tr64"], s_ld)
            P.op("vector", lambda e: e.memset(F2(of_nine, 1), 9.0), s_dv)
            nine_ap = F2(of_nine, 1)
            for t in range(NT):
                for c in range(NC):
                    P.op("tensor", lambda e, t=t, c=c:
                         e.matmul(PS0[:, 0:E], B2(o_ln2fm + c * S + 128 * t, 128),
                                  B3(oc_wr, NC, E)[:, c, :],
                                  start=(c == 0), stop=(c == NC - 1)),
                         s_mm if c == NC - 1 else None)
                P.wait("vector", s_mm, P.val(s_mm))
                lg = F2(of_prod, E)
                expP = F2(of_expall + E * t, E)
                ssum = F2(of_recall + t, 1)
                P.op("vector", lambda e, lg=lg: e.tensor_add(
                    out=lg, in0=PS0[:, 0:E], in1=F2(of_br, E)), s_dv)
                P.wait("tensor", s_dv, P.val(s_dv))
                P.wait("scalar", s_dv, P.val(s_dv))
                P.op("scalar", lambda e, lg=lg, expP=expP, ssum=ssum:
                     e.activation(out=expP, in_=lg, func=AF.Exp, accum_out=ssum))
                P.op("scalar", lambda e, ssum=ssum:
                     e.activation(out=ssum, in_=ssum, func=AF.Ln))
                P.op("scalar", lambda e, ssum=ssum:
                     e.activation(out=ssum, in_=ssum, func=AF.Exp, scale=-1.0), s_ac)
                P.wait("vector", s_ac, P.val(s_ac))
                srt = F2(of_srt, E)
                P.op("vector", lambda e, srt=srt, expP=expP:
                     e.max(out=srt, in_=expP), s_dv)
                P.wait("scalar", s_dv, P.val(s_dv))
                sc0 = F2(of_sc2, 1); sc1 = F2(of_sc2 + 1, 1)
                P.op("scalar", lambda e, srt=srt, sc0=sc0:
                     e.activation(out=sc0, in_=srt[:, 0:1], func=AF.Copy))
                P.op("scalar", lambda e, srt=srt, sc1=sc1:
                     e.activation(out=sc1, in_=srt[:, 1:2], func=AF.Copy), s_ac)
                P.wait("vector", s_ac, P.val(s_ac))
                spacer()
                P.op("vector", lambda e, srt=srt, ssum=ssum, t=t: e.tensor_scalar(
                    out=F2(of_Parr + 2 * t, 1), in0=srt[:, 0:1], scalar1=ssum,
                    scalar2=None, op0=AL.mult))
                P.op("vector", lambda e, srt=srt, ssum=ssum, t=t: e.tensor_scalar(
                    out=F2(of_Parr + 2 * t + 1, 1), in0=srt[:, 1:2], scalar1=ssum,
                    scalar2=None, op0=AL.mult), s_dv)
                r8a = F2(of_r8a, E)
                P.op("vector", lambda e, r8a=r8a, expP=expP, sc0=sc0:
                     e.scalar_tensor_tensor(out=r8a, in0=expP, scalar=sc0, in1=im9,
                                            op0=AL.is_equal, op1=AL.mult), s_dv)
                spacer()
                P.op("vector", lambda e, r8a=r8a: e.tensor_reduce(
                    out=F2(of_e1m9, 1), in_=r8a, axis=AX.X, op=AL.min), s_dv)
                P.wait("scalar", s_dv, P.val(s_dv))
                P.op("scalar", lambda e, t=t: e.activation(
                    out=F2(of_Earr + 2 * t, 1), in_=F2(of_e1m9, 1), func=AF.Relu,
                    bias=nine_ap), s_ac)
                P.wait("vector", s_ac, P.val(s_ac))
                e1ap = F2(of_Earr + 2 * t, 1)
                P.op("vector", lambda e, e1ap=e1ap: e.tensor_scalar(
                    out=B2(oc_ohs, E), in0=erange, scalar1=e1ap, scalar2=None,
                    op0=AL.is_equal), s_dv)
                r8b = F2(of_r8b, E)
                P.op("vector", lambda e, r8b=r8b, expP=expP, sc1=sc1:
                     e.scalar_tensor_tensor(out=r8b, in0=expP, scalar=sc1, in1=im9,
                                            op0=AL.is_equal, op1=AL.mult))
                P.op("vector", lambda e, e1ap=e1ap: e.tensor_scalar(
                    out=F2(of_prod + 8, E), in0=erange, scalar1=e1ap, scalar2=20.0,
                    op0=AL.is_equal, op1=AL.mult), s_dv)
                spacer()
                P.op("vector", lambda e, r8b=r8b: e.tensor_add(
                    out=r8b, in0=r8b, in1=F2(of_prod + 8, E)), s_dv)
                spacer()
                P.op("vector", lambda e, r8b=r8b: e.tensor_reduce(
                    out=F2(of_e2m9, 1), in_=r8b, axis=AX.X, op=AL.min), s_dv)
                P.wait("scalar", s_dv, P.val(s_dv))
                P.op("scalar", lambda e, t=t: e.activation(
                    out=F2(of_Earr + 2 * t + 1, 1), in_=F2(of_e2m9, 1), func=AF.Relu,
                    bias=nine_ap), s_ac)
                P.wait("vector", s_ac, P.val(s_ac))
                e2ap = F2(of_Earr + 2 * t + 1, 1)
                P.op("vector", lambda e, e2ap=e2ap: e.tensor_scalar(
                    out=B2(oc_oh2, E), in0=erange, scalar1=e2ap, scalar2=None,
                    op0=AL.is_equal), s_dv)
                P.wait("tensor", s_dv, P.val(s_dv))
                P.op("tensor", lambda e, t=t:
                     e.matmul(PS1[0:16, 0:E], B2(oc_onescol + 16 * t, 16),
                              B2(oc_ohs, E), start=(t == 0), stop=False))
                P.op("tensor", lambda e, t=t:
                     e.matmul(PS1[0:16, 0:E], B2(oc_onescol + 16 * t, 16),
                              B2(oc_oh2, E), start=False, stop=(t == NT - 1)),
                     s_mm if t == NT - 1 else None)
            P.wait("vector", s_mm, P.val(s_mm))
            P.op("vector", lambda e: e.tensor_copy(
                out=BFh[0:16, oc_apbf + 8:oc_apbf + 16], in_=PS1[0:16, 0:E]), s_dv)
            P.wait("tensor", s_dv, P.val(s_dv))
            P.op("tensor", lambda e: e.matmul(
                PS2[0:16, 0:E], BFh[0:16, oc_lt16:oc_lt16 + 16],
                BFh[0:16, oc_apbf + 8:oc_apbf + 16], start=True, stop=True), s_mm)
            P.wait("vector", s_mm, P.val(s_mm))
            a_f = F2(of_prod, E, parts=16)
            kq = F2(of_prod + 8, E, parts=16)
            P.op("vector", lambda e, a_f=a_f: e.tensor_copy(
                out=a_f, in_=PS2[0:16, 0:E]), s_dv)
            spacer()
            P.op("vector", lambda e, a_f=a_f, kq=kq: e.tensor_scalar(
                out=kq, in0=a_f, scalar1=1.0 / 256.0, scalar2=MAGIC - 0.5,
                op0=AL.mult, op1=AL.add), s_dv)
            spacer()
            P.op("vector", lambda e, kq=kq: e.tensor_scalar(
                out=kq, in0=kq, scalar1=MAGIC, scalar2=None, op0=AL.subtract), s_dv)
            spacer()
            P.op("vector", lambda e, kq=kq: e.tensor_scalar(
                out=BFh[0:16, oc_apbf:oc_apbf + 8], in0=kq, scalar1=256.0,
                scalar2=None, op0=AL.mult), s_dv)
            P.op("vector", lambda e, kq=kq, a_f=a_f: e.scalar_tensor_tensor(
                out=BFh[0:16, oc_aplo:oc_aplo + 8], in0=kq, scalar=-256.0,
                in1=a_f, op0=AL.mult, op1=AL.add), s_dv)
            apre_dv = P.val(s_dv)

        if phases >= 10:
            w1_h = ap["w1s"].rearrange("e (c p) h -> e p c h", p=128)
            w2_h = ap["w2s"].rearrange("e (hc p) d -> e p hc d", p=128)
            slot_off = [o_wA, o_wB]
            pre_free = P.val(s_mm)
            slot_free_mm = [pre_free, pre_free]
            loads = []
            for e_i in range(E):
                loads += [(e_i, "w1", 0), (e_i, "w1", 1),
                          (e_i, "w2", 0), (e_i, "w2", 1)]
            lv_load = [None] * len(loads)
            lv2_load = [0] * len(loads)

            def ensure_load(i):
                if i >= len(loads) or lv_load[i] is not None:
                    return
                e_i, kind, half = loads[i]
                sl = i % 2
                P.wait("gpsimd", s_mm, slot_free_mm[sl])
                P.wait("sync", s_mm, slot_free_mm[sl])
                P.wait("sync", s_w, cast_val[e_i])
                if kind == "w1":
                    src = w1_h[e_i][:, :, 2048 * half:2048 * (half + 1)]
                    dst = B3(slot_off[sl], 8, 2048)
                    lv_load[i] = P.dma("gpsimd", dst[:, 0:4, :], src[:, 0:4, :],
                                       s_w)
                    lv2_load[i] = P.dma("sync", dst[:, 4:8, :], src[:, 4:8, :],
                                        s_w2)
                else:
                    src = w2_h[e_i][:, :, 512 * half:512 * (half + 1)]
                    dst = B3(slot_off[sl], 32, 512)
                    lv_load[i] = P.dma("gpsimd", dst[:, 0:16, :], src[:, 0:16, :],
                                       s_w)
                    lv2_load[i] = P.dma("sync", dst[:, 16:32, :], src[:, 16:32, :],
                                        s_w2)

            ensure_load(0)
            ensure_load(1)

        if phases >= 7:
            P.wait("tensor", s_dv, apre_dv)
            for t in range(NT):
                e1ap = F2(of_Earr + 2 * t, 1)
                e2ap = F2(of_Earr + 2 * t + 1, 1)
                P.wait("vector", s_mm, P.val(s_mm))
                P.op("vector", lambda e, e1ap=e1ap: e.tensor_scalar(
                    out=B2(oc_ohs, E), in0=erange, scalar1=e1ap, scalar2=None,
                    op0=AL.is_equal))
                P.op("vector", lambda e, e2ap=e2ap: e.tensor_scalar(
                    out=B2(oc_oh2, E), in0=erange, scalar1=e2ap, scalar2=None,
                    op0=AL.is_equal), s_dv)
                P.wait("tensor", s_dv, P.val(s_dv))
                P.op("tensor", lambda e, t=t: e.matmul(
                    PS0[:, 0:E],
                    BFh[0:16, oc_rowsel + 128 * t:oc_rowsel + 128 * (t + 1)],
                    BFh[0:16, oc_apbf:oc_apbf + 8], start=True, stop=False))
                P.op("tensor", lambda e, t=t: e.matmul(
                    PS0[:, 0:E],
                    BFh[0:16, oc_rowsel + 128 * t:oc_rowsel + 128 * (t + 1)],
                    BFh[0:16, oc_aplo:oc_aplo + 8], start=False, stop=False))
                P.op("tensor", lambda e: e.matmul(
                    PS0[:, 0:E], BFh[0:128, oc_lt128:oc_lt128 + 128],
                    B2(oc_ohs, E), start=False, stop=False))
                P.op("tensor", lambda e: e.matmul(
                    PS0[:, 0:E], BFh[0:128, oc_lt128:oc_lt128 + 128],
                    B2(oc_oh2, E), start=False, stop=True), s_mm)
                P.wait("vector", s_mm, P.val(s_mm))
                P.op("vector", lambda e, e1ap=e1ap: e.scalar_tensor_tensor(
                    out=F2(of_prod, E), in0=erange, scalar=e1ap, in1=PS0[:, 0:E],
                    op0=AL.is_equal, op1=AL.mult))
                P.op("vector", lambda e, e2ap=e2ap: e.scalar_tensor_tensor(
                    out=F2(of_prod + 8, E), in0=erange, scalar=e2ap, in1=PS0[:, 0:E],
                    op0=AL.is_equal, op1=AL.mult), s_dv)
                spacer()
                P.op("vector", lambda e, t=t: e.reduce_sum(
                    out=F2(of_posall + 2 * t, 1), in_=F2(of_prod, E), axis=AX.X))
                P.op("vector", lambda e, t=t: e.reduce_sum(
                    out=F2(of_posall + 2 * t + 1, 1), in_=F2(of_prod + 8, E),
                    axis=AX.X), s_dv)
                P.wait("tensor", s_dv, P.val(s_dv))
            P.wait("vector", s_ld, lv64)
            spacer()
            posall = F2(of_posall, 64)
            nk = F2(of_nk, 64); kp = F2(of_kp, 64)
            slc = F2(of_slc, 64); sca = F2(of_sca, 64); scb = F2(of_scb, 64)
            P.op("vector", lambda e: e.tensor_scalar(
                out=nk, in0=posall, scalar1=float(CAP) - 0.5, scalar2=None,
                op0=AL.is_gt), s_dv)
            P.wait("scalar", s_dv, P.val(s_dv))
            P.op("scalar", lambda e: e.activation(
                out=kp, in_=nk, func=AF.Relu, scale=-1.0, bias=F2(of_one, 1)), s_ac)
            P.wait("vector", s_ac, P.val(s_ac))
            P.op("vector", lambda e: e.tensor_mul(
                out=F2(of_Garr, 64), in0=F2(of_Parr, 64), in1=kp), s_dv)
            P.op("vector", lambda e: e.tensor_scalar(
                out=slc, in0=posall, scalar1=float(CAP - 1), scalar2=None,
                op0=AL.min), s_dv)
            spacer()
            P.op("vector", lambda e: e.scalar_tensor_tensor(
                out=slc, in0=F2(of_Earr, 64), scalar=float(CAP), in1=slc,
                op0=AL.mult, op1=AL.add), s_dv)
            spacer()
            P.op("vector", lambda e: e.tensor_copy(
                out=U3h[0:128, 0:32], in_=F2(of_slc, 32)), s_dv)
            P.op("vector", lambda e: e.tensor_mul(out=sca, in0=slc, in1=kp))
            P.op("vector", lambda e: e.tensor_mul(out=scb, in0=F2(of_tr64, 64),
                                                  in1=nk), s_dv)
            spacer()
            P.op("vector", lambda e: e.tensor_add(out=sca, in0=sca, in1=scb), s_dv)
            spacer()
            P.op("vector", lambda e: e.tensor_copy(
                out=U3h[0:128, 32:64], in_=F2(of_sca, 32)), s_dv)
            route_dv = P.val(s_dv)
            if "route" in dbg:
                P.wait("gpsimd", s_dv, route_dv)
                P.dma("gpsimd", dbg_out["route"],
                      F2(of_Earr, 256), s_io)

        if phases >= 8:
            P.op("vector", lambda e: e.memset(SMh[:, 0:512], 0.0), s_dv)
            zdv = P.val(s_dv)
            P.wait("gpsimd", s_dv, zdv)
            sb_flat = ap["slotbuf"].rearrange("(a p) c -> p a c", p=128)
            for (lo, hi) in ((0, 8), (8, 16), (16, 21)):
                P.dma("gpsimd", sb_flat[:, lo:hi, :],
                      SMh[:, 0:(hi - lo) * 64].rearrange("p (a c) -> p a c", c=64), s_io)
            zio = P.val(s_io)
            P.wait("vector", s_dv, route_dv)
            P.op("vector", lambda e: e.memset(F2(of_scsrc, NT * 64), 0.0), s_dv)
            scs3 = FPh[0:128, of_scsrc:of_scsrc + NT * 64].rearrange(
                "p (t c) -> p t c", t=NT)
            P.wait("vector", s_dv, P.val(s_dv))
            P.op("vector", lambda e: e.tensor_scalar(
                out=scs3[:, :, 0:1].rearrange("p t c -> p (t c)"),
                in0=F2(of_tokid, NT),
                scalar1=1.0, scalar2=None, op0=AL.add), s_dv)
            scdv = P.val(s_dv)
            P.wait("gpsimd", s_dv, scdv)
            P.wait("gpsimd", s_io, zio)
            for t in range(NT):
                for kk in range(2):
                    idx_col = U3h[0:128, 32 + 2 * t + kk:33 + 2 * t + kk]
                    P.iscatter(ap["slotbuf"], idx_col, scs3[:, t, :], s_g)
            P.wait("gpsimd", s_g, P.val(s_g))
            if "u3" in dbg:
                P.dma("gpsimd", dbg_out["u3"], U3h[0:128, 0:64], s_io)
            if "scs" in dbg:
                P.dma("gpsimd", dbg_out["scs"], F2(of_scsrc, NT * 64), s_io)
            if "fpd" in dbg:
                P.dma("gpsimd", dbg_out["fpd"], F2(2300, 128), s_io)
            tok_src = ap["slotbuf"].rearrange("(b p) c -> p b c", p=128)
            P.dma("gpsimd",
                  FPh[0:128, of_tokf:of_tokf + NB].rearrange("p (a b) -> p a b", b=1),
                  tok_src[:, 0:NB, 0:1], s_io, allow_slow_non_contiguous=True)
            rv = P.val(s_io)
            P.wait("vector", s_io, rv)
            P.op("vector", lambda e: e.tensor_scalar(
                out=U3h[0:128, 2 * NT:2 * NT + NB], in0=F2(of_tokf, NB),
                scalar1=-1.0, scalar2=0.0, op0=AL.add, op1=AL.max), s_dv)
            tok_dv = P.val(s_dv)
            if "tok128" in dbg:
                P.wait("gpsimd", s_dv, tok_dv)
                P.dma("gpsimd", dbg_out["tok128"], U3h[0:128, 2 * NT:2 * NT + NB], s_io)

        if phases >= 9:
            buf3 = B3(o_buffm, 8, NSLOT)
            P.wait("gpsimd", s_dv, tok_dv)
            P.wait("gpsimd", s_io, ln2_io)
            mm_after_tile = []
            for b in range(NB):
                idx_col = U3h[0:128, 2 * NT + b:2 * NT + b + 1]
                if b >= 1:
                    P.wait("gpsimd", s_mm, mm_after_tile[b - 1])
                P.igather(B2(o_g0, D), ap["ln2_hbm"], idx_col, s_g)
                gv = P.val(s_g)
                P.wait("tensor", s_g, gv)
                for c in range(NC):
                    pt = PT_B0[:, 0:128] if (c % 2 == 0) else PT_B1[:, 0:128]
                    P.op("tensor", lambda e, pt=pt, c=c:
                         e.transpose(pt, B2(o_g0 + 128 * c, 128), ident_bf), s_mm)
                    P.wait("vector", s_mm, P.val(s_mm))
                    dst = buf3[:, c, 128 * b:128 * (b + 1)]
                    P.op("vector", lambda e, dst=dst, pt=pt:
                         e.tensor_copy(out=dst, in_=pt), s_dv)
                    P.wait("tensor", s_dv, P.val(s_dv) - 1)
                mm_after_tile.append(P.val(s_mm))
            buf_dv = P.val(s_dv)
            if "buf" in dbg:
                P.wait("gpsimd", s_dv, buf_dv)
                P.dma("gpsimd", dbg_out["buf"], B2(o_buffm, 8 * NSLOT), s_io)

        if phases >= 10:
            h3 = B3(o_hfm, 32, CAP)
            P.wait("tensor", s_dv, buf_dv)
            for e_i in range(E):
                for half in range(2):
                    i = 4 * e_i + half
                    ensure_load(i)
                    ensure_load(i + 1)
                    P.wait("tensor", s_w, lv_load[i])
                    P.wait("tensor", s_w2, lv2_load[i])
                    woff = slot_off[i % 2]
                    for ht in range(16):
                        ps = PS0 if (ht % 2 == 0) else PS1
                        P.wait("tensor", s_ac, max(0, P.val(s_ac) - 1))
                        for c in range(NC):
                            P.op("tensor",
                                 lambda e, ps=ps, woff=woff, c=c, ht=ht, e_i=e_i:
                                 e.matmul(ps[:, 0:CAP],
                                          B3(woff, 8, 2048)[:, c,
                                                            128 * ht:128 * (ht + 1)],
                                          buf3[:, c, e_i * CAP:(e_i + 1) * CAP],
                                          start=(c == 0), stop=(c == NC - 1)),
                                 s_mm if c == NC - 1 else None)
                        P.wait("scalar", s_mm, P.val(s_mm))
                        hti = 16 * half + ht
                        b1_ap = F2(of_b1l + e_i * 32 + hti, 1)
                        P.op("scalar", lambda e, ps=ps, hti=hti, b1_ap=b1_ap:
                             e.activation(out=h3[:, hti, :], in_=ps[:, 0:CAP],
                                          func=AF.Relu, bias=b1_ap), s_ac)
                    slot_free_mm[i % 2] = P.val(s_mm)
                P.wait("tensor", s_ac, P.val(s_ac))
                for chalf in range(2):
                    i = 4 * e_i + 2 + chalf
                    ensure_load(i)
                    ensure_load(i + 1)
                    P.wait("gpsimd", s_mm, P.val(s_mm))
                    b2lv = P.dma("gpsimd", B2(o_b2slot, 512, parts=1),
                                 ap["b2"][loads[i][0]:loads[i][0] + 1,
                                          512 * chalf:512 * (chalf + 1)], s_w)
                    P.wait("tensor", s_w, max(lv_load[i], b2lv))
                    P.wait("tensor", s_w2, lv2_load[i])
                    woff = slot_off[i % 2]
                    for ss in range(3):
                        sn = 128 if ss < 2 else 64
                        ps = PS2 if (ss % 2 == 0) else PS3
                        P.wait("tensor", s_dv, max(0, P.val(s_dv) - 1))
                        for hh2 in range(32):
                            P.op("tensor", lambda e, ps=ps, woff=woff, hh2=hh2, ss=ss,
                                 sn=sn:
                                 e.matmul(ps[0:sn, 0:512],
                                          h3[:, hh2, ss * 128:ss * 128 + sn],
                                          B3(woff, 32, 512)[:, hh2, :],
                                          start=(hh2 == 0), stop=False))
                        P.op("tensor", lambda e, ps=ps, sn=sn:
                             e.matmul(ps[0:sn, 0:512],
                                      BFh[0:1, oc_ones1:oc_ones1 + sn],
                                      BFh[0:1, o_b2slot:o_b2slot + 512],
                                      start=False, stop=True), s_mm)
                        P.wait("vector", s_mm, P.val(s_mm))
                        P.op("vector", lambda e, ps=ps, sn=sn:
                             e.tensor_copy(out=B2(o_yst, 512, parts=sn),
                                           in_=ps[0:sn, 0:512]), s_dv)
                        ydv = P.val(s_dv)
                        P.wait("gpsimd", s_dv, ydv)
                        P.dma("gpsimd",
                              ap["y_hbm"][e_i * CAP + ss * 128:
                                          e_i * CAP + ss * 128 + sn,
                                          512 * chalf:512 * (chalf + 1)],
                              B2(o_yst, 512, parts=sn), s_io)
                        P.wait("vector", s_io, P.val(s_io))
                    slot_free_mm[i % 2] = P.val(s_mm)
            y_io = P.val(s_io)
            if "h0" in dbg:
                hv = P.val(s_ac)
                P.wait("gpsimd", s_ac, hv)
                P.dma("gpsimd", dbg_out["h0"], B2(o_hfm, 32 * CAP), s_io)

        if phases >= 11:
            out_t = ap["out"].rearrange("(t p) d -> t p d", p=128)
            P.wait("gpsimd", s_io, y_io)
            for t in range(NT):
                lvx = P.dma("sync", F2(of_xtile, D), x1_tiled[t], s_ld)
                P.igather(B2(o_g0, D), ap["y_hbm"],
                          U3h[0:128, 2 * t:2 * t + 1], s_g)
                g1v = P.igather(B2(o_g1, D), ap["y_hbm"],
                                U3h[0:128, 2 * t + 1:2 * t + 2], s_g)
                P.wait("vector", s_ld, lvx)
                P.wait("vector", s_g, g1v)
                g0_ap = F2(of_Garr + 2 * t, 1)
                g1_ap = F2(of_Garr + 2 * t + 1, 1)
                P.op("vector", lambda e, g0_ap=g0_ap: e.scalar_tensor_tensor(
                    out=F2(of_xtile, D), in0=B2(o_g0, D), scalar=g0_ap,
                    in1=F2(of_xtile, D), op0=AL.mult, op1=AL.add))
                P.op("vector", lambda e, g1_ap=g1_ap: e.scalar_tensor_tensor(
                    out=F2(of_xtile, D), in0=B2(o_g1, D), scalar=g1_ap,
                    in1=F2(of_xtile, D), op0=AL.mult, op1=AL.add), s_dv)
                odv = P.val(s_dv)
                P.wait("gpsimd", s_dv, odv)
                P.dma("gpsimd", out_t[t], F2(of_xtile, D), s_st)
                P.wait("sync", s_dv, odv)
                P.wait("sync", s_st, P.val(s_st))
                P.wait("gpsimd", s_g, P.val(s_g))
            P.wait("gpsimd", s_st, P.val(s_st))
            P.wait("gpsimd", s_io, P.val(s_io))
            P.wait("sync", s_st, P.val(s_st))

        for sem in (s_st, s_io, s_g, s_sh, s_w, s_ld):
            P.wait("gpsimd", sem, P.val(sem))
        P.wait("gpsimd", s_dv, P.val(s_dv))
        P.wait("gpsimd", s_ac, P.val(s_ac))
        P.wait("gpsimd", s_mm, P.val(s_mm))
        P.wait("gpsimd", s_ex, P.val(s_ex))

        @block.sync
        def _(e):
            for fn in P.steps["sync"]:
                fn(e)

        @block.gpsimd
        def _(e):
            for fn in P.steps["gpsimd"]:
                fn(e)

        @block.vector
        def _(e):
            for fn in P.steps["vector"]:
                fn(e)

        @block.scalar
        def _(e):
            for fn in P.steps["scalar"]:
                fn(e)

        @block.tensor
        def _(e):
            for fn in P.steps["tensor"]:
                fn(e)

    return nc


_consts_cache = None


def kernel(**inputs):
    global _consts_cache
    if _consts_cache is None:
        _consts_cache = _consts_host()
    consts = _consts_cache
    shared = _prep_inputs(inputs)
    x = np.asarray(inputs["L0_attn_x"], np.float32)
    B = x.shape[0]
    nc = build_program()
    in_maps = []
    for b in range(B):
        m = {"x": np.ascontiguousarray(x[b])}
        m.update(shared)
        m.update(consts)
        in_maps.append(m)
    res = run_bass_kernel_spmd(nc, in_maps, core_ids=list(range(B)))
    out = np.stack([res.results[b]["out"] for b in range(B)], axis=0)
    return out


# version 06 (final)
# speedup vs baseline: 1.3896x; 1.0022x over previous
import math
import numpy as np
import ml_dtypes

import concourse.bass as bass
import concourse.mybir as mybir
from concourse.bass_core_rust import IndirectOffsetOnAxis
from concourse.bass_utils import run_bass_kernel_spmd

F32 = mybir.dt.float32
BF16 = mybir.dt.bfloat16
U32 = mybir.dt.uint32
AF = mybir.ActivationFunctionType
AL = mybir.AluOpType
AX = mybir.AxisListType

bfloat16 = ml_dtypes.bfloat16

S = 2048
D = 1024
H = 16
DH = 64
NT = S // 128
NC = D // 128
E = 8
CAP = 320
NSLOT = E * CAP
NB = NSLOT // 128
HID = 4096
SM_SCALE = float(1.0 / math.sqrt(2 * DH))

PI = float(np.pi)
TWO_PI = float(2 * np.pi)
INV_2PI = float(1.0 / (2 * np.pi))
MAGIC = float(1.5 * 2.0 ** 23)
CW1 = float(np.float32(6.28125))
CW2 = float(np.float32(2.0 * np.pi - 6.28125))
NEG_BIG = -1.0e30

ENGINES = ("sync", "gpsimd", "vector", "scalar", "tensor")

BF_TOTAL = 76800
oc_ident = 0
oc_lt128 = 128
oc_onescol = 256
oc_rowsel = 512
oc_lt16 = 2560
oc_ones1 = 2576
oc_apbf = 2704
oc_ohs = 2720
oc_oh2 = 2728
oc_wr = 2736
oc_aplo = 2800
oR1 = 2816
o_ln1tm = oR1
o_attn = oR1
o_hfm = oR1
o_yst = oR1 + 10240
o_g0 = oR1 + 10752
o_g1 = oR1 + 11776
o_b2slot = oR1 + 12800
oR2 = 19200
o_ln1fm = oR2
o_ln2fm = oR2
o_wA = oR2
oR3 = 35584
o_vtm = oR3
o_ln2tm = oR3
o_buffm = oR3
oR4 = 52224
o_qp = oR4
o_kp = oR4 + 4096
o_sC = oR4 + 8192
o_sS = oR4 + 10240
o_psb = oR4 + 12288
o_wB = 56064

FP_TOTAL = 13312
of_identf = 0
of_trimask = 128
of_trow = 256
of_erange = 2304
of_im9 = 2312
of_tokid = 2320
of_iotap = 2336
of_eps = 2337
of_hpi = 2338
of_one = 2339
of_pp = 2340
of_stats = 2388
of_routT = 2396
of_b1l = 2492
of_br = 2748
of_rW = 2756
of_rB = 3780
of_expall = of_rW + 0
of_recall = of_rW + 128
of_sc2 = of_rW + 144
of_e1m9 = of_rW + 146
of_e2m9 = of_rW + 147
of_Earr = of_rW + 160
of_Parr = of_rW + 224
of_Garr = of_rW + 288
of_posall = of_rW + 352
of_prod = of_rW + 416
of_nk = of_rW + 432
of_kp = of_rW + 496
of_slc = of_rW + 560
of_sca = of_rW + 624
of_scb = of_rW + 688
of_tr64 = of_rW + 752
of_nine = of_rW + 816
of_r8a = of_rW + 824
of_r8b = of_rW + 832
oT = 4808
of_thq, of_k2, of_u2, of_w2 = oT, oT + 2048, oT + 4096, oT + 6144
of_th, of_k1, of_u1, of_w1 = oT, oT + 1024, oT + 2048, oT + 3072
of_sv, of_cv = oT + 4096, oT + 5120
of_xtile, of_sq = oT + 6144, oT + 7168
of_scsrc = oT
of_tokf = 13000
of_prob = 13020
of_srt = 13028
of_oh1 = 13036
of_oh2 = 13044
of_rt1 = 13052
of_rt2 = 13060
assert 13068 <= FP_TOTAL


class Prog:
    def __init__(self):
        self.steps = {e: [] for e in ENGINES}
        self.sem_val = {}

    def emit(self, eng, fn):
        self.steps[eng].append(fn)

    def inc(self, sem, n):
        k = sem.num
        self.sem_val[k] = self.sem_val.get(k, 0) + n
        return self.sem_val[k]

    def val(self, sem):
        return self.sem_val.get(sem.num, 0)

    def dma(self, q, out, in_, sem, **kw):
        v = self.inc(sem, 16)
        self.emit(q, lambda e, out=out, in_=in_, sem=sem, kw=kw:
                  e.dma_start(out=out, in_=in_, **kw).then_inc(sem, 16))
        return v

    def igather(self, out, in_, idx_col, sem):
        v = self.inc(sem, 16)
        self.emit("gpsimd", lambda e, out=out, in_=in_, idx_col=idx_col, sem=sem:
                  e.indirect_dma_start(out=out, out_offset=None, in_=in_,
                                       in_offset=IndirectOffsetOnAxis(idx_col, 0))
                  .then_inc(sem, 16))
        return v

    def iscatter(self, out, idx_col, in_, sem):
        v = self.inc(sem, 16)
        self.emit("gpsimd", lambda e, out=out, in_=in_, idx_col=idx_col, sem=sem:
                  e.indirect_dma_start(out=out,
                                       out_offset=IndirectOffsetOnAxis(idx_col, 0),
                                       in_=in_, in_offset=None)
                  .then_inc(sem, 16))
        return v

    def wait(self, eng, sem, v):
        if v > 0:
            self.emit(eng, lambda e, sem=sem, v=v: e.wait_ge(sem, v))

    def op(self, eng, fn, sem=None, n=1):
        if sem is None:
            self.emit(eng, fn)
            return None
        v = self.inc(sem, n)
        self.emit(eng, lambda e, fn=fn, sem=sem, n=n: fn(e).then_inc(sem, n))
        return v


def _consts_host():
    c = {}
    c["c_ident_bf"] = np.eye(128, dtype=bfloat16)
    c["c_ident_f32"] = np.eye(128, dtype=np.float32)
    c["c_lt128"] = np.triu(np.ones((128, 128), np.float32), k=1).astype(bfloat16)
    c["c_lt16"] = np.triu(np.ones((16, 16), np.float32), k=1).astype(bfloat16)
    m = np.zeros((128, 128), np.float32)
    m[np.arange(128)[:, None] > np.arange(128)[None, :]] = NEG_BIG
    c["c_trimask"] = m
    c["c_trow"] = np.broadcast_to(np.arange(S, dtype=np.float32), (128, S)).copy()
    oc = np.zeros((128, 16 * 16), np.float32)
    for t in range(16):
        oc[:, 16 * t + t] = 1.0
    c["c_onescol"] = oc.astype(bfloat16)
    rs = np.zeros((16, 16 * 128), np.float32)
    for t in range(16):
        rs[t, 128 * t:128 * (t + 1)] = 1.0
    c["c_rowsel"] = rs.astype(bfloat16)
    c["c_erange"] = np.broadcast_to(np.arange(E, dtype=np.float32), (128, E)).copy()
    c["c_im9"] = np.broadcast_to(np.arange(E, dtype=np.float32) - 9.0, (128, E)).copy()
    c["c_tokid"] = (np.arange(NT)[None, :] * 128
                    + np.arange(128)[:, None]).astype(np.float32).copy()
    c["c_iotap"] = np.arange(128, dtype=np.float32).reshape(128, 1).copy()
    c["c_tr64"] = np.broadcast_to(
        (np.arange(128, dtype=np.float32) + NSLOT).reshape(128, 1), (128, 64)).copy()
    c["c_ones1"] = np.ones((1, 128), dtype=bfloat16)
    return c


def _prep_inputs(ins):
    f32 = np.float32
    d = {}

    def pair_layout(w):
        out = np.zeros((128, 8), f32)
        for hp in range(8):
            out[0:64, hp] = w[2 * hp]
            out[64:128, hp] = w[2 * hp + 1]
        return out

    for nm in ("w_q", "b_q", "phi_q", "w_k", "b_k", "phi_k"):
        d["p_" + nm] = pair_layout(np.asarray(ins["L0_attn_" + nm], f32))
    for nm in ("w_v", "b_v"):
        d["r_" + nm] = np.broadcast_to(
            np.asarray(ins["L0_attn_" + nm], f32).reshape(-1), (128, D)).copy()
    for nm in ("w_out", "b_out"):
        d["r_" + nm] = np.broadcast_to(
            np.asarray(ins["L0_attn_" + nm], f32), (128, D)).copy()
    d["r_br"] = np.broadcast_to(np.asarray(ins["L0_ffn_br"], f32), (128, E)).copy()
    d["w_r"] = np.ascontiguousarray(np.asarray(ins["L0_ffn_Wr"], f32))
    d["w1"] = np.ascontiguousarray(np.asarray(ins["L0_ffn_W1"], f32))
    d["w2"] = np.ascontiguousarray(np.asarray(ins["L0_ffn_W2"], f32))
    b1 = np.asarray(ins["L0_ffn_b1"], f32)
    b1l = np.zeros((128, E * 32), f32)
    for e in range(E):
        for ht in range(32):
            b1l[:, e * 32 + ht] = b1[e, 128 * ht:128 * (ht + 1)]
    d["b1l"] = b1l
    d["b2"] = np.ascontiguousarray(np.asarray(ins["L0_ffn_b2"], f32))
    return d


def build_program(dbg=(), phases=11):
    nc = bass.Bass("TRN2", target_bir_lowering=False)
    ap = {}

    def IN(name, shape, dt=F32):
        ap[name] = nc.dram_tensor(name, list(shape), dt, kind="ExternalInput").ap()

    def OUT(name, shape, dt=F32):
        ap[name] = nc.dram_tensor(name, list(shape), dt, kind="ExternalOutput").ap()

    def INT(name, shape, dt=F32):
        ap[name] = nc.dram_tensor(name, list(shape), dt, kind="Internal").ap()

    IN("x", (S, D))
    for nm in ("p_w_q", "p_b_q", "p_phi_q", "p_w_k", "p_b_k", "p_phi_k"):
        IN(nm, (128, 8))
    for nm in ("r_w_v", "r_b_v", "r_w_out", "r_b_out"):
        IN(nm, (128, D))
    IN("r_br", (128, E))
    IN("w_r", (D, E))
    IN("w1", (E, D, HID))
    IN("w2", (E, HID, D))
    IN("b1l", (128, E * 32))
    IN("b2", (E, D))
    IN("c_ident_bf", (128, 128), BF16)
    IN("c_ident_f32", (128, 128))
    IN("c_lt128", (128, 128), BF16)
    IN("c_lt16", (16, 16), BF16)
    IN("c_trimask", (128, 128))
    IN("c_trow", (128, S))
    IN("c_onescol", (128, 256), BF16)
    IN("c_rowsel", (16, 2048), BF16)
    IN("c_erange", (128, E))
    IN("c_im9", (128, E))
    IN("c_tokid", (128, NT))
    IN("c_iotap", (128, 1))
    IN("c_tr64", (128, 64))
    IN("c_ones1", (1, 128), BF16)

    INT("w1s", (E, D, HID), BF16)
    INT("w2s", (E, HID, D), BF16)
    INT("ln2_hbm", (S, D), BF16)
    INT("x1_hbm", (S, D))
    INT("slotbuf", (NSLOT + 128, 64))
    INT("y_hbm", (NSLOT, D), BF16)
    OUT("out", (S, D))

    dbg_out = {}
    dbg_shapes = {
        "ln1": ((128, NT * D), BF16), "v": ((128, NT * H * 65), BF16),
        "q0": ((128, 2 * S), BF16), "k0": ((128, 2 * S), BF16),
        "attn": ((128, NT * D), BF16), "ln2": ((128, NT * D), BF16),
        "route": ((128, 256), F32), "u3": ((128, 64), U32), "scs": ((128, NT * 64), F32), "fpd": ((128, 128), F32), "x1": ((128, NT * D), F32), "thout": ((128, NT * D), F32),
        "tok128": ((128, NB), U32), "buf": ((128, 8 * NSLOT), BF16),
        "h0": ((128, 32 * CAP), BF16),
    }
    for nm in dbg:
        sh, dt = dbg_shapes[nm]
        OUT("dbg_" + nm, sh, dt)
        dbg_out[nm] = ap["dbg_" + nm]

    P = Prog()

    from contextlib import ExitStack
    with ExitStack() as _stack:
        BFh = _stack.enter_context(nc.sbuf_tensor("BF", [128, BF_TOTAL], BF16))
        FPh = _stack.enter_context(nc.sbuf_tensor("FP", [128, FP_TOTAL], F32))
        SMh = _stack.enter_context(nc.sbuf_tensor("SM", [128, 1024], F32))
        U3h = _stack.enter_context(nc.sbuf_tensor("U3", [128, 64], U32))
        PS0 = _stack.enter_context(nc.psum_tensor("PS0", [128, 512], F32))
        PS1 = _stack.enter_context(nc.psum_tensor("PS1", [128, 512], F32))
        PS2 = _stack.enter_context(nc.psum_tensor("PS2", [128, 512], F32))
        PS3 = _stack.enter_context(nc.psum_tensor("PS3", [128, 512], F32))
        PT_B0 = _stack.enter_context(nc.psum_tensor("PT_B0", [128, 128], BF16))
        PT_B1 = _stack.enter_context(nc.psum_tensor("PT_B1", [128, 128], BF16))
        PT_F = _stack.enter_context(nc.psum_tensor("PT_F", [128, 128], F32))
        PS4 = _stack.enter_context(nc.psum_tensor("PS4", [128, 512], F32))
        s_ld = _stack.enter_context(nc.semaphore("s_ld"))
        s_w = _stack.enter_context(nc.semaphore("s_w"))
        s_w2 = _stack.enter_context(nc.semaphore("s_w2"))
        s_w3 = _stack.enter_context(nc.semaphore("s_w3"))
        s_dv = _stack.enter_context(nc.semaphore("s_dv"))
        s_ac = _stack.enter_context(nc.semaphore("s_ac"))
        s_mm = _stack.enter_context(nc.semaphore("s_mm"))
        s_ex = _stack.enter_context(nc.semaphore("s_ex"))
        s_sh = _stack.enter_context(nc.semaphore("s_sh"))
        s_io = _stack.enter_context(nc.semaphore("s_io"))
        s_g = _stack.enter_context(nc.semaphore("s_g"))
        s_st = _stack.enter_context(nc.semaphore("s_st"))
        block = _stack.enter_context(nc.Block())
        def B2(off, n, parts=128, p0=0):
            return BFh[p0:p0 + parts, off:off + n]

        def B3(off, a, b):
            return BFh[0:128, off:off + a * b].rearrange("p (a b) -> p a b", a=a)

        def F2(off, n, parts=128, p0=0):
            return FPh[p0:p0 + parts, off:off + n]

        def spacer():
            P.op("vector", lambda e: e.tensor_copy(out=SMh[:, 768:1024],
                                                   in_=SMh[:, 512:768]))

        ident_bf = B2(oc_ident, 128)
        ident_f = F2(of_identf, 128)
        trimask = F2(of_trimask, 128)
        eps_ap = F2(of_eps, 1)
        halfpi_ap = F2(of_hpi, 1)
        erange = F2(of_erange, E)
        im9 = F2(of_im9, E)

        for nm, (off, n, parts, isbf) in {
            "c_ident_bf": (oc_ident, 128, 128, True),
            "c_lt128": (oc_lt128, 128, 128, True),
            "c_onescol": (oc_onescol, 256, 128, True),
            "c_rowsel": (oc_rowsel, 2048, 16, True),
            "c_lt16": (oc_lt16, 16, 16, True),
            "c_ones1": (oc_ones1, 128, 1, True),
            "c_ident_f32": (of_identf, 128, 128, False),
            "c_trimask": (of_trimask, 128, 128, False),
            "c_trow": (of_trow, S, 128, False),
            "c_erange": (of_erange, E, 128, False),
            "c_im9": (of_im9, E, 128, False),
            "c_tokid": (of_tokid, NT, 128, False),
            "c_iotap": (of_iotap, 1, 128, False),
            "b1l": (of_b1l, E * 32, 128, False),
            "r_br": (of_br, E, 128, False),
        }.items():
            dst = B2(off, n, parts=parts) if isbf else F2(off, n, parts=parts)
            P.dma("sync", dst, ap[nm], s_ld)
        P.dma("gpsimd", B3(oc_wr, NC, E),
              ap["w_r"].rearrange("(c p) e -> p c e", p=128), s_ld)
        P.op("vector", lambda e: e.memset(eps_ap, 1e-5))
        P.op("vector", lambda e: e.memset(F2(of_one, 1), 1.0))
        P.op("vector", lambda e: e.memset(halfpi_ap, PI / 2), s_dv)

        names = ("p_w_q", "p_b_q", "p_phi_q", "p_w_k", "p_b_k", "p_phi_k")
        for i, nm in enumerate(names):
            dst = F2(of_pp + 8 * i, 8)
            lv = P.dma("sync", dst, ap[nm], s_ld)
            if i in (0, 3):
                P.wait("scalar", s_ld, lv)
                P.wait("scalar", s_dv, P.val(s_dv))
                P.op("scalar", lambda e, dst=dst: e.activation(
                    out=dst, in_=dst, func=AF.Abs))
                P.op("scalar", lambda e, dst=dst: e.activation(
                    out=dst, in_=dst, func=AF.Ln, bias=F2(of_one, 1)))
                P.op("scalar", lambda e, dst=dst: e.activation(
                    out=dst, in_=dst, func=AF.Exp, scale=-1.0), s_ac)
        ld_consts = P.val(s_ld)
        ac_pp = P.val(s_ac)

        def emit_ln(src_ap, dst_bf_ap):
            s_sum = F2(of_stats, 1); s_ss = F2(of_stats + 1, 1)
            s_msq = F2(of_stats + 2, 1); s_ssD = F2(of_stats + 3, 1)
            s_var = F2(of_stats + 4, 1); s_rstd = F2(of_stats + 5, 1)
            s_negm = F2(of_stats + 6, 1); s_mb = F2(of_stats + 7, 1)
            sq = F2(of_sq, D)
            P.wait("scalar", s_dv, P.val(s_dv))
            P.op("vector", lambda e: e.reduce_sum(out=s_sum, in_=src_ap, axis=AX.X),
                 s_dv)
            P.op("scalar", lambda e: e.activation(out=sq, in_=src_ap, func=AF.Square,
                                                  accum_out=s_ss), s_ac)
            P.wait("scalar", s_dv, P.val(s_dv))
            P.op("scalar", lambda e: e.activation(out=s_msq, in_=s_sum,
                                                  func=AF.Square, scale=1.0 / D))
            P.op("scalar", lambda e: e.activation(out=s_ssD, in_=s_ss, func=AF.Copy,
                                                  scale=1.0 / D))
            P.op("scalar", lambda e: e.activation(out=s_negm, in_=s_sum, func=AF.Copy,
                                                  scale=-1.0 / D), s_ac)
            P.wait("vector", s_ac, P.val(s_ac))
            P.op("vector", lambda e: e.tensor_sub(out=s_var, in0=s_ssD, in1=s_msq),
                 s_dv)
            P.wait("scalar", s_dv, P.val(s_dv))
            P.op("scalar", lambda e: e.activation(out=s_rstd, in_=s_var, func=AF.Ln,
                                                  bias=eps_ap))
            P.op("scalar", lambda e: e.activation(out=s_rstd, in_=s_rstd, func=AF.Exp,
                                                  scale=-0.5), s_ac)
            P.wait("vector", s_ac, P.val(s_ac))
            P.op("vector", lambda e: e.tensor_scalar(out=s_mb, in0=s_negm,
                                                     scalar1=s_rstd, scalar2=None,
                                                     op0=AL.mult), s_dv)
            spacer()
            P.op("vector", lambda e: e.tensor_scalar(out=dst_bf_ap, in0=src_ap,
                                                     scalar1=s_rstd, scalar2=s_mb,
                                                     op0=AL.mult, op1=AL.add), s_dv)

        def emit_trig(tho, n, ko, uo, wo, sin_out, cos_out, add_out=None,
                      add_srcs=None):
            th = F2(tho, n)
            k = F2(ko, n); u = F2(uo, n); w = F2(wo, n)
            P.op("vector", lambda e: e.tensor_scalar(out=k, in0=th, scalar1=INV_2PI,
                                                     scalar2=MAGIC, op0=AL.mult,
                                                     op1=AL.add))
            P.op("vector", lambda e: e.tensor_scalar(out=k, in0=k, scalar1=MAGIC,
                                                     scalar2=None, op0=AL.subtract))
            P.op("vector", lambda e: e.scalar_tensor_tensor(
                out=u, in0=k, scalar=-CW1, in1=th, op0=AL.mult, op1=AL.add))
            P.op("vector", lambda e: e.scalar_tensor_tensor(
                out=u, in0=k, scalar=-CW2, in1=u, op0=AL.mult, op1=AL.add))
            P.op("vector", lambda e: e.tensor_scalar(out=w, in0=u, scalar1=PI / 2,
                                                     scalar2=None, op0=AL.is_gt))
            P.op("vector", lambda e: e.scalar_tensor_tensor(
                out=w, in0=w, scalar=-TWO_PI, in1=u, op0=AL.mult, op1=AL.add))
            P.op("vector", lambda e: e.tensor_scalar(out=w, in0=w, scalar1=PI / 2,
                                                     scalar2=-1.5 * PI, op0=AL.min,
                                                     op1=AL.max))
            P.op("vector", lambda e: e.tensor_scalar(out=u, in0=u, scalar1=PI,
                                                     scalar2=-PI, op0=AL.min,
                                                     op1=AL.max), s_dv)
            P.wait("scalar", s_dv, P.val(s_dv))
            P.op("scalar", lambda e: e.activation(out=sin_out, in_=u, func=AF.Sin))
            P.op("scalar", lambda e: e.activation(out=cos_out, in_=w, func=AF.Sin,
                                                  bias=halfpi_ap), s_ac)
            if add_out is not None:
                a0, a1 = add_srcs if add_srcs is not None else (sin_out, cos_out)
                P.wait("vector", s_ac, P.val(s_ac))
                P.op("vector", lambda e: e.tensor_add(out=add_out, in0=a0,
                                                      in1=a1), s_dv)

        x_tiled = ap["x"].rearrange("(t p) d -> t p d", p=128)
        P.wait("vector", s_ld, ld_consts)
        P.wait("scalar", s_ld, ld_consts)
        for t in range(NT):
            lv = P.dma("sync", F2(of_xtile, D), x_tiled[t], s_ld)
            P.wait("vector", s_ld, lv)
            P.wait("scalar", s_ld, lv)
            emit_ln(F2(of_xtile, D), B2(o_ln1tm + t * D, D))
            P.wait("sync", s_dv, P.val(s_dv))
            P.wait("sync", s_ac, P.val(s_ac))
        ln1_dv = P.val(s_dv)
        if "ln1" in dbg:
            P.wait("gpsimd", s_dv, ln1_dv)
            P.dma("gpsimd", dbg_out["ln1"], B2(o_ln1tm, NT * D), s_io)

        if phases >= 2:
            P.wait("tensor", s_ld, ld_consts)
            P.wait("tensor", s_dv, ln1_dv)
            tcnt = 0
            for t in range(NT):
                for c in range(NC):
                    src = B2(o_ln1tm + t * D + 128 * c, 128)
                    pt = PT_B0[:, 0:128] if (tcnt % 2 == 0) else PT_B1[:, 0:128]
                    P.op("tensor", lambda e, pt=pt, src=src:
                         e.transpose(pt, src, ident_bf), s_mm)
                    P.wait("vector", s_mm, P.val(s_mm))
                    dst = B2(o_ln1fm + c * S + 128 * t, 128)
                    P.op("vector", lambda e, dst=dst, pt=pt:
                         e.tensor_copy(out=dst, in_=pt), s_dv)
                    if tcnt >= 1:
                        P.wait("tensor", s_dv, P.val(s_dv) - 1)
                    tcnt += 1
            ln1fm_dv = P.val(s_dv)

        if phases >= 3:
            lv = P.dma("sync", F2(of_rW, D), ap["r_w_v"], s_ld)
            lv = P.dma("sync", F2(of_rB, D), ap["r_b_v"], s_ld)
            P.wait("vector", s_ld, lv)
            P.wait("scalar", s_ld, P.val(s_ld))
            P.op("scalar", lambda e: e.activation(out=F2(of_rW, D), in_=F2(of_rW, D),
                                                  func=AF.Abs))
            P.op("scalar", lambda e: e.activation(out=F2(of_rW, D), in_=F2(of_rW, D),
                                                  func=AF.Ln, bias=F2(of_one, 1)))
            P.op("scalar", lambda e: e.activation(out=F2(of_rW, D), in_=F2(of_rW, D),
                                                  func=AF.Exp, scale=-1.0), s_ac)
            vtm4 = BFh[0:128, o_vtm:o_vtm + NT * H * 65].rearrange(
                "p (t h c) -> p t h c", t=NT, h=H)
            P.op("vector", lambda e: e.memset(vtm4[:, :, :, 64:65], 1.0), s_dv)
            P.wait("vector", s_ac, P.val(s_ac))
            for t in range(NT):
                th = F2(of_th, D)
                src = B2(o_ln1tm + t * D, D)
                P.op("vector", lambda e, th=th, src=src: e.tensor_tensor(
                    out=th, in0=src, in1=F2(of_rW, D), op=AL.mult))
                P.op("vector", lambda e, th=th: e.tensor_tensor(
                    out=th, in0=th, in1=F2(of_rB, D), op=AL.add))
                vts = vtm4[:, t, :, 0:64]
                sv3 = F2(of_sv, D).rearrange("p (h c) -> p h c", h=H)
                cv3 = F2(of_cv, D).rearrange("p (h c) -> p h c", h=H)
                emit_trig(of_th, D, of_k1, of_u1, of_w1,
                          F2(of_sv, D), F2(of_cv, D), add_out=vts,
                          add_srcs=(sv3, cv3))
            v_dv = P.val(s_dv)
            if "v" in dbg:
                P.wait("gpsimd", s_dv, v_dv)
                P.dma("gpsimd", dbg_out["v"], B2(o_vtm, NT * H * 65), s_io)

        if phases >= 4:
            av_free_dv = [0, 0]
            w1c_s = ap["w1"].rearrange("e (c p) h -> e p c h", p=128)
            w1c_d = ap["w1s"].rearrange("e (c p) h -> e p c h", p=128)
            w2c_s = ap["w2"].rearrange("e (hc p) d -> e p hc d", p=128)
            w2c_d = ap["w2s"].rearrange("e (hc p) d -> e p hc d", p=128)
            cast_val = [0] * E
            for hp in range(8):
                mm_prev = P.val(s_mm)
                for (dst, base) in ((o_qp, 0), (o_kp, 24)):
                    inv_ap = F2(of_pp + base + 0 + hp, 1)
                    b_ap = F2(of_pp + base + 8 + hp, 1)
                    phi_ap = F2(of_pp + base + 16 + hp, 1)
                    th = F2(of_thq, S)
                    src = B2(o_ln1fm + hp * S, S)
                    P.wait("vector", s_ac, ac_pp)
                    P.wait("vector", s_dv, max(ln1fm_dv, v_dv))
                    P.wait("vector", s_mm, mm_prev)
                    P.op("vector", lambda e, th=th, src=src, inv_ap=inv_ap, b_ap=b_ap:
                         e.tensor_scalar(out=th, in0=src, scalar1=inv_ap, scalar2=b_ap,
                                         op0=AL.mult, op1=AL.add))
                    P.op("vector", lambda e, th=th, phi_ap=phi_ap:
                         e.scalar_tensor_tensor(out=th, in0=F2(of_trow, S),
                                                scalar=phi_ap, in1=th,
                                                op0=AL.mult, op1=AL.add))
                    emit_trig(of_thq, S, of_k2, of_u2, of_w2,
                              B2(o_sS, S), B2(o_sC, S))
                    P.wait("vector", s_ac, P.val(s_ac))
                    P.op("vector", lambda e, dst=dst: e.tensor_copy(
                        out=BFh[0:64, dst:dst + S], in_=BFh[0:64, o_sC:o_sC + S]))
                    P.op("vector", lambda e, dst=dst: e.tensor_copy(
                        out=BFh[64:128, dst + S:dst + 2 * S],
                        in_=BFh[64:128, o_sS:o_sS + S]), s_dv)
                    P.wait("gpsimd", s_ac, P.val(s_ac))
                    P.wait("gpsimd", s_mm, mm_prev)
                    P.dma("gpsimd", BFh[64:128, dst:dst + S],
                          BFh[0:64, o_sS:o_sS + S], s_sh)
                    P.dma("gpsimd", BFh[0:64, dst + S:dst + 2 * S],
                          BFh[64:128, o_sC:o_sC + S], s_sh)
                    P.wait("vector", s_sh, P.val(s_sh))
                    P.wait("scalar", s_sh, P.val(s_sh))
                if phases >= 10:
                    for c in range(8):
                        P.dma("gpsimd", w1c_d[hp][:, c, :], w1c_s[hp][:, c, :],
                              s_w)
                    for g4 in range(8):
                        cast_val[hp] = P.dma(
                            "gpsimd", w2c_d[hp][:, 4 * g4:4 * g4 + 4, :],
                            w2c_s[hp][:, 4 * g4:4 * g4 + 4, :], s_w)
                qk_dv = P.val(s_dv)
                qk_sh = P.val(s_sh)
                P.wait("tensor", s_dv, qk_dv)
                P.wait("tensor", s_sh, qk_sh)
                for hh in range(2):
                    h = 2 * hp + hh
                    qoff = o_qp + hh * S
                    koff = o_kp + hh * S
                    SCB = (PS0, PS1, PS4)
                    for qt in range(4):
                        av_ps = PS2 if (qt % 2 == 0) else PS3
                        nkt = 4 * qt + 4
                        scv = [0] * nkt
                        exv = [0] * nkt
                        avv = [0] * nkt

                        def emit_sc(kt):
                            dd = kt - 4 * qt
                            qlo_rel = 128 * dd if dd >= 0 else 0
                            qlo = qt * 512 + qlo_rel
                            qn = 512 - qlo_rel
                            sc_ps = SCB[kt % 3]
                            if kt >= 3:
                                P.wait("tensor", s_ex, exv[kt - 3])
                            P.op("tensor", lambda e, sc_ps=sc_ps, koff=koff, kt=kt,
                                 qoff=qoff, qlo=qlo, qn=qn:
                                 e.matmul(sc_ps[:, 0:qn], B2(koff + kt * 128, 128),
                                          B2(qoff + qlo, qn), start=True, stop=True),
                                 s_mm)
                            scv[kt] = P.val(s_mm)

                        def emit_exp(kt):
                            dd = kt - 4 * qt
                            qlo_rel = 128 * dd if dd >= 0 else 0
                            qn = 512 - qlo_rel
                            sc_ps = SCB[kt % 3]
                            if dd >= 0:
                                P.wait("vector", s_mm, scv[kt])
                                P.op("vector", lambda e, sc_ps=sc_ps:
                                     e.tensor_tensor(out=sc_ps[:, 0:128],
                                                     in0=sc_ps[:, 0:128],
                                                     in1=trimask, op=AL.add), s_dv)
                                P.wait("scalar", s_dv, P.val(s_dv))
                            else:
                                P.wait("scalar", s_mm, scv[kt])
                            if kt >= 4:
                                P.wait("scalar", s_mm, avv[kt - 4])
                            psb = o_psb + (kt % 4) * 512
                            P.op("scalar", lambda e, psb=psb, sc_ps=sc_ps, qn=qn:
                                 e.activation(out=B2(psb, qn), in_=sc_ps[:, 0:qn],
                                              func=AF.Exp, scale=SM_SCALE), s_ex)
                            exv[kt] = P.val(s_ex)

                        def emit_av(kt):
                            dd = kt - 4 * qt
                            qlo_rel = 128 * dd if dd >= 0 else 0
                            qn = 512 - qlo_rel
                            psb = o_psb + (kt % 4) * 512
                            P.wait("tensor", s_ex, exv[kt])
                            if kt == 0 and av_free_dv[qt % 2]:
                                P.wait("tensor", s_dv, av_free_dv[qt % 2])
                            vsl = vtm4[:, kt, h, :]
                            P.op("tensor", lambda e, av_ps=av_ps, vsl=vsl, psb=psb,
                                 qlo_rel=qlo_rel, qn=qn, first=(kt == 0),
                                 last=(kt == nkt - 1):
                                 e.matmul(av_ps[0:65, qlo_rel:qlo_rel + qn], vsl,
                                          B2(psb, qn), start=first, stop=last), s_mm)
                            avv[kt] = P.val(s_mm)

                        for kt in range(nkt):
                            emit_sc(kt)
                            emit_exp(kt)
                            if kt >= 2:
                                emit_av(kt - 2)
                        emit_av(nkt - 2)
                        emit_av(nkt - 1)

                        stg = SMh[0:65, 0:512]
                        P.wait("vector", s_mm, avv[nkt - 1])
                        P.op("vector", lambda e, av_ps=av_ps, stg=stg:
                             e.tensor_copy(out=stg, in_=av_ps[0:65, 0:512]), s_dv)
                        av_free_dv[qt % 2] = P.val(s_dv)
                        oS = of_sq
                        cp_dv = []
                        for sub in range(4):
                            P.wait("tensor", s_dv,
                                   cp_dv[sub - 1] if sub >= 1
                                   else av_free_dv[qt % 2])
                            P.op("tensor", lambda e, stg=stg, sub=sub:
                                 e.transpose(PT_F[:, 0:65],
                                             stg[0:65, 128 * sub:128 * (sub + 1)],
                                             ident_f[0:65, 0:65]), s_mm)
                            P.wait("vector", s_mm, P.val(s_mm))
                            P.op("vector", lambda e, sub=sub, oS=oS: e.tensor_copy(
                                 out=F2(oS + 65 * sub, 65), in_=PT_F[:, 0:65]),
                                 s_dv)
                            cp_dv.append(P.val(s_dv))
                        sums3 = FPh[0:128, oS:oS + 260].rearrange(
                            "p (s c) -> p s c", s=4)[:, :, 64:65]
                        P.wait("scalar", s_dv, cp_dv[-1])
                        P.op("scalar", lambda e, sums3=sums3: e.activation(
                             out=sums3, in_=sums3, func=AF.Ln))
                        P.op("scalar", lambda e, sums3=sums3: e.activation(
                             out=sums3, in_=sums3, func=AF.Exp, scale=-1.0), s_ac)
                        P.wait("vector", s_ac, P.val(s_ac))
                        for sub in range(4):
                            t_idx = qt * 4 + sub
                            dst = B2(o_attn + t_idx * D + h * 64, 64)
                            rec = F2(oS + 65 * sub + 64, 1)
                            P.op("vector", lambda e, dst=dst, rec=rec, oS=oS,
                                 sub=sub:
                                 e.tensor_scalar(out=dst, in0=F2(oS + 65 * sub, 64),
                                                 scalar1=rec, scalar2=None,
                                                 op0=AL.mult), s_dv)
            attn_dv = P.val(s_dv)
            if "attn" in dbg:
                P.wait("gpsimd", s_dv, attn_dv)
                P.dma("gpsimd", dbg_out["attn"], B2(o_attn, NT * D), s_io)
            if "q0" in dbg:
                P.wait("gpsimd", s_dv, attn_dv)
                P.dma("gpsimd", dbg_out["q0"], B2(o_qp, 2 * S), s_io)
            if "k0" in dbg:
                P.wait("gpsimd", s_dv, attn_dv)
                P.dma("gpsimd", dbg_out["k0"], B2(o_kp, 2 * S), s_io)

        if phases >= 5:
            P.wait("sync", s_dv, attn_dv)
            lv = P.dma("sync", F2(of_rW, D), ap["r_w_out"], s_ld)
            lv = P.dma("sync", F2(of_rB, D), ap["r_b_out"], s_ld)
            P.wait("vector", s_ld, lv)
            P.wait("vector", s_dv, attn_dv)
            P.wait("scalar", s_ld, P.val(s_ld))
            P.op("scalar", lambda e: e.activation(out=F2(of_rW, D), in_=F2(of_rW, D),
                                                  func=AF.Abs))
            P.op("scalar", lambda e: e.activation(out=F2(of_rW, D), in_=F2(of_rW, D),
                                                  func=AF.Ln, bias=F2(of_one, 1)))
            P.op("scalar", lambda e: e.activation(out=F2(of_rW, D), in_=F2(of_rW, D),
                                                  func=AF.Exp, scale=-1.0), s_ac)
            P.wait("vector", s_ac, P.val(s_ac))
            x1_tiled = ap["x1_hbm"].rearrange("(t p) d -> t p d", p=128)
            ln2_hbm_t = ap["ln2_hbm"].rearrange("(t p) d -> t p d", p=128)
            for t in range(NT):
                th = F2(of_th, D)
                att_bf = B2(o_attn + t * D, D)
                P.op("vector", lambda e, th=th, att_bf=att_bf: e.tensor_tensor(
                    out=th, in0=att_bf, in1=F2(of_rW, D), op=AL.mult))
                P.op("vector", lambda e, th=th: e.tensor_tensor(
                    out=th, in0=th, in1=F2(of_rB, D), op=AL.add))
                emit_trig(of_th, D, of_k1, of_u1, of_w1, F2(of_sv, D), F2(of_cv, D),
                          add_out=F2(of_sq, D))
                lvx = P.dma("sync", F2(of_xtile, D), x_tiled[t], s_ld)
                P.wait("vector", s_ld, lvx)
                P.op("vector", lambda e: e.tensor_tensor(
                    out=F2(of_xtile, D), in0=F2(of_xtile, D),
                    in1=F2(of_sq, D), op=AL.add), s_dv)
                P.wait("gpsimd", s_dv, P.val(s_dv))
                P.dma("gpsimd", x1_tiled[t], F2(of_xtile, D), s_io)
                if "x1" in dbg:
                    P.dma("gpsimd", dbg_out["x1"][:, t * D:(t + 1) * D],
                          F2(of_xtile, D), s_io)
                if "thout" in dbg:
                    P.dma("gpsimd", dbg_out["thout"][:, t * D:(t + 1) * D],
                          F2(of_sq, D), s_io)
                emit_ln(F2(of_xtile, D), B2(o_ln2tm + t * D, D))
                P.wait("gpsimd", s_dv, P.val(s_dv))
                P.dma("gpsimd", ln2_hbm_t[t], B2(o_ln2tm + t * D, D), s_io)
                P.wait("sync", s_dv, P.val(s_dv))
                P.wait("sync", s_ac, P.val(s_ac))
                P.wait("sync", s_io, P.val(s_io))
            ln2_dv = P.val(s_dv)
            ln2_io = P.val(s_io)
            if "ln2" in dbg:
                P.wait("gpsimd", s_dv, ln2_dv)
                P.dma("gpsimd", dbg_out["ln2"], B2(o_ln2tm, NT * D), s_io)

        if phases >= 6:
            P.wait("tensor", s_dv, ln2_dv)
            tcnt = 0
            for t in range(NT):
                for c in range(NC):
                    src = B2(o_ln2tm + t * D + 128 * c, 128)
                    pt = PT_B0[:, 0:128] if (tcnt % 2 == 0) else PT_B1[:, 0:128]
                    P.op("tensor", lambda e, pt=pt, src=src:
                         e.transpose(pt, src, ident_bf), s_mm)
                    P.wait("vector", s_mm, P.val(s_mm))
                    dst = B2(o_ln2fm + c * S + 128 * t, 128)
                    P.op("vector", lambda e, dst=dst, pt=pt:
                         e.tensor_copy(out=dst, in_=pt), s_dv)
                    if tcnt >= 1:
                        P.wait("tensor", s_dv, P.val(s_dv) - 1)
                    tcnt += 1
            ln2fm_dv = P.val(s_dv)
            P.wait("tensor", s_dv, ln2fm_dv)
            P.wait("tensor", s_ld, P.val(s_ld))
            P.wait("sync", s_dv, ln2_dv)
            lv64 = P.dma("sync", F2(of_tr64, 64), ap["c_tr64"], s_ld)
            P.op("vector", lambda e: e.memset(F2(of_nine, 1), 9.0), s_dv)
            nine_ap = F2(of_nine, 1)
            for t in range(NT):
                for c in range(NC):
                    P.op("tensor", lambda e, t=t, c=c:
                         e.matmul(PS0[:, 0:E], B2(o_ln2fm + c * S + 128 * t, 128),
                                  B3(oc_wr, NC, E)[:, c, :],
                                  start=(c == 0), stop=(c == NC - 1)),
                         s_mm if c == NC - 1 else None)
                P.wait("vector", s_mm, P.val(s_mm))
                lg = F2(of_prod, E)
                expP = F2(of_expall + E * t, E)
                ssum = F2(of_recall + t, 1)
                P.op("vector", lambda e, lg=lg: e.tensor_add(
                    out=lg, in0=PS0[:, 0:E], in1=F2(of_br, E)), s_dv)
                P.wait("tensor", s_dv, P.val(s_dv))
                P.wait("scalar", s_dv, P.val(s_dv))
                P.op("scalar", lambda e, lg=lg, expP=expP, ssum=ssum:
                     e.activation(out=expP, in_=lg, func=AF.Exp, accum_out=ssum))
                P.op("scalar", lambda e, ssum=ssum:
                     e.activation(out=ssum, in_=ssum, func=AF.Ln))
                P.op("scalar", lambda e, ssum=ssum:
                     e.activation(out=ssum, in_=ssum, func=AF.Exp, scale=-1.0), s_ac)
                P.wait("vector", s_ac, P.val(s_ac))
                srt = F2(of_srt, E)
                P.op("vector", lambda e, srt=srt, expP=expP:
                     e.max(out=srt, in_=expP), s_dv)
                P.wait("scalar", s_dv, P.val(s_dv))
                sc0 = F2(of_sc2, 1); sc1 = F2(of_sc2 + 1, 1)
                P.op("scalar", lambda e, srt=srt, sc0=sc0:
                     e.activation(out=sc0, in_=srt[:, 0:1], func=AF.Copy))
                P.op("scalar", lambda e, srt=srt, sc1=sc1:
                     e.activation(out=sc1, in_=srt[:, 1:2], func=AF.Copy), s_ac)
                P.wait("vector", s_ac, P.val(s_ac))
                spacer()
                P.op("vector", lambda e, srt=srt, ssum=ssum, t=t: e.tensor_scalar(
                    out=F2(of_Parr + 2 * t, 1), in0=srt[:, 0:1], scalar1=ssum,
                    scalar2=None, op0=AL.mult))
                P.op("vector", lambda e, srt=srt, ssum=ssum, t=t: e.tensor_scalar(
                    out=F2(of_Parr + 2 * t + 1, 1), in0=srt[:, 1:2], scalar1=ssum,
                    scalar2=None, op0=AL.mult), s_dv)
                r8a = F2(of_r8a, E)
                P.op("vector", lambda e, r8a=r8a, expP=expP, sc0=sc0:
                     e.scalar_tensor_tensor(out=r8a, in0=expP, scalar=sc0, in1=im9,
                                            op0=AL.is_equal, op1=AL.mult), s_dv)
                spacer()
                P.op("vector", lambda e, r8a=r8a: e.tensor_reduce(
                    out=F2(of_e1m9, 1), in_=r8a, axis=AX.X, op=AL.min), s_dv)
                P.wait("scalar", s_dv, P.val(s_dv))
                P.op("scalar", lambda e, t=t: e.activation(
                    out=F2(of_Earr + 2 * t, 1), in_=F2(of_e1m9, 1), func=AF.Relu,
                    bias=nine_ap), s_ac)
                P.wait("vector", s_ac, P.val(s_ac))
                e1ap = F2(of_Earr + 2 * t, 1)
                P.op("vector", lambda e, e1ap=e1ap: e.tensor_scalar(
                    out=B2(oc_ohs, E), in0=erange, scalar1=e1ap, scalar2=None,
                    op0=AL.is_equal), s_dv)
                r8b = F2(of_r8b, E)
                P.op("vector", lambda e, r8b=r8b, expP=expP, sc1=sc1:
                     e.scalar_tensor_tensor(out=r8b, in0=expP, scalar=sc1, in1=im9,
                                            op0=AL.is_equal, op1=AL.mult))
                P.op("vector", lambda e, e1ap=e1ap: e.tensor_scalar(
                    out=F2(of_prod + 8, E), in0=erange, scalar1=e1ap, scalar2=20.0,
                    op0=AL.is_equal, op1=AL.mult), s_dv)
                spacer()
                P.op("vector", lambda e, r8b=r8b: e.tensor_add(
                    out=r8b, in0=r8b, in1=F2(of_prod + 8, E)), s_dv)
                spacer()
                P.op("vector", lambda e, r8b=r8b: e.tensor_reduce(
                    out=F2(of_e2m9, 1), in_=r8b, axis=AX.X, op=AL.min), s_dv)
                P.wait("scalar", s_dv, P.val(s_dv))
                P.op("scalar", lambda e, t=t: e.activation(
                    out=F2(of_Earr + 2 * t + 1, 1), in_=F2(of_e2m9, 1), func=AF.Relu,
                    bias=nine_ap), s_ac)
                P.wait("vector", s_ac, P.val(s_ac))
                e2ap = F2(of_Earr + 2 * t + 1, 1)
                P.op("vector", lambda e, e2ap=e2ap: e.tensor_scalar(
                    out=B2(oc_oh2, E), in0=erange, scalar1=e2ap, scalar2=None,
                    op0=AL.is_equal), s_dv)
                P.wait("tensor", s_dv, P.val(s_dv))
                P.op("tensor", lambda e, t=t:
                     e.matmul(PS1[0:16, 0:E], B2(oc_onescol + 16 * t, 16),
                              B2(oc_ohs, E), start=(t == 0), stop=False))
                P.op("tensor", lambda e, t=t:
                     e.matmul(PS1[0:16, 0:E], B2(oc_onescol + 16 * t, 16),
                              B2(oc_oh2, E), start=False, stop=(t == NT - 1)),
                     s_mm if t == NT - 1 else None)
            P.wait("vector", s_mm, P.val(s_mm))
            P.op("vector", lambda e: e.tensor_copy(
                out=BFh[0:16, oc_apbf + 8:oc_apbf + 16], in_=PS1[0:16, 0:E]), s_dv)
            P.wait("tensor", s_dv, P.val(s_dv))
            P.op("tensor", lambda e: e.matmul(
                PS2[0:16, 0:E], BFh[0:16, oc_lt16:oc_lt16 + 16],
                BFh[0:16, oc_apbf + 8:oc_apbf + 16], start=True, stop=True), s_mm)
            P.wait("vector", s_mm, P.val(s_mm))
            a_f = F2(of_prod, E, parts=16)
            kq = F2(of_prod + 8, E, parts=16)
            P.op("vector", lambda e, a_f=a_f: e.tensor_copy(
                out=a_f, in_=PS2[0:16, 0:E]), s_dv)
            spacer()
            P.op("vector", lambda e, a_f=a_f, kq=kq: e.tensor_scalar(
                out=kq, in0=a_f, scalar1=1.0 / 256.0, scalar2=MAGIC - 0.5,
                op0=AL.mult, op1=AL.add), s_dv)
            spacer()
            P.op("vector", lambda e, kq=kq: e.tensor_scalar(
                out=kq, in0=kq, scalar1=MAGIC, scalar2=None, op0=AL.subtract), s_dv)
            spacer()
            P.op("vector", lambda e, kq=kq: e.tensor_scalar(
                out=BFh[0:16, oc_apbf:oc_apbf + 8], in0=kq, scalar1=256.0,
                scalar2=None, op0=AL.mult), s_dv)
            P.op("vector", lambda e, kq=kq, a_f=a_f: e.scalar_tensor_tensor(
                out=BFh[0:16, oc_aplo:oc_aplo + 8], in0=kq, scalar=-256.0,
                in1=a_f, op0=AL.mult, op1=AL.add), s_dv)
            apre_dv = P.val(s_dv)

        if phases >= 10:
            w1_h = ap["w1s"].rearrange("e (c p) h -> e p c h", p=128)
            w2_h = ap["w2s"].rearrange("e (hc p) d -> e p hc d", p=128)
            slot_off = [o_wA, o_wB]
            pre_free = P.val(s_mm)
            slot_free_mm = [pre_free, pre_free]
            loads = []
            for e_i in range(E):
                loads += [(e_i, "w1", 0), (e_i, "w1", 1),
                          (e_i, "w2", 0), (e_i, "w2", 1)]
            lv_load = [None] * len(loads)
            lv2_load = [0] * len(loads)
            lv3_load = [0] * len(loads)

            def ensure_load(i):
                if i >= len(loads) or lv_load[i] is not None:
                    return
                e_i, kind, half = loads[i]
                sl = i % 2
                P.wait("gpsimd", s_mm, slot_free_mm[sl])
                P.wait("sync", s_mm, slot_free_mm[sl])
                P.wait("sync", s_w, cast_val[e_i])
                P.wait("scalar", s_mm, slot_free_mm[sl])
                P.wait("scalar", s_w, cast_val[e_i])
                if kind == "w1":
                    src = w1_h[e_i][:, :, 2048 * half:2048 * (half + 1)]
                    dst = B3(slot_off[sl], 8, 2048)
                    lv_load[i] = P.dma("gpsimd", dst[:, 0:3, :], src[:, 0:3, :],
                                       s_w)
                    lv2_load[i] = P.dma("sync", dst[:, 3:6, :], src[:, 3:6, :],
                                        s_w2)
                    lv3_load[i] = P.dma("scalar", dst[:, 6:8, :], src[:, 6:8, :],
                                        s_w3)
                else:
                    src = w2_h[e_i][:, :, 512 * half:512 * (half + 1)]
                    dst = B3(slot_off[sl], 32, 512)
                    lv_load[i] = P.dma("gpsimd", dst[:, 0:11, :], src[:, 0:11, :],
                                       s_w)
                    lv2_load[i] = P.dma("sync", dst[:, 11:22, :], src[:, 11:22, :],
                                        s_w2)
                    lv3_load[i] = P.dma("scalar", dst[:, 22:32, :],
                                        src[:, 22:32, :], s_w3)

            ensure_load(0)
            ensure_load(1)

        if phases >= 7:
            P.wait("tensor", s_dv, apre_dv)
            for t in range(NT):
                e1ap = F2(of_Earr + 2 * t, 1)
                e2ap = F2(of_Earr + 2 * t + 1, 1)
                P.wait("vector", s_mm, P.val(s_mm))
                P.op("vector", lambda e, e1ap=e1ap: e.tensor_scalar(
                    out=B2(oc_ohs, E), in0=erange, scalar1=e1ap, scalar2=None,
                    op0=AL.is_equal))
                P.op("vector", lambda e, e2ap=e2ap: e.tensor_scalar(
                    out=B2(oc_oh2, E), in0=erange, scalar1=e2ap, scalar2=None,
                    op0=AL.is_equal), s_dv)
                P.wait("tensor", s_dv, P.val(s_dv))
                P.op("tensor", lambda e, t=t: e.matmul(
                    PS0[:, 0:E],
                    BFh[0:16, oc_rowsel + 128 * t:oc_rowsel + 128 * (t + 1)],
                    BFh[0:16, oc_apbf:oc_apbf + 8], start=True, stop=False))
                P.op("tensor", lambda e, t=t: e.matmul(
                    PS0[:, 0:E],
                    BFh[0:16, oc_rowsel + 128 * t:oc_rowsel + 128 * (t + 1)],
                    BFh[0:16, oc_aplo:oc_aplo + 8], start=False, stop=False))
                P.op("tensor", lambda e: e.matmul(
                    PS0[:, 0:E], BFh[0:128, oc_lt128:oc_lt128 + 128],
                    B2(oc_ohs, E), start=False, stop=False))
                P.op("tensor", lambda e: e.matmul(
                    PS0[:, 0:E], BFh[0:128, oc_lt128:oc_lt128 + 128],
                    B2(oc_oh2, E), start=False, stop=True), s_mm)
                P.wait("vector", s_mm, P.val(s_mm))
                P.op("vector", lambda e, e1ap=e1ap: e.scalar_tensor_tensor(
                    out=F2(of_prod, E), in0=erange, scalar=e1ap, in1=PS0[:, 0:E],
                    op0=AL.is_equal, op1=AL.mult))
                P.op("vector", lambda e, e2ap=e2ap: e.scalar_tensor_tensor(
                    out=F2(of_prod + 8, E), in0=erange, scalar=e2ap, in1=PS0[:, 0:E],
                    op0=AL.is_equal, op1=AL.mult), s_dv)
                spacer()
                P.op("vector", lambda e, t=t: e.reduce_sum(
                    out=F2(of_posall + 2 * t, 1), in_=F2(of_prod, E), axis=AX.X))
                P.op("vector", lambda e, t=t: e.reduce_sum(
                    out=F2(of_posall + 2 * t + 1, 1), in_=F2(of_prod + 8, E),
                    axis=AX.X), s_dv)
                P.wait("tensor", s_dv, P.val(s_dv))
            P.wait("vector", s_ld, lv64)
            spacer()
            posall = F2(of_posall, 64)
            nk = F2(of_nk, 64); kp = F2(of_kp, 64)
            slc = F2(of_slc, 64); sca = F2(of_sca, 64); scb = F2(of_scb, 64)
            P.op("vector", lambda e: e.tensor_scalar(
                out=nk, in0=posall, scalar1=float(CAP) - 0.5, scalar2=None,
                op0=AL.is_gt), s_dv)
            P.wait("scalar", s_dv, P.val(s_dv))
            P.op("scalar", lambda e: e.activation(
                out=kp, in_=nk, func=AF.Relu, scale=-1.0, bias=F2(of_one, 1)), s_ac)
            P.wait("vector", s_ac, P.val(s_ac))
            P.op("vector", lambda e: e.tensor_mul(
                out=F2(of_Garr, 64), in0=F2(of_Parr, 64), in1=kp), s_dv)
            P.op("vector", lambda e: e.tensor_scalar(
                out=slc, in0=posall, scalar1=float(CAP - 1), scalar2=None,
                op0=AL.min), s_dv)
            spacer()
            P.op("vector", lambda e: e.scalar_tensor_tensor(
                out=slc, in0=F2(of_Earr, 64), scalar=float(CAP), in1=slc,
                op0=AL.mult, op1=AL.add), s_dv)
            spacer()
            P.op("vector", lambda e: e.tensor_copy(
                out=U3h[0:128, 0:32], in_=F2(of_slc, 32)), s_dv)
            P.op("vector", lambda e: e.tensor_mul(out=sca, in0=slc, in1=kp))
            P.op("vector", lambda e: e.tensor_mul(out=scb, in0=F2(of_tr64, 64),
                                                  in1=nk), s_dv)
            spacer()
            P.op("vector", lambda e: e.tensor_add(out=sca, in0=sca, in1=scb), s_dv)
            spacer()
            P.op("vector", lambda e: e.tensor_copy(
                out=U3h[0:128, 32:64], in_=F2(of_sca, 32)), s_dv)
            route_dv = P.val(s_dv)
            if "route" in dbg:
                P.wait("gpsimd", s_dv, route_dv)
                P.dma("gpsimd", dbg_out["route"],
                      F2(of_Earr, 256), s_io)

        if phases >= 8:
            P.op("vector", lambda e: e.memset(SMh[:, 0:512], 0.0), s_dv)
            zdv = P.val(s_dv)
            P.wait("gpsimd", s_dv, zdv)
            sb_flat = ap["slotbuf"].rearrange("(a p) c -> p a c", p=128)
            for (lo, hi) in ((0, 8), (8, 16), (16, 21)):
                P.dma("gpsimd", sb_flat[:, lo:hi, :],
                      SMh[:, 0:(hi - lo) * 64].rearrange("p (a c) -> p a c", c=64), s_io)
            zio = P.val(s_io)
            P.wait("vector", s_dv, route_dv)
            P.op("vector", lambda e: e.memset(F2(of_scsrc, NT * 64), 0.0), s_dv)
            scs3 = FPh[0:128, of_scsrc:of_scsrc + NT * 64].rearrange(
                "p (t c) -> p t c", t=NT)
            P.wait("vector", s_dv, P.val(s_dv))
            P.op("vector", lambda e: e.tensor_scalar(
                out=scs3[:, :, 0:1].rearrange("p t c -> p (t c)"),
                in0=F2(of_tokid, NT),
                scalar1=1.0, scalar2=None, op0=AL.add), s_dv)
            scdv = P.val(s_dv)
            P.wait("gpsimd", s_dv, scdv)
            P.wait("gpsimd", s_io, zio)
            for t in range(NT):
                for kk in range(2):
                    idx_col = U3h[0:128, 32 + 2 * t + kk:33 + 2 * t + kk]
                    P.iscatter(ap["slotbuf"], idx_col, scs3[:, t, :], s_g)
            P.wait("gpsimd", s_g, P.val(s_g))
            if "u3" in dbg:
                P.dma("gpsimd", dbg_out["u3"], U3h[0:128, 0:64], s_io)
            if "scs" in dbg:
                P.dma("gpsimd", dbg_out["scs"], F2(of_scsrc, NT * 64), s_io)
            if "fpd" in dbg:
                P.dma("gpsimd", dbg_out["fpd"], F2(2300, 128), s_io)
            tok_src = ap["slotbuf"].rearrange("(b p) c -> p b c", p=128)
            P.dma("gpsimd",
                  FPh[0:128, of_tokf:of_tokf + NB].rearrange("p (a b) -> p a b", b=1),
                  tok_src[:, 0:NB, 0:1], s_io, allow_slow_non_contiguous=True)
            rv = P.val(s_io)
            P.wait("vector", s_io, rv)
            P.op("vector", lambda e: e.tensor_scalar(
                out=U3h[0:128, 2 * NT:2 * NT + NB], in0=F2(of_tokf, NB),
                scalar1=-1.0, scalar2=0.0, op0=AL.add, op1=AL.max), s_dv)
            tok_dv = P.val(s_dv)
            if "tok128" in dbg:
                P.wait("gpsimd", s_dv, tok_dv)
                P.dma("gpsimd", dbg_out["tok128"], U3h[0:128, 2 * NT:2 * NT + NB], s_io)

        if phases >= 9:
            buf3 = B3(o_buffm, 8, NSLOT)
            P.wait("gpsimd", s_dv, tok_dv)
            P.wait("gpsimd", s_io, ln2_io)
            mm_after_tile = []
            for b in range(NB):
                idx_col = U3h[0:128, 2 * NT + b:2 * NT + b + 1]
                if b >= 1:
                    P.wait("gpsimd", s_mm, mm_after_tile[b - 1])
                P.igather(B2(o_g0, D), ap["ln2_hbm"], idx_col, s_g)
                gv = P.val(s_g)
                P.wait("tensor", s_g, gv)
                for c in range(NC):
                    pt = PT_B0[:, 0:128] if (c % 2 == 0) else PT_B1[:, 0:128]
                    P.op("tensor", lambda e, pt=pt, c=c:
                         e.transpose(pt, B2(o_g0 + 128 * c, 128), ident_bf), s_mm)
                    P.wait("vector", s_mm, P.val(s_mm))
                    dst = buf3[:, c, 128 * b:128 * (b + 1)]
                    P.op("vector", lambda e, dst=dst, pt=pt:
                         e.tensor_copy(out=dst, in_=pt), s_dv)
                    P.wait("tensor", s_dv, P.val(s_dv) - 1)
                mm_after_tile.append(P.val(s_mm))
            buf_dv = P.val(s_dv)
            if "buf" in dbg:
                P.wait("gpsimd", s_dv, buf_dv)
                P.dma("gpsimd", dbg_out["buf"], B2(o_buffm, 8 * NSLOT), s_io)

        if phases >= 10:
            h3 = B3(o_hfm, 32, CAP)
            P.wait("tensor", s_dv, buf_dv)
            for e_i in range(E):
                for half in range(2):
                    i = 4 * e_i + half
                    ensure_load(i)
                    ensure_load(i + 1)
                    P.wait("tensor", s_w, lv_load[i])
                    P.wait("tensor", s_w2, lv2_load[i])
                    P.wait("tensor", s_w3, lv3_load[i])
                    woff = slot_off[i % 2]
                    for ht in range(16):
                        ps = PS0 if (ht % 2 == 0) else PS1
                        P.wait("tensor", s_ac, max(0, P.val(s_ac) - 1))
                        for c in range(NC):
                            P.op("tensor",
                                 lambda e, ps=ps, woff=woff, c=c, ht=ht, e_i=e_i:
                                 e.matmul(ps[:, 0:CAP],
                                          B3(woff, 8, 2048)[:, c,
                                                            128 * ht:128 * (ht + 1)],
                                          buf3[:, c, e_i * CAP:(e_i + 1) * CAP],
                                          start=(c == 0), stop=(c == NC - 1)),
                                 s_mm if c == NC - 1 else None)
                        P.wait("scalar", s_mm, P.val(s_mm))
                        hti = 16 * half + ht
                        b1_ap = F2(of_b1l + e_i * 32 + hti, 1)
                        P.op("scalar", lambda e, ps=ps, hti=hti, b1_ap=b1_ap:
                             e.activation(out=h3[:, hti, :], in_=ps[:, 0:CAP],
                                          func=AF.Relu, bias=b1_ap), s_ac)
                    slot_free_mm[i % 2] = P.val(s_mm)
                P.wait("tensor", s_ac, P.val(s_ac))
                for chalf in range(2):
                    i = 4 * e_i + 2 + chalf
                    ensure_load(i)
                    ensure_load(i + 1)
                    P.wait("gpsimd", s_mm, P.val(s_mm))
                    b2lv = P.dma("gpsimd", B2(o_b2slot, 512, parts=1),
                                 ap["b2"][loads[i][0]:loads[i][0] + 1,
                                          512 * chalf:512 * (chalf + 1)], s_w)
                    P.wait("tensor", s_w, max(lv_load[i], b2lv))
                    P.wait("tensor", s_w2, lv2_load[i])
                    P.wait("tensor", s_w3, lv3_load[i])
                    woff = slot_off[i % 2]
                    for ss in range(3):
                        sn = 128 if ss < 2 else 64
                        ps = PS2 if (ss % 2 == 0) else PS3
                        P.wait("tensor", s_dv, max(0, P.val(s_dv) - 1))
                        for hh2 in range(32):
                            P.op("tensor", lambda e, ps=ps, woff=woff, hh2=hh2, ss=ss,
                                 sn=sn:
                                 e.matmul(ps[0:sn, 0:512],
                                          h3[:, hh2, ss * 128:ss * 128 + sn],
                                          B3(woff, 32, 512)[:, hh2, :],
                                          start=(hh2 == 0), stop=False))
                        P.op("tensor", lambda e, ps=ps, sn=sn:
                             e.matmul(ps[0:sn, 0:512],
                                      BFh[0:1, oc_ones1:oc_ones1 + sn],
                                      BFh[0:1, o_b2slot:o_b2slot + 512],
                                      start=False, stop=True), s_mm)
                        P.wait("vector", s_mm, P.val(s_mm))
                        P.op("vector", lambda e, ps=ps, sn=sn:
                             e.tensor_copy(out=B2(o_yst, 512, parts=sn),
                                           in_=ps[0:sn, 0:512]), s_dv)
                        ydv = P.val(s_dv)
                        P.wait("gpsimd", s_dv, ydv)
                        P.dma("gpsimd",
                              ap["y_hbm"][e_i * CAP + ss * 128:
                                          e_i * CAP + ss * 128 + sn,
                                          512 * chalf:512 * (chalf + 1)],
                              B2(o_yst, 512, parts=sn), s_io)
                        P.wait("vector", s_io, P.val(s_io))
                    slot_free_mm[i % 2] = P.val(s_mm)
            y_io = P.val(s_io)
            if "h0" in dbg:
                hv = P.val(s_ac)
                P.wait("gpsimd", s_ac, hv)
                P.dma("gpsimd", dbg_out["h0"], B2(o_hfm, 32 * CAP), s_io)

        if phases >= 11:
            out_t = ap["out"].rearrange("(t p) d -> t p d", p=128)
            P.wait("gpsimd", s_io, y_io)
            for t in range(NT):
                lvx = P.dma("sync", F2(of_xtile, D), x1_tiled[t], s_ld)
                P.igather(B2(o_g0, D), ap["y_hbm"],
                          U3h[0:128, 2 * t:2 * t + 1], s_g)
                g1v = P.igather(B2(o_g1, D), ap["y_hbm"],
                                U3h[0:128, 2 * t + 1:2 * t + 2], s_g)
                P.wait("vector", s_ld, lvx)
                P.wait("vector", s_g, g1v)
                g0_ap = F2(of_Garr + 2 * t, 1)
                g1_ap = F2(of_Garr + 2 * t + 1, 1)
                P.op("vector", lambda e, g0_ap=g0_ap: e.scalar_tensor_tensor(
                    out=F2(of_xtile, D), in0=B2(o_g0, D), scalar=g0_ap,
                    in1=F2(of_xtile, D), op0=AL.mult, op1=AL.add))
                P.op("vector", lambda e, g1_ap=g1_ap: e.scalar_tensor_tensor(
                    out=F2(of_xtile, D), in0=B2(o_g1, D), scalar=g1_ap,
                    in1=F2(of_xtile, D), op0=AL.mult, op1=AL.add), s_dv)
                odv = P.val(s_dv)
                P.wait("gpsimd", s_dv, odv)
                P.dma("gpsimd", out_t[t], F2(of_xtile, D), s_st)
                P.wait("sync", s_dv, odv)
                P.wait("sync", s_st, P.val(s_st))
                P.wait("gpsimd", s_g, P.val(s_g))
            P.wait("gpsimd", s_st, P.val(s_st))
            P.wait("gpsimd", s_io, P.val(s_io))
            P.wait("sync", s_st, P.val(s_st))

        for sem in (s_st, s_io, s_g, s_sh, s_w, s_ld):
            P.wait("gpsimd", sem, P.val(sem))
        P.wait("gpsimd", s_dv, P.val(s_dv))
        P.wait("gpsimd", s_ac, P.val(s_ac))
        P.wait("gpsimd", s_mm, P.val(s_mm))
        P.wait("gpsimd", s_ex, P.val(s_ex))

        @block.sync
        def _(e):
            for fn in P.steps["sync"]:
                fn(e)

        @block.gpsimd
        def _(e):
            for fn in P.steps["gpsimd"]:
                fn(e)

        @block.vector
        def _(e):
            for fn in P.steps["vector"]:
                fn(e)

        @block.scalar
        def _(e):
            for fn in P.steps["scalar"]:
                fn(e)

        @block.tensor
        def _(e):
            for fn in P.steps["tensor"]:
                fn(e)

    return nc


_consts_cache = None


def kernel(**inputs):
    global _consts_cache
    if _consts_cache is None:
        _consts_cache = _consts_host()
    consts = _consts_cache
    shared = _prep_inputs(inputs)
    x = np.asarray(inputs["L0_attn_x"], np.float32)
    B = x.shape[0]
    nc = build_program()
    in_maps = []
    for b in range(B):
        m = {"x": np.ascontiguousarray(x[b])}
        m.update(shared)
        m.update(consts)
        in_maps.append(m)
    res = run_bass_kernel_spmd(nc, in_maps, core_ids=list(range(B)))
    out = np.stack([res.results[b]["out"] for b in range(B)], axis=0)
    return out
